# version 15
# baseline (speedup 1.0000x reference)
import numpy as np
import ml_dtypes

import concourse.bass as bass
import concourse.mybir as mybir
from concourse.bass_utils import run_bass_kernel_spmd

F32 = mybir.dt.float32
BF16 = mybir.dt.bfloat16
AX = mybir.AxisListType.X
OP = mybir.AluOpType
AF = mybir.ActivationFunctionType
BF = ml_dtypes.bfloat16

T = 4096
C = 1024
H = 16
DH = 64
NK = C // 128
NTC = T // 512

MODE = "fused"
PHASES = ["ln0", "retp", "reth", "reto", "fap", "fah", "fao", "moe",
          "ln1", "a1p", "a1h", "a1o", "ff1"]

QUEUES = ("pe", "act", "dve", "pool", "sp")


class Buf:
    __slots__ = ("writers", "readers", "name")

    def __init__(self, name=""):
        self.writers = []
        self.readers = []
        self.name = name


NLANES = 8


class OpH:
    __slots__ = ("q", "fn", "deps", "inc", "sig", "val", "dma", "key", "bg")

    def __init__(self, q, fn, deps, dma):
        self.q = q
        self.fn = fn
        self.deps = deps
        self.inc = 16 if dma else 1
        self.dma = dma
        self.sig = False
        self.val = None
        self.key = None
        self.bg = False


class Sched:
    def __init__(self):
        self.ops = {q: [] for q in QUEUES}
        self.default_deps = []
        self.dma_count = {q: 0 for q in QUEUES}

    def op(self, q, fn, reads=(), writes=(), pwrites=(), deps=(), dma=False,
           dma_key=None):
        dl = list(self.default_deps)
        for b in reads:
            dl.extend(b.writers)
        for b in writes:
            dl.extend(b.writers)
            dl.extend(b.readers)
        for b in pwrites:
            dl.extend(b.readers)
        dl.extend(d for d in deps if d is not None)
        seen, deps2 = set(), []
        for d in dl:
            if id(d) not in seen:
                seen.add(id(d))
                deps2.append(d)
        o = OpH(q, fn, deps2, dma)
        if dma:
            kq = dma_key if dma_key is not None else q
            self.dma_count.setdefault(kq, 0)
            o.key = (kq, self.dma_count[kq] % NLANES)
            self.dma_count[kq] += 1
        else:
            o.key = q
        for d in deps2:
            if not (q == "pe" and d.key == "pe"):
                d.sig = True
        for b in reads:
            b.readers.append(o)
        for b in writes:
            b.writers = [o]
            b.readers = []
        for b in pwrites:
            b.writers.append(o)
        self.ops[q].append(o)
        return o

    def barrier(self):
        lasts = []
        for q in QUEUES:
            for o in reversed(self.ops[q]):
                if not o.bg:
                    lasts.append(o)
                    break
        for q in QUEUES:
            self.op(q, lambda e: e.nop(), deps=lasts)

    def finalize(self, sems):
        totals = {}
        for q in QUEUES:
            for o in self.ops[q]:
                if o.dma:
                    o.sig = True
                if o.sig:
                    v = totals.get(o.key, 0) + o.inc
                    o.val = v
                    totals[o.key] = v

        def mk(q):
            def run(eng):
                waited = {}

                def wait(key, val):
                    if val is None or waited.get(key, 0) >= val:
                        return
                    eng.wait_ge(sems[key], val)
                    waited[key] = val

                for o in self.ops[q]:
                    for d in o.deps:
                        if q == "pe" and d.key == "pe":
                            continue
                        wait(d.key, d.val)
                    if o.dma and o.val is not None and o.val > 16:
                        wait(o.key, o.val - 16)
                    inst = o.fn(eng)
                    if o.sig:
                        assert inst is not None, f"op on {q} needs inc but returned None"
                        inst.then_inc(sems[o.key], o.inc)
                for key, tv in sorted(totals.items(), key=str):
                    if key != q:
                        wait(key, tv)
            return run
        return {q: mk(q) for q in QUEUES}


class SBuf:
    __slots__ = ("t", "c0", "cols", "dt", "buf")

    def __init__(self, t, c0, cols, dt, name):
        self.t = t
        self.c0 = c0
        self.cols = cols
        self.dt = dt
        self.buf = Buf(name)

    def ap(self, rs=None, cs=None):
        r = rs if rs is not None else slice(0, 128)
        c = cs if cs is not None else slice(0, self.cols)
        if self.dt == BF16:
            return self.t[r, self.c0 + c.start: self.c0 + c.stop]
        return self.t[r, self.c0 + 2 * c.start: self.c0 + 2 * c.stop].bitcast(F32)


class Arena:

    def __init__(self, nc, name, cols_bf):
        self.t = nc.alloc_sbuf_tensor(name, [128, cols_bf], BF16)
        self.cols = cols_bf
        self.off = 0
        self.name = name

    def alloc(self, cols, dt=BF16, name=""):
        units = cols * (2 if dt == F32 else 1)
        c0 = (self.off + 15) & ~15
        assert c0 + units <= self.cols, \
            f"arena {self.name} overflow: {c0}+{units}>{self.cols} ({name})"
        self.off = c0 + units
        return SBuf(self.t, c0, cols, dt, name)

    def reset(self):
        self.off = 0


def make_consts():
    t = np.arange(T, dtype=np.float64)
    inv = 1.0 / 10000.0 ** (np.arange(0, DH, 2, dtype=np.float64) / DH)
    fr = t[:, None] * inv[None, :]
    emb = np.concatenate([fr, fr], axis=1)
    cosT = np.cos(emb).T.astype(np.float32)
    sinS = np.sin(emb).T.astype(np.float32)
    sinS[:32] *= -1.0
    cos8 = (cosT * 0.125).astype(np.float32)
    sin8 = (sinS * 0.125).astype(np.float32)

    gam = 1.0 - 2.0 ** (-5.0 - np.arange(H, dtype=np.float64))
    lng = np.log(gam)
    j = np.arange(512, dtype=np.float64)
    colf = np.exp(lng[:, None] * j[None, :])
    colf = np.repeat(colf[:, None, :], DH, axis=1).astype(np.float32)
    d0s = -384.0 + 128.0 * np.arange(35, dtype=np.float64)
    i = np.arange(128, dtype=np.float64)
    ex = d0s[None, :, None] - i[None, None, :]
    with np.errstate(over="ignore", under="ignore"):
        rf = np.exp(lng[:, None, None] * ex)
    rf = np.where(np.isfinite(rf), rf, 0.0).astype(np.float32)
    rf = np.ascontiguousarray(rf.transpose(2, 0, 1))
    jj = np.arange(512)
    ii = np.arange(128)
    masks = np.stack([(jj[None, :] - ii[:, None] + (-384 + 128 * k)) >= 0
                      for k in range(4)], axis=0).astype(BF)
    return {
        "cosT": cosT, "sinS": sinS, "cos8": cos8, "sin8": sin8,
        "colf": colf.reshape(H * DH, 512), "rf": rf,
        "masks": np.ascontiguousarray(masks.transpose(1, 0, 2).reshape(128, 4 * 512)),
        "ones_bf": np.ones((128, 1), dtype=BF),
        "ones_f": np.ones((1, 128), dtype=np.float32),
        "ones_row_bf": np.ones((1, 128), dtype=BF),
        "ident": np.eye(128, dtype=BF),
        "sel8": np.ascontiguousarray(np.kron(np.eye(8), np.ones((1, 128))).astype(BF)),
    }


def prep_weights(inp):
    w = {}
    WqT = np.ascontiguousarray(inp["L0_attn_Wq"].T)
    WkT = np.ascontiguousarray(inp["L0_attn_Wk"].T)
    w["ret_qk"] = np.ascontiguousarray(np.stack(
        [np.concatenate([WqT[:, h * 64:(h + 1) * 64], WkT[:, h * 64:(h + 1) * 64]], 1)
         for h in range(H)]))
    w["ret_v"] = np.ascontiguousarray(inp["L0_attn_Wv"].T)
    w["ret_wo"] = np.ascontiguousarray(inp["L0_attn_Wo"].T)
    w["gnw"] = np.ascontiguousarray(inp["L0_attn_gn_weight"].reshape(16, 64).T)
    w["gnb"] = np.ascontiguousarray(inp["L0_attn_gn_bias"].reshape(16, 64).T)

    qkv = inp["L0_ffn_qkv_w"]
    QT = np.ascontiguousarray(qkv[0:1024].T)
    KTm = np.ascontiguousarray(qkv[1024:2048].T)
    w["fa_qk"] = np.ascontiguousarray(np.stack(
        [np.concatenate([QT[:, h * 64:(h + 1) * 64], KTm[:, h * 64:(h + 1) * 64]], 1)
         for h in range(H)]))
    w["fa_v"] = np.ascontiguousarray(qkv[2048:3072].T)
    b = inp["L0_ffn_qkv_b"]
    w["fa_qkb"] = np.ascontiguousarray(np.stack(
        [np.concatenate([b[h * 64:(h + 1) * 64], b[1024 + h * 64:1024 + (h + 1) * 64]])
         for h in range(H)]).T)
    w["fa_vb"] = np.ascontiguousarray(b[2048:3072].reshape(1, 1024))
    w["fa_out"] = np.ascontiguousarray(inp["L0_ffn_out_w"].T)
    w["fa_outb"] = np.ascontiguousarray(inp["L0_ffn_out_b"].reshape(8, 128).T)
    w["norm1w"] = np.ascontiguousarray(inp["L0_ffn_norm1_w"].reshape(8, 128).T)
    w["norm2w"] = np.ascontiguousarray(inp["L0_ffn_norm2_w"].reshape(8, 128).T)
    w["gateT"] = np.ascontiguousarray(inp["L0_ffn_gate_w"].T)
    w["moe_w1"] = np.ascontiguousarray(inp["L0_ffn_w1"].transpose(0, 2, 1))
    w["moe_w2"] = np.ascontiguousarray(inp["L0_ffn_w2"].transpose(0, 2, 1))
    w["moe_w3"] = np.ascontiguousarray(inp["L0_ffn_w3"].transpose(0, 2, 1))
    w["moe_b1"] = np.ascontiguousarray(
        inp["L0_ffn_b1"].reshape(8, 16, 128).transpose(0, 2, 1))
    w["moe_b2"] = np.ascontiguousarray(
        inp["L0_ffn_b2"].reshape(8, 16, 128).transpose(0, 2, 1))
    w["moe_b3"] = np.ascontiguousarray(inp["L0_ffn_b3"])

    Wq1 = inp["L1_attn_Wqkv"]
    w["at1_qk"] = np.ascontiguousarray(Wq1[:, :, 0:128])
    w["at1_v"] = np.ascontiguousarray(
        Wq1[:, :, 128:192].transpose(1, 0, 2).reshape(1024, 1024))
    b1 = inp["L1_attn_bqkv"]
    w["at1_qkb"] = np.ascontiguousarray(b1[:, 0:128].T)
    w["at1_vb"] = np.ascontiguousarray(b1[:, 128:192].reshape(1, 1024))
    w["at1_out"] = np.ascontiguousarray(inp["L1_attn_Wout"])
    w["at1_outb"] = np.ascontiguousarray(inp["L1_attn_bout"].reshape(8, 128).T)

    w["ff_w1"] = np.ascontiguousarray(inp["L1_ffn_w1_w"].T)
    w["ff_w2"] = np.ascontiguousarray(inp["L1_ffn_w2_w"].T)
    w["ff_h1"] = np.ascontiguousarray(inp["L1_ffn_h1_w"].T)
    w["ff_h2"] = np.ascontiguousarray(inp["L1_ffn_h2_w"].T)
    w["ff_w3"] = np.ascontiguousarray(inp["L1_ffn_w3_w"].T)
    w["ff_s1"] = np.ascontiguousarray(inp["L1_ffn_w1_s"].reshape(32, 128).T)
    w["ff_s2"] = np.ascontiguousarray(inp["L1_ffn_w2_s"].reshape(32, 128).T)
    w["ff_hs1"] = np.ascontiguousarray(inp["L1_ffn_h1_s"].reshape(32, 128).T)
    w["ff_hs2"] = np.ascontiguousarray(inp["L1_ffn_h2_s"].reshape(32, 128).T)
    w["ff_s3"] = np.ascontiguousarray(inp["L1_ffn_w3_s"].reshape(8, 128).T)
    return w


CACHED = {"ret_qk", "ret_v", "ret_wo", "fa_qk", "fa_v", "fa_out", "gateT",
          "moe_w1", "moe_w2", "moe_w3", "moe_b3", "at1_qk", "at1_v", "at1_out",
          "ff_w1", "ff_w2", "ff_h1", "ff_h2", "ff_w3"}

SCRATCH = {
    "xnT": ((C, T), F32, "ln0"),
    "xnTb": ((C, T), BF16, "ln0"),
    "partialT": ((C, T), F32, "reto"),
    "x1T": ((C, T), F32, "fao"),
    "y0T": ((C, T), F32, "moe"),
    "ynT": ((C, T), F32, "ln1"),
    "ynTb": ((C, T), BF16, "ln1"),
    "p1T": ((C, T), F32, "a1o"),
    "r1st": ((1, T), F32, "ln0"),
    "qkf_ret": ((2048, T), F32, "retp"),
    "vt_ret": ((T, C), BF16, "retp"),
    "hd_ret": ((C, T), BF16, "reth"),
    "qkf_fa": ((2048, T), F32, "fap"),
    "vt_fa": ((T, C), BF16, "fap"),
    "hd_fa": ((C, T), BF16, "fah"),
    "qkf_at1": ((2048, T), F32, "a1p"),
    "vt_at1": ((T, C), BF16, "a1p"),
    "hd_at1": ((C, T), BF16, "a1h"),
}

PHASE_NEEDS = {
    "ln0": ["xT"],
    "retp": ["xnTb"], "reth": ["qkf_ret", "vt_ret"], "reto": ["hd_ret", "xT"],
    "fap": ["xnT", "r1st"], "fah": ["qkf_fa", "vt_fa"], "fao": ["hd_fa", "xnT"],
    "moe": ["x1T", "partialT"],
    "ln1": ["y0T"],
    "a1p": ["ynTb"], "a1h": ["qkf_at1", "vt_at1"], "a1o": ["hd_at1", "y0T"],
    "ff1": ["ynTb", "p1T"],
}
PHASE_MAKES = {
    "ln0": ["xnT", "xnTb", "r1st"],
    "retp": ["qkf_ret", "vt_ret"], "reth": ["hd_ret"], "reto": ["partialT"],
    "fap": ["qkf_fa", "vt_fa"], "fah": ["hd_fa"], "fao": ["x1T"],
    "moe": ["y0T"],
    "ln1": ["ynT", "ynTb"],
    "a1p": ["qkf_at1", "vt_at1"], "a1h": ["hd_at1"], "a1o": ["p1T"],
    "ff1": ["outT"],
}
PHASE_WEIGHTS = {
    "ln0": [], "ln1": [],
    "retp": ["ret_qk", "ret_v"], "reth": ["gnw", "gnb"],
    "reto": ["ret_wo"],
    "fap": ["fa_qk", "fa_v", "fa_qkb", "fa_vb", "norm1w"], "fah": [],
    "fao": ["fa_out", "fa_outb"],
    "moe": ["gateT", "moe_w1", "moe_w2", "moe_w3", "moe_b1", "moe_b2",
            "moe_b3", "norm2w"],
    "a1p": ["at1_qk", "at1_v", "at1_qkb", "at1_vb"], "a1h": [],
    "a1o": ["at1_out", "at1_outb"],
    "ff1": ["ff_w1", "ff_w2", "ff_h1", "ff_h2", "ff_w3",
            "ff_s1", "ff_s2", "ff_hs1", "ff_hs2", "ff_s3"],
}

ACFG = {
    "ret": dict(qk="ret_qk", v="ret_v", qkb=None, vb=None, xsrc="xnTb",
                rms=None, rope=True, kind="decay", causal=True,
                qkf="qkf_ret", vt="vt_ret", hd="hd_ret",
                ow="ret_wo", ob=None, res="xT", dst="partialT"),
    "fa": dict(qk="fa_qk", v="fa_v", qkb="fa_qkb", vb="fa_vb", xsrc=None,
               rms=("xnT", "norm1w", 1e-6), rope=True, kind="softmax",
               causal=False, qkf="qkf_fa", vt="vt_fa", hd="hd_fa",
               ow="fa_out", ob="fa_outb", res="xnT", dst="x1T",
               out_stats=True),
    "at1": dict(qk="at1_qk", v="at1_v", qkb="at1_qkb", vb="at1_vb",
                xsrc="ynTb", rms=None, rope=False, kind="softmax",
                causal=True, qkf="qkf_at1", vt="vt_at1", hd="hd_at1",
                ow="at1_out", ob="at1_outb", res="y0T", dst="p1T"),
}


class Prog:
    def __init__(self, phases, wshapes, consts, later_needs=(), taps=()):
        self.nc = bass.Bass("TRN2", target_bir_lowering=False, debug=False)
        self.s = Sched()
        self.phases = phases
        nc = self.nc
        self.ext_in = []
        self.dram = {}
        self.dbuf = {}
        self.cache_done = {}

        needs, makes = set(), set()
        for ph in phases:
            needs |= set(PHASE_NEEDS[ph])
            makes |= set(PHASE_MAKES[ph])

        if "xT" in needs:
            self.dram["xT"] = nc.dram_tensor("xT", [C, T], F32, kind="ExternalInput")
            self.dbuf["xT"] = Buf("xT")
            self.ext_in.append("xT")
        if "outT" in makes:
            self.dram["outT"] = nc.dram_tensor("outT", [C, T], F32, kind="ExternalOutput")
            self.dbuf["outT"] = Buf("outT")
        for name, (shape, dt, prod) in SCRATCH.items():
            if name not in (needs | makes):
                continue
            if name in makes and name in later_needs:
                kind = "ExternalOutput"
            elif name in needs and name not in makes:
                kind = "ExternalInput"
            else:
                kind = "Internal"
            self.dram[name] = nc.dram_tensor(name, list(shape), dt, kind=kind)
            self.dbuf[name] = Buf(name)
            if kind == "ExternalInput":
                self.ext_in.append(name)

        self.wext = {}
        self.wcache = {}
        for ph in phases:
            for wn in PHASE_WEIGHTS[ph]:
                if wn in self.wext:
                    continue
                self.wext[wn] = nc.dram_tensor("w_" + wn, list(wshapes[wn]), F32,
                                               kind="ExternalInput")
                self.dbuf["w_" + wn] = Buf("w_" + wn)
                self.ext_in.append(wn)
                if wn in CACHED:
                    self.wcache[wn] = nc.dram_tensor("bc_" + wn, list(wshapes[wn]),
                                                     BF16, kind="Internal")
                    self.dbuf["bc_" + wn] = Buf("bc_" + wn)

        self.ct = {k: nc.inline_tensor(np.ascontiguousarray(v), name="c_" + k)
                   for k, v in consts.items()}
        self.cbuf = {k: Buf("c_" + k) for k in consts}

        self.ar = Arena(nc, "mega", 100608)
        self.psum = nc.alloc_psum_tensor("ps", [128, 4096], F32)
        self.sems = {q: nc.alloc_semaphore(name="sem_" + q) for q in QUEUES}
        for q in list(QUEUES) + ["bgp"]:
            for ln in range(NLANES):
                self.sems[(q, ln)] = nc.alloc_semaphore(name=f"sem_{q}_d{ln}")

        cvals = [0.0, 1.0, 1e-5, 1e-6, 0.125, -1.0]
        cv_np = np.tile(np.array([cvals], dtype=np.float32), (128, 1))
        cv_t = nc.inline_tensor(cv_np, name="c_scal")
        cv_sb = nc.alloc_sbuf_tensor("cscal", [128, 16], F32)
        boot = self.s.op("sp", lambda e: e.dma_start(cv_sb[0:128, 0:len(cvals)], cv_t.ap()),
                         dma=True)
        for i, v in enumerate(cvals):
            nc.const_aps.register(F32, float(v), cv_sb[0:128, i:i + 1])
        self.s.default_deps = [boot]

        self.taps = list(taps)
        self.tap_out = {}
        for tn in self.taps:
            if tn in SCRATCH:
                shape, dt, _ = SCRATCH[tn]
            else:
                shape, dt = (C, T), F32
            self.tap_out[tn] = nc.dram_tensor("tap_" + tn, list(shape), dt,
                                              kind="ExternalOutput")

    def ab(self, cols, name=""):
        return self.ar.alloc(cols, BF16, name)

    def af(self, cols, name=""):
        return self.ar.alloc(cols, F32, name)

    def cache(self, wn):
        if wn not in self.cache_done:
            src, dst = self.wext[wn], self.wcache[wn]
            o = self.s.op(
                "pool", lambda e: e.dma_start(dst.ap(), src.ap()),
                reads=[self.dbuf["w_" + wn]], writes=[self.dbuf["bc_" + wn]],
                dma=True, dma_key="bgp")
            o.bg = True
            self.cache_done[wn] = o
        return self.wcache[wn], self.dbuf["bc_" + wn]

    def pb(self, bank, nbanks=1, rows=slice(0, 128)):
        return self.psum[rows, bank * 512:(bank + nbanks) * 512]

    def tap(self, name):
        if name not in self.tap_out:
            return
        src, dst = self.dram[name], self.tap_out[name]
        self.s.op("sp", lambda e: e.dma_start(dst.ap(), src.ap()),
                  reads=[self.dbuf[name]], writes=[Buf()], dma=True)

    def build(self):
        for i, ph in enumerate(self.phases):
            self.pbk = [Buf(f"bank{i}") for i in range(8)]
            for ph2 in self.phases[i + 1:i + 3]:
                for wn in PHASE_WEIGHTS[ph2]:
                    if wn in self.wcache:
                        self.cache(wn)
            PHASE_FNS[ph](self)
            for mk in PHASE_MAKES[ph]:
                if mk in self.taps:
                    self.tap(mk)
            self.s.barrier()
            self.ar.reset()
        emitters = self.s.finalize(self.sems)
        with self.nc.Block("main") as blk:
            blk.tensor(emitters["pe"])
            blk.scalar(emitters["act"])
            blk.vector(emitters["dve"])
            blk.gpsimd(emitters["pool"])
            blk.sync(emitters["sp"])
        return self.nc


def load_weight_f32(P, wn, sbuf, src_slice=None, rows=128, cols=None):
    src = P.wext[wn]
    ap = src.ap() if src_slice is None else src_slice(src)
    cols = cols if cols is not None else ap.shape[-1]
    return P.s.op("sp", lambda e: e.dma_start(
        sbuf.ap(slice(0, rows), slice(0, cols)), ap),
        reads=[P.dbuf["w_" + wn]], writes=[sbuf.buf], dma=True)


def load_const(P, cname, sbuf, rows, cols):
    src = P.ct[cname]
    return P.s.op("sp", lambda e: e.dma_start(
        sbuf.ap(slice(0, rows), slice(0, cols)), src.ap()),
        reads=[P.cbuf[cname]], writes=[sbuf.buf], dma=True)


def ph_ln0(P):
    _ln_common(P, "xT", "xnT", "xnTb", sumsq_to="r1st")


def ph_ln1(P):
    _ln_common(P, "y0T", "ynT", "ynTb")


def _ln_common(P, srcn, dstfn, dstbn, sumsq_to=None):
    s = P.s
    src, srcb = P.dram[srcn], P.dbuf[srcn]
    dstf, dstfb = P.dram[dstfn], P.dbuf[dstfn]
    dstb, dstbb = P.dram[dstbn], P.dbuf[dstbn]

    ones_b = P.ab(16, "ln_ones")
    load_const(P, "ones_bf", ones_b, 128, 1)
    onesf = P.af(144, "ln_onesf")
    for r0 in (0, 32, 64):
        P.s.op("sp", lambda e, r0=r0: e.dma_start(
            onesf.ap(slice(r0, r0 + 1), slice(0, 128)), P.ct["ones_f"].ap()),
            reads=[P.cbuf["ones_f"]], pwrites=[onesf.buf], dma=True)

    xin = [P.af(4096, f"ln_x{i}") for i in range(2)]
    xbf = [P.ab(4096, f"ln_xbf{i}") for i in range(2)]
    xsq = [P.ab(4096, f"ln_xsq{i}") for i in range(2)]
    sum_ps, sq_ps = Buf("sum_ps"), Buf("sq_ps")
    for c in range(NK):
        xb, bb, qb = xin[c % 2], xbf[c % 2], xsq[c % 2]
        s.op("sp", lambda e, c=c, xb=xb: e.dma_start(
            xb.ap(), src[c * 128:(c + 1) * 128, :]),
            reads=[srcb], writes=[xb.buf], dma=True)
        s.op("act", lambda e, xb=xb, bb=bb: e.activation(bb.ap(), xb.ap(), AF.Copy),
             reads=[xb.buf], writes=[bb.buf])
        s.op("act", lambda e, xb=xb, qb=qb: e.activation(qb.ap(), xb.ap(), AF.Square),
             reads=[xb.buf], writes=[qb.buf])

        def mk_mm(row, tile, c=c):
            def f(e):
                last = None
                for tc in range(NTC):
                    last = e.matmul(P.psum[row:row + 1, tc * 512:(tc + 1) * 512],
                                    ones_b.ap(slice(0, 128), slice(0, 1)),
                                    tile.ap(slice(0, 128), slice(tc * 512, (tc + 1) * 512)),
                                    start=(c == 0), stop=(c == NK - 1))
                return last
            return f
        s.op("pe", mk_mm(0, bb), reads=[bb.buf, ones_b.buf],
             writes=[sum_ps] if c == 0 else [], pwrites=[sum_ps] if c > 0 else [])
        s.op("pe", mk_mm(32, qb), reads=[qb.buf, ones_b.buf],
             writes=[sq_ps] if c == 0 else [], pwrites=[sq_ps] if c > 0 else [])

    rows = P.af(4096, "ln_rows")
    rM, rR, rTm = slice(0, 1), slice(32, 33), slice(64, 65)
    s.op("act", lambda e: e.activation(rows.ap(rM), P.psum[0:1, :], AF.Copy, scale=1.0 / C),
         reads=[sum_ps], writes=[rows.buf])
    s.op("act", lambda e: e.activation(rows.ap(rTm), rows.ap(rM), AF.Square),
         reads=[rows.buf], pwrites=[rows.buf])
    s.op("dve", lambda e: e.scalar_tensor_tensor(rows.ap(rR), P.psum[32:33, :], 1.0 / C,
                                                 rows.ap(rTm), OP.mult, OP.subtract),
         reads=[sq_ps, rows.buf], pwrites=[rows.buf])
    s.op("act", lambda e: e.activation(rows.ap(rTm), rows.ap(rR), AF.Ln, bias=1e-5),
         reads=[rows.buf], pwrites=[rows.buf])
    s.op("act", lambda e: e.activation(rows.ap(rR), rows.ap(rTm), AF.Exp, scale=-0.5),
         reads=[rows.buf], pwrites=[rows.buf])

    xt2 = [P.af(512, f"ln_t{i}") for i in range(4)]
    xnf = [P.af(512, f"ln_v{i}") for i in range(4)]
    xnb = [P.ab(512, f"ln_b{i}") for i in range(4)]
    meanB, rstdB = Buf("meanB"), Buf("rstdB")
    if sumsq_to is not None:
        sqb = [P.ab(512, f"ln_s{i}") for i in range(4)]
        r1row = P.af(4096, "ln_r1row")
        ssD, ssB = P.dram[sumsq_to], P.dbuf[sumsq_to]
    for tc in range(NTC):
        cs = slice(tc * 512, (tc + 1) * 512)
        s.op("pe", lambda e, cs=cs: e.matmul(
            P.pb(6), onesf.ap(slice(0, 1), slice(0, 128)), rows.ap(rM, cs),
            start=True, stop=True),
            reads=[rows.buf, onesf.buf], writes=[meanB])
        s.op("pe", lambda e, cs=cs: e.matmul(
            P.pb(7), onesf.ap(slice(32, 33), slice(0, 128)), rows.ap(rR, cs),
            start=True, stop=True),
            reads=[rows.buf, onesf.buf], writes=[rstdB])
        for c in range(NK):
            i = c % 4
            xb, xo, xB = xt2[i], xnf[i], xnb[i]
            s.op("sp", lambda e, c=c, cs=cs, xb=xb: e.dma_start(
                xb.ap(), src[c * 128:(c + 1) * 128, cs]),
                reads=[srcb], writes=[xb.buf], dma=True)
            s.op("dve", lambda e, xb=xb, xo=xo: e.tensor_sub(xo.ap(), xb.ap(), P.pb(6)),
                 reads=[xb.buf, meanB], writes=[xo.buf])
            s.op("dve", lambda e, xo=xo: e.tensor_mul(xo.ap(), xo.ap(), P.pb(7)),
                 reads=[rstdB, xo.buf], pwrites=[xo.buf])
            s.op("act", lambda e, xo=xo, xB=xB: e.activation(xB.ap(), xo.ap(), AF.Copy),
                 reads=[xo.buf], writes=[xB.buf])
            s.op("sp", lambda e, c=c, cs=cs, xo=xo: e.dma_start(
                dstf[c * 128:(c + 1) * 128, cs], xo.ap()),
                reads=[xo.buf], pwrites=[dstfb], dma=True)
            s.op("sp", lambda e, c=c, cs=cs, xB=xB: e.dma_start(
                dstb[c * 128:(c + 1) * 128, cs], xB.ap()),
                reads=[xB.buf], pwrites=[dstbb], dma=True)
            if sumsq_to is not None:
                sb2 = sqb[i]
                s.op("act", lambda e, xo=xo, sb2=sb2: e.activation(
                    sb2.ap(), xo.ap(), AF.Square),
                    reads=[xo.buf], writes=[sb2.buf])
                s.op("pe", lambda e, c=c, sb2=sb2: e.matmul(
                    P.psum[0:1, 2560:3072], ones_b.ap(slice(0, 128), slice(0, 1)),
                    sb2.ap(), start=(c == 0), stop=(c == NK - 1)),
                    reads=[sb2.buf, ones_b.buf],
                    writes=[P.pbk[5]] if c == 0 else [],
                    pwrites=[P.pbk[5]] if c else [])
        if sumsq_to is not None:
            s.op("act", lambda e, cs=cs: e.activation(
                r1row.ap(slice(0, 1), cs), P.psum[0:1, 2560:3072], AF.Copy),
                reads=[P.pbk[5]],
                writes=[r1row.buf] if tc == 0 else [],
                pwrites=[r1row.buf] if tc else [])
    if sumsq_to is not None:
        s.op("sp", lambda e: e.dma_start(ssD.ap(), r1row.ap(slice(0, 1))),
             reads=[r1row.buf], writes=[ssB], dma=True)


def _rms_apply(P, rms, xsb):
    s = P.s
    srcn, wn, eps = rms
    src, srcb = P.dram[srcn], P.dbuf[srcn]
    ssD, ssB = P.dram["r1st"], P.dbuf["r1st"]
    nw = P.af(16, "rms_w")
    load_weight_f32(P, wn, nw, cols=8)
    onesf = P.af(144, "rms_onesf")
    P.s.op("sp", lambda e: e.dma_start(onesf.ap(slice(0, 1), slice(0, 128)),
                                       P.ct["ones_f"].ap()),
           reads=[P.cbuf["ones_f"]], writes=[onesf.buf], dma=True)
    xin = [P.af(512, f"rms_x{i}") for i in range(4)]
    rows = P.af(4096, "rms_rows")
    srow = P.af(4096, "rms_srow")
    r1 = slice(0, 1)
    s.op("sp", lambda e: e.dma_start(srow.ap(r1), ssD.ap()),
         reads=[ssB], writes=[srow.buf], dma=True)
    s.op("act", lambda e: e.activation(srow.ap(r1), srow.ap(r1),
                                       AF.Ln, scale=1.0 / C, bias=eps),
         reads=[srow.buf], pwrites=[srow.buf])
    s.op("act", lambda e: e.activation(rows.ap(r1), srow.ap(r1), AF.Exp, scale=-0.5),
         reads=[srow.buf], writes=[rows.buf])
    for tc in range(NTC):
        cs = slice(tc * 512, (tc + 1) * 512)
        s.op("pe", lambda e, cs=cs: e.matmul(
            P.pb(6), onesf.ap(slice(0, 1), slice(0, 128)), rows.ap(r1, cs),
            start=True, stop=True),
            reads=[rows.buf, onesf.buf], writes=[P.pbk[6]])
        for c in range(NK):
            i = c % 4
            xb, qb = xin[i], xin[(i + 2) % 4]
            s.op("sp", lambda e, c=c, cs=cs, xb=xb: e.dma_start(
                xb.ap(), src[c * 128:(c + 1) * 128, cs]),
                reads=[srcb], writes=[xb.buf], dma=True)
            s.op("dve", lambda e, xb=xb, qb=qb: e.tensor_mul(
                qb.ap(), xb.ap(), P.pb(6)),
                reads=[xb.buf, P.pbk[6]], writes=[qb.buf])
            s.op("act", lambda e, c=c, cs=cs, qb=qb: e.activation(
                xsb[c].ap(None, cs), qb.ap(), AF.Copy,
                scale=nw.ap(None, slice(c, c + 1))),
                reads=[qb.buf, nw.buf],
                writes=[xsb[c].buf] if tc == 0 else [],
                pwrites=[] if tc == 0 else [xsb[c].buf])


def attn_prep(P, cfg):
    s = P.s
    xsb = [P.ab(4096, f"px{c}") for c in range(8)]
    if cfg["rms"] is not None:
        _rms_apply(P, cfg["rms"], xsb)
    else:
        src, sbf = P.dram[cfg["xsrc"]], P.dbuf[cfg["xsrc"]]
        for c in range(8):
            s.op("sp", lambda e, c=c, t=xsb[c]: e.dma_start(
                t.ap(), src[c * 128:(c + 1) * 128, :]),
                reads=[sbf], writes=[xsb[c].buf], dma=True)

    vtd, vtb = P.dram[cfg["vt"]], P.dbuf[cfg["vt"]]
    vC, vCb = P.cache(cfg["v"])
    vw = [P.ab(1024, f"pv{c}") for c in range(8)]
    for c in range(8):
        vwc = vw[c]
        s.op("sp", lambda e, c=c, t=vwc: e.dma_start(
            t.ap(), vC[c * 128:(c + 1) * 128, :]),
            reads=[vCb], writes=[vwc.buf], dma=True)
    onesr = P.ab(144, "p_onesr")
    load_const(P, "ones_row_bf", onesr, 1, 128)
    vbrow = None
    if cfg["vb"] is not None:
        vbrow = P.ab(1024, "p_vbrow")
        vbsrc = P.wext[cfg["vb"]]
        s.op("pool", lambda e: e.dma_start(
            vbrow.ap(slice(0, 1), slice(0, 1024)), vbsrc.ap()),
            reads=[P.dbuf["w_" + cfg["vb"]]], writes=[vbrow.buf], dma=True)
    vob = [P.ab(1024, f"pvo{i}") for i in range(2)]
    for tt in range(32):
        ob = vob[tt % 2]
        banks = ((tt % 2) * 2, (tt % 2) * 2 + 1)

        def mkv(tt=tt, banks=banks):
            def f(e):
                last = None
                for hf in range(2):
                    for c in range(8):
                        last = e.matmul(
                            P.pb(banks[hf]),
                            xsb[c].ap(None, slice(tt * 128, (tt + 1) * 128)),
                            vw[c].ap(None, slice(hf * 512, (hf + 1) * 512)),
                            start=(c == 0), stop=(c == 7 and vbrow is None))
                    if vbrow is not None:
                        last = e.matmul(
                            P.pb(banks[hf]), onesr.ap(slice(0, 1), slice(0, 128)),
                            vbrow.ap(slice(0, 1), slice(hf * 512, (hf + 1) * 512)),
                            start=False, stop=True)
                return last
            return f
        rd = [t.buf for t in xsb] + [t.buf for t in vw] + [onesr.buf]
        if vbrow is not None:
            rd.append(vbrow.buf)
        s.op("pe", mkv(), reads=rd, writes=[P.pbk[banks[0]], P.pbk[banks[1]]])
        for hf in range(2):
            s.op("act", lambda e, b=banks[hf], hf=hf, ob=ob: e.activation(
                ob.ap(None, slice(hf * 512, (hf + 1) * 512)), P.pb(b), AF.Copy),
                reads=[P.pbk[banks[hf]]],
                writes=[ob.buf] if hf == 0 else [], pwrites=[ob.buf] if hf else [])
        s.op("sp", lambda e, tt=tt, ob=ob: e.dma_start(
            vtd[tt * 128:(tt + 1) * 128, :], ob.ap()),
            reads=[ob.buf], pwrites=[vtb], dma=True)

    qkC, qkCb = P.cache(cfg["qk"])
    qkbv = None
    if cfg["qkb"] is not None:
        qkbv = P.af(16, "p_qkb")
        load_weight_f32(P, cfg["qkb"], qkbv, cols=16)
    wq = [P.ab(1024, f"pwq{i}") for i in range(2)]
    qf = [P.af(4096, f"pqf{i}") for i in range(2)]
    qkfd, qkfb = P.dram[cfg["qkf"]], P.dbuf[cfg["qkf"]]
    for h in range(H):
        wqs, qfs = wq[h % 2], qf[h % 2]
        s.op("sp", lambda e, h=h, t=wqs: e.dma_start(
            t.ap().rearrange("p (k o) -> p k o", o=128),
            qkC[h].rearrange("(k p) o -> p k o", p=128)),
            reads=[qkCb], writes=[wqs.buf], dma=True)
        for tc in range(NTC):
            bank = 4 + (tc % 4)

            def mkp(wqs=wqs, tc=tc, bank=bank):
                def f(e):
                    last = None
                    for c in range(8):
                        last = e.matmul(
                            P.pb(bank), wqs.ap(None, slice(c * 128, (c + 1) * 128)),
                            xsb[c].ap(None, slice(tc * 512, (tc + 1) * 512)),
                            start=(c == 0), stop=(c == 7))
                    return last
                return f
            s.op("pe", mkp(), reads=[wqs.buf] + [t.buf for t in xsb],
                 writes=[P.pbk[bank]])
            if qkbv is not None:
                s.op("dve", lambda e, tc=tc, bank=bank, qfs=qfs, h=h:
                     e.tensor_scalar_add(
                         qfs.ap(None, slice(tc * 512, (tc + 1) * 512)), P.pb(bank),
                         qkbv.ap(None, slice(h, h + 1))),
                     reads=[P.pbk[bank], qkbv.buf],
                     writes=[qfs.buf] if tc == 0 else [],
                     pwrites=[qfs.buf] if tc else [])
            else:
                s.op("act", lambda e, tc=tc, bank=bank, qfs=qfs: e.activation(
                    qfs.ap(None, slice(tc * 512, (tc + 1) * 512)), P.pb(bank), AF.Copy),
                    reads=[P.pbk[bank]],
                    writes=[qfs.buf] if tc == 0 else [],
                    pwrites=[qfs.buf] if tc else [])
        s.op("sp", lambda e, h=h, qfs=qfs: e.dma_start(
            qkfd[h * 128:(h + 1) * 128, :], qfs.ap()),
            reads=[qfs.buf], pwrites=[qkfb], dma=True)


RETH_STAGE = 99
RETH_GN6 = 3


def attn_heads(P, cfg):
    s = P.s
    decay = cfg["kind"] == "decay"
    soft = not decay
    causal = cfg["causal"]
    rope = cfg["rope"]

    onesf = P.af(144, "h_onesf")
    for r0 in (0, 64):
        s.op("sp", lambda e, r0=r0: e.dma_start(
            onesf.ap(slice(r0, r0 + 1), slice(0, 128)), P.ct["ones_f"].ap()),
            reads=[P.cbuf["ones_f"]], pwrites=[onesf.buf], dma=True)
    ones_b = P.ab(16, "h_onesb")
    load_const(P, "ones_bf", ones_b, 128, 1)
    if rope:
        tabC = P.af(4096, "h_tabC")
        tabS = P.af(4096, "h_tabS")
        for tab, top, bot in ((tabC, "cos8", "cosT"), (tabS, "sin8", "sinS")):
            s.op("sp", lambda e, tab=tab, top=top: e.dma_start(
                tab.ap(slice(0, 64), slice(0, T)), P.ct[top].ap()),
                reads=[P.cbuf[top]], pwrites=[tab.buf], dma=True)
            s.op("sp", lambda e, tab=tab, bot=bot: e.dma_start(
                tab.ap(slice(64, 128), slice(0, T)), P.ct[bot].ap()),
                reads=[P.cbuf[bot]], pwrites=[tab.buf], dma=True)
    if decay:
        rf = P.af(560, "h_rf")
        load_const(P, "rf", rf, 128, 560)
        colf = [P.af(512, f"h_cf{i}") for i in range(2)]
        gv = P.af(16, "h_gnw")
        gb = P.af(16, "h_gnb")
        load_weight_f32(P, "gnw", gv, rows=64, cols=16)
        load_weight_f32(P, "gnb", gb, rows=64, cols=16)
    if causal:
        msk = P.ab(2048, "h_masks")
        load_const(P, "masks", msk, 128, 2048)

    QKF = [P.af(4096, f"h_qkf{i}") for i in range(2)]
    SW = [P.af(4096, f"h_sw{i}") for i in range(2)]
    SWb = []
    for i in range(2):
        b = SBuf(P.ar.t, SW[i].c0, 4096, BF16, f"h_swb{i}")
        b.buf = SW[i].buf
        SWb.append(b)
    QA = [P.ab(4096, f"h_qa{i}") for i in range(2)]
    KA = [P.ab(4096, f"h_ka{i}") for i in range(2)]
    npt = 4
    pt = [P.ab(1024 if soft else 512, f"h_pt{i}") for i in range(npt)]
    vst = 65 if soft else 64
    Vh = [P.ab(32 * vst, f"h_vh{i}") for i in range(2)]
    rows = P.af(4096, "h_rows")
    att = P.ab(4096, "h_att")
    rob = [P.ab(512, f"h_rob{i}") for i in range(2)]
    rosq = [P.ab(512, f"h_rosq{i}") for i in range(2)]

    hdD, hdB = P.dram[cfg["hd"]], P.dbuf[cfg["hd"]]
    qkfD, qkfB = P.dram[cfg["qkf"]], P.dbuf[cfg["qkf"]]
    vtD, vtB = P.dram[cfg["vt"]], P.dbuf[cfg["vt"]]

    if soft:
        for i in range(2):
            s.op("dve", lambda e, i=i: e.memset(KA[i].ap(slice(64, 65)), 1.0),
                 pwrites=[KA[i].buf])
            s.op("dve", lambda e, i=i: e.memset(
                P.ar.t[0:128, Vh[i].c0:Vh[i].c0 + 32 * 65].rearrange(
                    "p (kt d) -> p kt d", d=65)[:, :, 64:65], 1.0),
                 pwrites=[Vh[i].buf])

    def prep_early(h):
        p2 = h % 2
        qkf, qa, ka, vh, sw, swb = QKF[p2], QA[p2], KA[p2], Vh[p2], SW[p2], SWb[p2]
        s.op("sp", lambda e, h=h, qkf=qkf: e.dma_start(
            qkf.ap(), qkfD[h * 128:(h + 1) * 128, :]),
            reads=[qkfB], writes=[qkf.buf], dma=True)
        vdst = P.ar.t[0:128, vh.c0:vh.c0 + 32 * vst].rearrange(
            "p (kt d) -> p kt d", d=vst)[:, :, 0:64]
        for part in range(4):
            ks = slice(part * 8, (part + 1) * 8)
            s.op("sp", lambda e, h=h, ks=ks, part=part, vdst=vdst, vh=vh: e.dma_start(
                vdst[:, ks, :],
                vtD[part * 1024:(part + 1) * 1024, h * 64:(h + 1) * 64]
                .rearrange("(kt p) d -> p kt d", p=128)),
                reads=[vtB], pwrites=[vh.buf], dma=True)
        if decay:
            cf = colf[p2]
            s.op("sp", lambda e, h=h, cf=cf: e.dma_start(
                cf.ap(slice(0, 64), slice(0, 512)),
                P.ct["colf"][h * 64:(h + 1) * 64, :]),
                reads=[P.cbuf["colf"]], writes=[cf.buf], dma=True)
        if rope:
            for (d0, s0) in ((0, 32), (32, 0), (64, 96), (96, 64)):
                s.op("sp", lambda e, d0=d0, s0=s0, qkf=qkf, sw=sw: e.dma_start(
                    sw.ap(slice(d0, d0 + 32)), qkf.ap(slice(s0, s0 + 32))),
                    reads=[qkf.buf], pwrites=[sw.buf], dma=True)
            s.op("pool", lambda e, qkf=qkf: e.tensor_mul(qkf.ap(), qkf.ap(), tabC.ap()),
                 reads=[qkf.buf, tabC.buf], pwrites=[qkf.buf])
            s.op("pool", lambda e, sw=sw: e.tensor_mul(sw.ap(), sw.ap(), tabS.ap()),
                 reads=[sw.buf, tabS.buf], pwrites=[sw.buf])
            s.op("pool", lambda e, qkf=qkf, sw=sw: e.tensor_add(
                qkf.ap(), qkf.ap(), sw.ap()),
                reads=[qkf.buf, sw.buf], pwrites=[qkf.buf])
        if decay:
            cf = colf[p2]
            for tc in range(NTC):
                s.op("pool", lambda e, tc=tc, qa=qa, qkf=qkf, cf=cf: e.tensor_mul(
                    qa.ap(slice(0, 64), slice(tc * 512, (tc + 1) * 512)),
                    qkf.ap(slice(0, 64), slice(tc * 512, (tc + 1) * 512)),
                    cf.ap(slice(0, 64), slice(0, 512))),
                    reads=[qkf.buf, cf.buf],
                    writes=[qa.buf] if tc == 0 else [], pwrites=[qa.buf] if tc else [])
        elif rope:
            s.op("pool", lambda e, qa=qa, qkf=qkf: e.tensor_copy(
                qa.ap(slice(0, 64)), qkf.ap(slice(0, 64))),
                reads=[qkf.buf], writes=[qa.buf])
        else:
            s.op("pool", lambda e, qa=qa, qkf=qkf: e.tensor_scalar_mul(
                qa.ap(slice(0, 64)), qkf.ap(slice(0, 64)), 0.125),
                reads=[qkf.buf], writes=[qa.buf])
        s.op("pool", lambda e, qkf=qkf, swb=swb: e.tensor_copy(
            swb.ap(slice(64, 128)), qkf.ap(slice(64, 128))),
            reads=[qkf.buf], pwrites=[swb.buf])
        s.op("sp", lambda e, ka=ka, swb=swb: e.dma_start(
            ka.ap(slice(0, 64)), swb.ap(slice(64, 128))),
            reads=[swb.buf], pwrites=[ka.buf], dma=True)

    def prep_late(h):
        if not soft:
            return
        p2 = h % 2
        qa, ka, swb = QA[p2], KA[p2], SWb[p2]
        po = p2 * 2048
        r64 = slice(64, 65)
        s.op("act", lambda e, ka=ka, swb=swb: e.activation(
            swb.ap(slice(0, 64)), ka.ap(slice(0, 64)), AF.Square),
            reads=[ka.buf], pwrites=[swb.buf])
        for tc in range(NTC):
            s.op("pe", lambda e, tc=tc, swb=swb: e.matmul(
                P.psum[64:65, 3584:4096], ones_b.ap(slice(0, 64), slice(0, 1)),
                swb.ap(slice(0, 64), slice(tc * 512, (tc + 1) * 512)),
                start=True, stop=True, tile_position=(0, 64)),
                reads=[swb.buf, ones_b.buf], writes=[P.pbk[7]])
            s.op("dve", lambda e, tc=tc: e.reduce_max(
                rows.ap(r64, slice(po + tc, po + tc + 1)),
                P.psum[64:65, 3584:4096], AX),
                reads=[P.pbk[7]], pwrites=[rows.buf])
        s.op("dve", lambda e: e.reduce_max(
            rows.ap(r64, slice(po + 8, po + 9)), rows.ap(r64, slice(po, po + 8)), AX),
            reads=[rows.buf], pwrites=[rows.buf])
        s.op("act", lambda e: e.activation(
            rows.ap(r64, slice(po + 9, po + 10)), rows.ap(r64, slice(po + 8, po + 9)),
            AF.Ln),
            reads=[rows.buf], pwrites=[rows.buf])
        s.op("act", lambda e, qa=qa, swb=swb: e.activation(
            swb.ap(slice(0, 64)), qa.ap(slice(0, 64)), AF.Square),
            reads=[qa.buf], pwrites=[swb.buf])
        for tc in range(NTC):
            s.op("pe", lambda e, tc=tc, swb=swb: e.matmul(
                P.psum[64:65, 3584:4096], ones_b.ap(slice(0, 64), slice(0, 1)),
                swb.ap(slice(0, 64), slice(tc * 512, (tc + 1) * 512)),
                start=True, stop=True, tile_position=(0, 64)),
                reads=[swb.buf, ones_b.buf], writes=[P.pbk[7]])
            s.op("act", lambda e, tc=tc: e.activation(
                rows.ap(r64, slice(po + 512, po + 1024)),
                P.psum[64:65, 3584:4096], AF.Ln),
                reads=[P.pbk[7]], pwrites=[rows.buf])
            s.op("dve", lambda e, tc=tc: e.tensor_scalar_add(
                rows.ap(r64, slice(po + 1024, po + 1536)),
                rows.ap(r64, slice(po + 512, po + 1024)),
                rows.ap(r64, slice(po + 9, po + 10))),
                reads=[rows.buf], pwrites=[rows.buf])
            s.op("act", lambda e, tc=tc: e.activation(
                rows.ap(r64, slice(po + 512, po + 1024)),
                rows.ap(r64, slice(po + 1024, po + 1536)), AF.Exp, scale=0.5),
                reads=[rows.buf], pwrites=[rows.buf])
            s.op("dve", lambda e, tc=tc, qa=qa: e.tensor_scalar_mul(
                qa.ap(r64, slice(tc * 512, (tc + 1) * 512)),
                rows.ap(r64, slice(po + 512, po + 1024)), -1.0),
                reads=[rows.buf], pwrites=[qa.buf])

    prep_early(0)
    prep_late(0)

    for h in range(H):
        p2 = h % 2
        qkf, qa, ka, vh = QKF[p2], QA[p2], KA[p2], Vh[p2]
        sw, swb = SW[p2], SWb[p2]
        po = p2 * 2048

        pend = []

        def flush_pend(pend=pend):
            while pend:
                pend.pop(0)()

        for qc in range(NTC):
            nkt = (4 * qc + 4) if causal else 32
            qs = slice(qc * 512, (qc + 1) * 512)
            if soft:
                ng = nkt // 2

                def em_qk(g, qs=qs, qa=qa, ka=ka):
                    b0 = (g % 3) * 2

                    def mmst(e, g=g, b0=b0):
                        last = None
                        for j in range(2):
                            kt = 2 * g + j
                            last = e.matmul(
                                P.pb(b0 + j),
                                ka.ap(slice(0, 65), slice(kt * 128, (kt + 1) * 128)),
                                qa.ap(slice(0, 65), qs),
                                start=True, stop=True)
                        return last
                    s.op("pe", mmst, reads=[qa.buf, ka.buf],
                         writes=[P.pbk[b0], P.pbk[b0 + 1]])

                def em_exp(g, qc=qc):
                    b0 = (g % 3) * 2
                    ptg = pt[g % 3]
                    s.op("act", lambda e, b0=b0, ptg=ptg: e.activation(
                        ptg.ap(), P.pb(b0, 2), AF.Exp),
                        reads=[P.pbk[b0], P.pbk[b0 + 1]], writes=[ptg.buf])
                    if causal:
                        for j in range(2):
                            kt = 2 * g + j
                            if kt >= 4 * qc:
                                mi = 3 - (kt - 4 * qc)
                                s.op("dve", lambda e, j=j, mi=mi, ptg=ptg: e.tensor_mul(
                                    ptg.ap(None, slice(j * 512, (j + 1) * 512)),
                                    ptg.ap(None, slice(j * 512, (j + 1) * 512)),
                                    msk.ap(None, slice(mi * 512, (mi + 1) * 512))),
                                    reads=[ptg.buf, msk.buf], pwrites=[ptg.buf])

                def em_pv(g, nkt=nkt, vh=vh):
                    ptg = pt[g % 3]

                    def mmpv(e, g=g, nkt=nkt, vh=vh, ptg=ptg):
                        last = None
                        for j in range(2):
                            kt = 2 * g + j
                            last = e.matmul(
                                P.psum[0:65, 3072:3584],
                                vh.ap(None, slice(kt * 65, kt * 65 + 65)),
                                ptg.ap(None, slice(j * 512, (j + 1) * 512)),
                                start=(kt == 0), stop=(kt == nkt - 1))
                        return last
                    s.op("pe", mmpv, reads=[ptg.buf, vh.buf],
                         writes=[P.pbk[6]] if g == 0 else [],
                         pwrites=[P.pbk[6]] if g else [])

                for g0 in range(min(3, ng)):
                    em_qk(g0)
                flush_pend()
                for g in range(ng):
                    em_exp(g)
                    if g + 3 < ng:
                        em_qk(g + 3)
                    em_pv(g)

                def epi(qc=qc, qs=qs, sw=sw, po=po):
                    r64 = slice(64, 65)
                    s.op("act", lambda e, qs=qs, sw=sw: e.activation(
                        sw.ap(slice(0, 65), qs), P.psum[0:65, 3072:3584], AF.Copy),
                        reads=[P.pbk[6]], pwrites=[sw.buf])
                    s.op("act", lambda e, qs=qs, sw=sw, po=po: e.activation(
                        rows.ap(r64, slice(po + 1536, po + 2048)),
                        sw.ap(slice(64, 65), qs), AF.Ln),
                        reads=[sw.buf], pwrites=[rows.buf])
                    s.op("act", lambda e, po=po: e.activation(
                        rows.ap(r64, slice(po + 1024, po + 1536)),
                        rows.ap(r64, slice(po + 1536, po + 2048)), AF.Exp,
                        scale=-1.0),
                        reads=[rows.buf], pwrites=[rows.buf])
                    s.op("pe", lambda e, po=po: e.matmul(
                        P.pb(7, rows=slice(0, 64)),
                        onesf.ap(slice(64, 65), slice(0, 64)),
                        rows.ap(r64, slice(po + 1024, po + 1536)),
                        start=True, stop=True),
                        reads=[rows.buf, onesf.buf], writes=[P.pbk[7]])
                    s.op("dve", lambda e, qs=qs, sw=sw: e.tensor_mul(
                        att.ap(slice(0, 64), qs), sw.ap(slice(0, 64), qs),
                        P.pb(7, rows=slice(0, 64))),
                        reads=[sw.buf, P.pbk[7]],
                        writes=[att.buf] if qc == 0 else [],
                        pwrites=[att.buf] if qc else [])
                pend.append(epi)
            else:
                def em_st(kt, qs=qs, qa=qa, ka=ka):
                    bank = kt % 4
                    s.op("pe", lambda e, kt=kt, bank=bank: e.matmul(
                        P.pb(bank), ka.ap(slice(0, 64), slice(kt * 128, (kt + 1) * 128)),
                        qa.ap(slice(0, 64), qs), start=True, stop=True),
                        reads=[qa.buf, ka.buf], writes=[P.pbk[bank]])

                def em_evac(kt, qc=qc, h=h):
                    bank = kt % 4
                    ptg = pt[kt % 4]
                    d0i = (qc * 512 - kt * 128 + 384) // 128
                    rfap = rf.ap(None, slice(h * 35 + d0i, h * 35 + d0i + 1))
                    if kt >= 4 * qc:
                        mi = 3 - (kt - 4 * qc)
                        s.op("dve", lambda e, bank=bank, ptg=ptg, rfap=rfap, mi=mi:
                             e.scalar_tensor_tensor(
                                 ptg.ap(), P.pb(bank), rfap,
                                 msk.ap(None, slice(mi * 512, (mi + 1) * 512)),
                                 OP.mult, OP.mult),
                             reads=[P.pbk[bank], rf.buf, msk.buf], writes=[ptg.buf])
                    elif kt % 2 == 0:
                        s.op("act", lambda e, bank=bank, ptg=ptg, rfap=rfap:
                             e.activation(ptg.ap(), P.pb(bank), AF.Copy, scale=rfap),
                             reads=[P.pbk[bank], rf.buf], writes=[ptg.buf])
                    else:
                        s.op("dve", lambda e, bank=bank, ptg=ptg, rfap=rfap:
                             e.tensor_scalar_mul(ptg.ap(), P.pb(bank), rfap),
                             reads=[P.pbk[bank], rf.buf], writes=[ptg.buf])

                def em_pvd(kt, nkt=nkt, vh=vh):
                    ptg = pt[kt % 4]
                    s.op("pe", lambda e, kt=kt, nkt=nkt, vh=vh, ptg=ptg: e.matmul(
                        P.psum[0:64, 3072:3584],
                        vh.ap(None, slice(kt * 64, (kt + 1) * 64)),
                        ptg.ap(), start=(kt == 0), stop=(kt == nkt - 1)),
                        reads=[ptg.buf, vh.buf],
                        writes=[P.pbk[6]] if kt == 0 else [],
                        pwrites=[P.pbk[6]] if kt else [])

                for kt0 in range(min(4, nkt)):
                    em_st(kt0)
                flush_pend()
                for kt in range(nkt):
                    em_evac(kt)
                    if kt + 4 < nkt:
                        em_st(kt + 4)
                    em_pvd(kt)

                def epi(qc=qc, qs=qs, h=h, sw=sw):
                    robs, rosqs = rob[qc % 2], rosq[qc % 2]
                    r1 = slice(0, 1)
                    s.op("act", lambda e, qs=qs, sw=sw: e.activation(
                        sw.ap(slice(0, 64), qs), P.psum[0:64, 3072:3584], AF.Copy),
                        reads=[P.pbk[6]], pwrites=[sw.buf])
                    s.op("dve", lambda e, qs=qs, robs=robs, sw=sw: e.tensor_copy(
                        robs.ap(slice(0, 64)), sw.ap(slice(0, 64), qs)),
                        reads=[sw.buf], writes=[robs.buf])
                    s.op("act", lambda e, qs=qs, rosqs=rosqs, sw=sw: e.activation(
                        rosqs.ap(slice(0, 64)), sw.ap(slice(0, 64), qs), AF.Square),
                        reads=[sw.buf], writes=[rosqs.buf])
                    s.op("pe", lambda e, robs=robs: e.matmul(
                        P.psum[0:1, 3584:4096], ones_b.ap(slice(0, 64), slice(0, 1)),
                        robs.ap(slice(0, 64)), start=True, stop=True),
                        reads=[robs.buf, ones_b.buf], writes=[P.pbk[7]])
                    s.op("pe", lambda e, rosqs=rosqs: e.matmul(
                        P.psum[0:1, 2560:3072], ones_b.ap(slice(0, 64), slice(0, 1)),
                        rosqs.ap(slice(0, 64)), start=True, stop=True),
                        reads=[rosqs.buf, ones_b.buf], writes=[P.pbk[5]])
                    s.op("act", lambda e: e.activation(
                        rows.ap(r1, slice(0, 512)), P.psum[0:1, 3584:4096],
                        AF.Copy, scale=1.0 / 64),
                        reads=[P.pbk[7]], pwrites=[rows.buf])
                    s.op("act", lambda e: e.activation(
                        rows.ap(r1, slice(512, 1024)), rows.ap(r1, slice(0, 512)),
                        AF.Square),
                        reads=[rows.buf], pwrites=[rows.buf])
                    s.op("dve", lambda e: e.scalar_tensor_tensor(
                        rows.ap(r1, slice(1024, 1536)), P.psum[0:1, 2560:3072],
                        1.0 / 64, rows.ap(r1, slice(512, 1024)),
                        OP.mult, OP.subtract),
                        reads=[P.pbk[5], rows.buf], pwrites=[rows.buf])
                    s.op("act", lambda e: e.activation(
                        rows.ap(r1, slice(1536, 2048)), rows.ap(r1, slice(1024, 1536)),
                        AF.Ln, bias=1e-5),
                        reads=[rows.buf], pwrites=[rows.buf])
                    s.op("act", lambda e: e.activation(
                        rows.ap(r1, slice(2048, 2560)), rows.ap(r1, slice(1536, 2048)),
                        AF.Exp, scale=-0.5),
                        reads=[rows.buf], pwrites=[rows.buf])
                    s.op("pe", lambda e: e.matmul(
                        P.pb(4, rows=slice(0, 64)), onesf.ap(slice(0, 1), slice(0, 64)),
                        rows.ap(r1, slice(0, 512)), start=True, stop=True),
                        reads=[rows.buf, onesf.buf], writes=[P.pbk[4]])
                    s.op("pe", lambda e: e.matmul(
                        P.pb(5, rows=slice(0, 64)), onesf.ap(slice(0, 1), slice(0, 64)),
                        rows.ap(r1, slice(2048, 2560)), start=True, stop=True),
                        reads=[rows.buf, onesf.buf], writes=[P.pbk[5]])
                    s.op("dve", lambda e, qs=qs, sw=sw: e.tensor_sub(
                        sw.ap(slice(0, 64), qs), sw.ap(slice(0, 64), qs),
                        P.pb(4, rows=slice(0, 64))),
                        reads=[sw.buf, P.pbk[4]], pwrites=[sw.buf])
                    s.op("dve", lambda e, qs=qs, sw=sw: e.tensor_mul(
                        sw.ap(slice(0, 64), qs), sw.ap(slice(0, 64), qs),
                        P.pb(5, rows=slice(0, 64))),
                        reads=[sw.buf, P.pbk[5]], pwrites=[sw.buf])
                    s.op("dve", lambda e, qs=qs, h=h, sw=sw: e.tensor_scalar(
                        att.ap(slice(0, 64), qs), sw.ap(slice(0, 64), qs),
                        gv.ap(slice(0, 64), slice(h, h + 1)),
                        gb.ap(slice(0, 64), slice(h, h + 1)), OP.mult, OP.add),
                        reads=[sw.buf, gv.buf, gb.buf],
                        writes=[att.buf] if qc == 0 else [],
                        pwrites=[att.buf] if qc else [])
                pend.append(epi)
            if qc == 0 and h + 1 < H:
                prep_early(h + 1)
            if qc == NTC - 2 and h + 1 < H:
                prep_late(h + 1)
        flush_pend()
        s.op("sp", lambda e, h=h: e.dma_start(
            hdD[h * 64:(h + 1) * 64, :], att.ap(slice(0, 64))),
            reads=[att.buf], pwrites=[hdB], dma=True)


def attn_out(P, cfg):
    s = P.s
    ostats = cfg.get("out_stats", False)
    hdD, hdB = P.dram[cfg["hd"]], P.dbuf[cfg["hd"]]
    resD, resB = P.dram[cfg["res"]], P.dbuf[cfg["res"]]
    dstD, dstB = P.dram[cfg["dst"]], P.dbuf[cfg["dst"]]
    wC, wCb = P.cache(cfg["ow"])
    hd = [P.ab(4096, f"o_hd{c}") for c in range(8)]
    ww = [P.ab(1024, f"o_w{c}") for c in range(8)]
    for c in range(8):
        hdc, wwc = hd[c], ww[c]
        s.op("sp", lambda e, c=c, t=hdc: e.dma_start(
            t.ap(), hdD[c * 128:(c + 1) * 128, :]),
            reads=[hdB], writes=[hdc.buf], dma=True)
        s.op("sp", lambda e, c=c, t=wwc: e.dma_start(
            t.ap(), wC[c * 128:(c + 1) * 128, :]),
            reads=[wCb], writes=[wwc.buf], dma=True)
    obv = None
    if cfg["ob"] is not None:
        obv = P.af(16, "o_b")
        load_weight_f32(P, cfg["ob"], obv, cols=8)
    xr = [P.af(4096, f"o_x{i}") for i in range(2)]
    ou = [P.af(4096, f"o_o{i}") for i in range(2)]
    if ostats:
        sqb = [P.ab(512, f"o_s{i}") for i in range(2)]
        ones_b = P.ab(16, "o_onesb")
        load_const(P, "ones_bf", ones_b, 128, 1)
    for o in range(8):
        xrs, ous = xr[o % 2], ou[o % 2]
        s.op("sp", lambda e, o=o, xrs=xrs: e.dma_start(
            xrs.ap(), resD[o * 128:(o + 1) * 128, :]),
            reads=[resB], writes=[xrs.buf], dma=True)
        for tc in range(NTC):
            bank = tc % 4

            def mko(o=o, tc=tc, bank=bank):
                def f(e):
                    last = None
                    for c in range(8):
                        last = e.matmul(
                            P.pb(bank), ww[c].ap(None, slice(o * 128, (o + 1) * 128)),
                            hd[c].ap(None, slice(tc * 512, (tc + 1) * 512)),
                            start=(c == 0), stop=(c == 7))
                    return last
                return f
            s.op("pe", mko(), reads=[t.buf for t in ww] + [t.buf for t in hd],
                 writes=[P.pbk[bank]])
            bias = obv.ap(None, slice(o, o + 1)) if obv is not None else 0.0
            s.op("dve", lambda e, tc=tc, bank=bank, ous=ous, xrs=xrs, bias=bias:
                 e.scalar_tensor_tensor(
                     ous.ap(None, slice(tc * 512, (tc + 1) * 512)), P.pb(bank), bias,
                     xrs.ap(None, slice(tc * 512, (tc + 1) * 512)), OP.add, OP.add),
                 reads=[P.pbk[bank], xrs.buf] + ([obv.buf] if obv is not None else []),
                 writes=[ous.buf] if tc == 0 else [], pwrites=[ous.buf] if tc else [])
            if ostats:
                sb2 = sqb[tc % 2]
                sr = 32 * (tc // 4)
                sbk = 4 + (tc % 4)
                s.op("act", lambda e, tc=tc, ous=ous, sb2=sb2: e.activation(
                    sb2.ap(), ous.ap(None, slice(tc * 512, (tc + 1) * 512)), AF.Square),
                    reads=[ous.buf], writes=[sb2.buf])
                s.op("pe", lambda e, o=o, sb2=sb2, sr=sr, sbk=sbk: e.matmul(
                    P.psum[sr:sr + 1, sbk * 512:(sbk + 1) * 512],
                    ones_b.ap(slice(0, 128), slice(0, 1)), sb2.ap(),
                    start=(o == 0), stop=(o == 7), tile_position=(0, sr)),
                    reads=[sb2.buf, ones_b.buf],
                    writes=[P.pbk[sbk]] if o == 0 and tc < 4 else [],
                    pwrites=[P.pbk[sbk]] if not (o == 0 and tc < 4) else [])
        s.op("sp", lambda e, o=o, ous=ous: e.dma_start(
            dstD[o * 128:(o + 1) * 128, :], ous.ap()),
            reads=[ous.buf], pwrites=[dstB], dma=True)


def ph_moe(P):
    s = P.s
    x1D, x1B = P.dram["x1T"], P.dbuf["x1T"]
    paD, paB = P.dram["partialT"], P.dbuf["partialT"]
    y0D, y0B = P.dram["y0T"], P.dbuf["y0T"]
    gC, gCb = P.cache("gateT")
    w1C, w1Cb = P.cache("moe_w1")
    w2C, w2Cb = P.cache("moe_w2")
    w3C, w3Cb = P.cache("moe_w3")
    b3C, b3Cb = P.cache("moe_b3")

    n2 = P.af(16, "m_n2")
    load_weight_f32(P, "norm2w", n2, cols=8)
    b1v = P.af(144, "m_b1")
    b2v = P.af(144, "m_b2")
    for vec, wn in ((b1v, "moe_b1"), (b2v, "moe_b2")):
        src = P.wext[wn]
        s.op("sp", lambda e, vec=vec, src=src: e.dma_start(
            vec.ap(None, slice(0, 128)).rearrange("p (e k) -> p e k", k=16),
            src.rearrange("e p k -> p e k")),
            reads=[P.dbuf["w_" + wn]], writes=[vec.buf], dma=True)
    onesf = P.af(144, "m_onesf")
    for r0 in (0, 32):
        s.op("sp", lambda e, r0=r0: e.dma_start(
            onesf.ap(slice(r0, r0 + 1), slice(0, 128)), P.ct["ones_f"].ap()),
            reads=[P.cbuf["ones_f"]], pwrites=[onesf.buf], dma=True)
    ones_b = P.ab(16, "m_onesb")
    load_const(P, "ones_bf", ones_b, 128, 1)
    ident = P.ab(128, "m_ident")
    load_const(P, "ident", ident, 128, 128)
    sel8 = P.ab(1024, "m_sel8")
    load_const(P, "sel8", sel8, 8, 1024)
    gsb = P.ab(64, "m_gate")
    s.op("sp", lambda e: e.dma_start(
        gsb.ap(None, slice(0, 64)).rearrange("p (c e) -> p c e", e=8),
        gC.rearrange("(c p) e -> p c e", p=128)),
        reads=[gCb], writes=[gsb.buf], dma=True)
    b3sb = P.ab(1024, "m_b3")
    s.op("sp", lambda e: e.dma_start(b3sb.ap(slice(0, 8)), b3C.ap()),
         reads=[b3Cb], writes=[b3sb.buf], dma=True)

    xin = [P.af(512, f"m_x{i}") for i in range(4)]
    sq = [P.ab(512, f"m_q{i}") for i in range(4)]
    r2 = [P.ab(512, f"m_r2{c}") for c in range(8)]
    acc = [P.af(512, f"m_ac{o}") for o in range(8)]
    lg = P.af(16, "m_lg")
    tp1 = P.af(16, "m_t1")
    tp2 = P.af(16, "m_t2")
    cmb = P.ab(16, "m_cmb")
    combT = P.ab(512, "m_combT")
    combB = P.ab(512, "m_combB")
    w1sb = [P.ab(16384, f"m_w1{i}") for i in range(2)]
    w2sb = [P.ab(16384, "m_w20")]
    w3r = [P.ab(1024, f"m_w3{i}") for i in range(2)]
    usil = P.af(512, "m_us")
    hsb = P.ab(8192, "m_h")
    t1s = P.af(512, "m_t1s")
    yo = [P.af(512, f"m_yo{i}") for i in range(2)]
    rows2 = P.af(4096, "m_rows")
    srow = SBuf(P.ar.t, hsb.c0, 4096, F32, "m_srow")
    srow.buf = hsb.buf

    assert MODE == "fused"
    r1 = slice(0, 1)
    for ch in range(8):
        r = 32 * (ch // 4)
        bank = 4 + (ch % 4)
        seg = slice(ch * 512, (ch + 1) * 512)
        s.op("act", lambda e, r=r, bank=bank, seg=seg: e.activation(
            srow.ap(slice(r, r + 1), seg),
            P.psum[r:r + 1, bank * 512:(bank + 1) * 512],
            AF.Ln, scale=1.0 / C, bias=1e-6),
            reads=[P.pbk[bank]],
            writes=[srow.buf] if ch == 0 else [], pwrites=[srow.buf] if ch else [])
        s.op("act", lambda e, r=r, seg=seg: e.activation(
            rows2.ap(slice(r, r + 1), seg), srow.ap(slice(r, r + 1), seg),
            AF.Exp, scale=-0.5),
            reads=[srow.buf],
            writes=[rows2.buf] if ch == 0 else [], pwrites=[rows2.buf] if ch else [])

    def ldw(slot, cache, cb, e):
        for c in range(8):
            s.op("sp", lambda e2, c=c, slot=slot, cache=cache, e=e: e2.dma_start(
                slot.ap(None, slice(c * 2048, (c + 1) * 2048)),
                cache[e, c * 128:(c + 1) * 128, :]),
                reads=[cb], writes=[slot.buf] if c == 0 else [],
                pwrites=[slot.buf] if c else [], dma=True)

    for ch in range(8):
        ts = slice(ch * 512, (ch + 1) * 512)
        rr = 32 * (ch // 4)
        s.op("pe", lambda e, ts=ts, rr=rr: e.matmul(
            P.pb(6), onesf.ap(slice(rr, rr + 1), slice(0, 128)),
            rows2.ap(slice(rr, rr + 1), ts), start=True, stop=True),
            reads=[rows2.buf, onesf.buf], writes=[P.pbk[6]])
        for c in range(8):
            xb, xb2 = xin[c % 2], xin[2 + c % 2]
            s.op("sp", lambda e, c=c, ts=ts, xb=xb: e.dma_start(
                xb.ap(), x1D[c * 128:(c + 1) * 128, ts]),
                reads=[x1B], writes=[xb.buf], dma=True)
            s.op("dve", lambda e, xb=xb, xb2=xb2: e.tensor_mul(
                xb2.ap(), xb.ap(), P.pb(6)),
                reads=[xb.buf, P.pbk[6]], writes=[xb2.buf])
            s.op("act", lambda e, c=c, xb2=xb2: e.activation(
                r2[c].ap(), xb2.ap(), AF.Copy,
                scale=n2.ap(None, slice(c, c + 1))),
                reads=[xb2.buf, n2.buf], writes=[r2[c].buf])
        for tt in range(4):
            def gmm(e, tt=tt):
                last = None
                for c in range(8):
                    last = e.matmul(P.psum[0:128, 3584:3592],
                                    r2[c].ap(None, slice(tt * 128, (tt + 1) * 128)),
                                    gsb.ap(None, slice(c * 8, (c + 1) * 8)),
                                    start=(c == 0), stop=(c == 7))
                return last
            s.op("pe", gmm, reads=[t.buf for t in r2] + [gsb.buf], writes=[P.pbk[7]])
            s.op("act", lambda e: e.activation(lg.ap(None, slice(0, 8)),
                                               P.psum[0:128, 3584:3592], AF.Copy),
                 reads=[P.pbk[7]], writes=[lg.buf])
            s.op("dve", lambda e: e.reduce_max(tp1.ap(None, slice(0, 1)),
                                               lg.ap(None, slice(0, 8)), AX),
                 reads=[lg.buf], writes=[tp1.buf])
            s.op("dve", lambda e: e.tensor_scalar(
                tp2.ap(None, slice(0, 8)), lg.ap(None, slice(0, 8)),
                tp1.ap(None, slice(0, 1)), -1e30, OP.is_equal, OP.mult),
                reads=[lg.buf, tp1.buf], writes=[tp2.buf])
            s.op("dve", lambda e: e.tensor_add(
                tp2.ap(None, slice(8, 16)), tp2.ap(None, slice(0, 8)),
                lg.ap(None, slice(0, 8))),
                reads=[tp2.buf, lg.buf], pwrites=[tp2.buf])
            s.op("dve", lambda e: e.reduce_max(tp1.ap(None, slice(1, 2)),
                                               tp2.ap(None, slice(8, 16)), AX),
                 reads=[tp2.buf], pwrites=[tp1.buf])
            s.op("dve", lambda e: e.tensor_scalar_mul(
                tp1.ap(None, slice(2, 3)), tp1.ap(None, slice(0, 1)), -1.0),
                reads=[tp1.buf], pwrites=[tp1.buf])
            s.op("act", lambda e: e.activation(
                tp1.ap(None, slice(3, 4)), tp1.ap(None, slice(1, 2)), AF.Exp,
                bias=tp1.ap(None, slice(2, 3))),
                reads=[tp1.buf], pwrites=[tp1.buf])
            s.op("dve", lambda e: e.tensor_scalar_add(
                tp1.ap(None, slice(4, 5)), tp1.ap(None, slice(3, 4)), 1.0),
                reads=[tp1.buf], pwrites=[tp1.buf])
            s.op("dve", lambda e: e.reciprocal(
                tp1.ap(None, slice(5, 6)), tp1.ap(None, slice(4, 5))),
                reads=[tp1.buf], pwrites=[tp1.buf])
            s.op("act", lambda e: e.activation(
                tp2.ap(None, slice(0, 8)), lg.ap(None, slice(0, 8)), AF.Exp,
                bias=tp1.ap(None, slice(2, 3))),
                reads=[lg.buf, tp1.buf], pwrites=[tp2.buf])
            s.op("dve", lambda e: e.tensor_scalar(
                tp2.ap(None, slice(8, 16)), lg.ap(None, slice(0, 8)),
                tp1.ap(None, slice(1, 2)), None, OP.is_ge),
                reads=[lg.buf, tp1.buf], pwrites=[tp2.buf])
            s.op("dve", lambda e: e.tensor_mul(
                tp2.ap(None, slice(0, 8)), tp2.ap(None, slice(0, 8)),
                tp2.ap(None, slice(8, 16))),
                reads=[tp2.buf], pwrites=[tp2.buf])
            s.op("dve", lambda e: e.tensor_scalar_mul(
                cmb.ap(None, slice(0, 8)), tp2.ap(None, slice(0, 8)),
                tp1.ap(None, slice(5, 6))),
                reads=[tp2.buf, tp1.buf], writes=[cmb.buf])
            s.op("pe", lambda e: e.transpose(
                P.psum[:, 1536:2048].bitcast(BF16)[0:8, 0:128],
                cmb.ap(None, slice(0, 8)), ident.ap()),
                reads=[cmb.buf, ident.buf], writes=[P.pbk[3]])
            s.op("act", lambda e, tt=tt: e.activation(
                combT.ap(slice(0, 8), slice(tt * 128, (tt + 1) * 128)),
                P.psum[:, 1536:2048].bitcast(BF16)[0:8, 0:128], AF.Copy),
                reads=[P.pbk[3]],
                writes=[combT.buf] if tt == 0 else [],
                pwrites=[combT.buf] if tt else [])
        for o in range(8):
            bank = o % 4
            s.op("pe", lambda e, o=o, bank=bank: e.matmul(
                P.pb(bank), b3sb.ap(slice(0, 8), slice(o * 128, (o + 1) * 128)),
                combT.ap(slice(0, 8)), start=True, stop=True),
                reads=[b3sb.buf, combT.buf], writes=[P.pbk[bank]])
            s.op("act", lambda e, o=o, bank=bank: e.activation(
                acc[o].ap(), P.pb(bank), AF.Copy),
                reads=[P.pbk[bank]], writes=[acc[o].buf])
        for ex in range(8):
            w1s, w2s = w1sb[ex % 2], w2sb[0]
            ldw(w1s, w1C, w1Cb, ex)
            ldw(w2s, w2C, w2Cb, ex)
            s.op("pe", lambda e, ex=ex: e.matmul(
                P.pb(4),
                sel8.ap(slice(0, 8), slice(ex * 128, (ex + 1) * 128)),
                combT.ap(slice(0, 8)), start=True, stop=True),
                reads=[sel8.buf, combT.buf], writes=[P.pbk[4]])
            s.op("dve", lambda e: e.tensor_copy(
                combB.ap(), P.pb(4)),
                reads=[P.pbk[4]], writes=[combB.buf])
            for ht in range(16):
                a = (ht % 2) * 2

                def umm(e, ht=ht, a=a, ws=w1s):
                    last = None
                    for c in range(8):
                        last = e.matmul(
                            P.pb(a),
                            ws.ap(None, slice(c * 2048 + ht * 128,
                                              c * 2048 + (ht + 1) * 128)),
                            r2[c].ap(), start=(c == 0), stop=(c == 7))
                    return last

                def umm2(e, ht=ht, a=a, ws=w2s):
                    last = None
                    for c in range(8):
                        last = e.matmul(
                            P.pb(a + 1),
                            ws.ap(None, slice(c * 2048 + ht * 128,
                                              c * 2048 + (ht + 1) * 128)),
                            r2[c].ap(), start=(c == 0), stop=(c == 7))
                    return last
                s.op("pe", umm, reads=[w1s.buf] + [t.buf for t in r2],
                     writes=[P.pbk[a]])
                s.op("pe", umm2, reads=[w2s.buf] + [t.buf for t in r2],
                     writes=[P.pbk[a + 1]])
                s.op("act", lambda e, ht=ht, a=a, ex=ex: e.activation(
                    usil.ap(), P.pb(a), AF.Silu,
                    bias=b1v.ap(None, slice(ex * 16 + ht, ex * 16 + ht + 1))),
                    reads=[P.pbk[a], b1v.buf], writes=[usil.buf])
                s.op("dve", lambda e, ht=ht, a=a, ex=ex: e.scalar_tensor_tensor(
                    hsb.ap(None, slice(ht * 512, (ht + 1) * 512)), P.pb(a + 1),
                    b2v.ap(None, slice(ex * 16 + ht, ex * 16 + ht + 1)),
                    usil.ap(), OP.add, OP.mult),
                    reads=[P.pbk[a + 1], b2v.buf, usil.buf],
                    writes=[hsb.buf] if ht == 0 else [],
                    pwrites=[hsb.buf] if ht else [])
            for ht in range(16):
                w3s = w3r[ht % 2]
                s.op("sp", lambda e, ht=ht, ex=ex, w3s=w3s: e.dma_start(
                    w3s.ap(), w3C[ex, ht * 128:(ht + 1) * 128, :]),
                    reads=[w3Cb], writes=[w3s.buf], dma=True)

                def w3mm(e, ht=ht, w3s=w3s):
                    last = None
                    for o in range(8):
                        last = e.matmul(
                            P.pb(o), w3s.ap(None, slice(o * 128, (o + 1) * 128)),
                            hsb.ap(None, slice(ht * 512, (ht + 1) * 512)),
                            start=(ht == 0), stop=(ht == 15))
                    return last
                s.op("pe", w3mm, reads=[w3s.buf, hsb.buf],
                     writes=[P.pbk[o2] for o2 in range(8)] if ht == 0 else [],
                     pwrites=[P.pbk[o2] for o2 in range(8)] if ht else [])
            for o in range(8):
                s.op("dve", lambda e, o=o: e.tensor_mul(
                    t1s.ap(), P.pb(o), combB.ap()),
                    reads=[P.pbk[o], combB.buf], writes=[t1s.buf])
                s.op("pool", lambda e, o=o: e.tensor_add(
                    acc[o].ap(), acc[o].ap(), t1s.ap()),
                    reads=[acc[o].buf, t1s.buf], pwrites=[acc[o].buf])
        for o in range(8):
            pi, x2 = xin[o % 2], xin[2 + o % 2]
            s.op("sp", lambda e, o=o, ts=ts, pi=pi: e.dma_start(
                pi.ap(), paD[o * 128:(o + 1) * 128, ts]),
                reads=[paB], writes=[pi.buf], dma=True)
            s.op("sp", lambda e, o=o, ts=ts, x2=x2: e.dma_start(
                x2.ap(), x1D[o * 128:(o + 1) * 128, ts]),
                reads=[x1B], writes=[x2.buf], dma=True)
            yos = yo[o % 2]
            s.op("dve", lambda e, o=o, x2=x2, yos=yos: e.tensor_add(
                yos.ap(), acc[o].ap(), x2.ap()),
                reads=[acc[o].buf, x2.buf], writes=[yos.buf])
            s.op("pool", lambda e, o=o, pi=pi, yos=yos: e.tensor_add(
                yos.ap(), yos.ap(), pi.ap()),
                reads=[yos.buf, pi.buf], pwrites=[yos.buf])
            s.op("sp", lambda e, o=o, ts=ts, yos=yos: e.dma_start(
                y0D[o * 128:(o + 1) * 128, ts], yos.ap()),
                reads=[yos.buf], pwrites=[y0B], dma=True)


def ph_ff1(P):
    s = P.s
    ynD, ynB = P.dram["ynTb"], P.dbuf["ynTb"]
    p1D, p1B = P.dram["p1T"], P.dbuf["p1T"]
    outD, outB = P.dram["outT"], P.dbuf["outT"]
    w1C, w1Cb = P.cache("ff_w1")
    w2C, w2Cb = P.cache("ff_w2")
    h1C, h1Cb = P.cache("ff_h1")
    h2C, h2Cb = P.cache("ff_h2")
    w3C, w3Cb = P.cache("ff_w3")
    s1 = P.af(32, "f_s1")
    s2 = P.af(32, "f_s2")
    hs1 = P.af(32, "f_hs1")
    hs2 = P.af(32, "f_hs2")
    s3 = P.af(16, "f_s3")
    load_weight_f32(P, "ff_s1", s1, cols=32)
    load_weight_f32(P, "ff_s2", s2, cols=32)
    load_weight_f32(P, "ff_hs1", hs1, cols=32)
    load_weight_f32(P, "ff_hs2", hs2, cols=32)
    load_weight_f32(P, "ff_s3", s3, cols=8)

    yn = [P.ab(1024, f"f_y{c}") for c in range(8)]
    hb = P.ab(32768, "f_h")
    gb = P.ab(32768, "f_g")
    wA = [P.ab(1024, f"f_wa{c}") for c in range(8)]
    wB = [P.ab(1024, f"f_wb{c}") for c in range(8)]
    wr = [P.ab(1024, f"f_wr{i}") for i in range(4)]
    usil = [P.af(512, f"f_u{i}") for i in range(4)]
    p1b = []
    for i in range(4):
        b = SBuf(P.ar.t, gb.c0 + i * 2048, 1024, F32, f"f_p{i}")
        b.buf = gb.buf
        p1b.append(b)
    ob = []
    for i in range(2):
        b = SBuf(P.ar.t, gb.c0 + (4 + i) * 2048, 1024, F32, f"f_o{i}")
        b.buf = gb.buf
        ob.append(b)
    TCW = 1024

    for ch in range(4):
        ts = slice(ch * TCW, (ch + 1) * TCW)
        for c in range(8):
            s.op("sp", lambda e, c=c, ts=ts: e.dma_start(
                yn[c].ap(), ynD[c * 128:(c + 1) * 128, ts]),
                reads=[ynB], writes=[yn[c].buf], dma=True)
        for ob4 in range(4):
            for c in range(8):
                s.op("sp", lambda e, c=c, ob4=ob4: e.dma_start(
                    wA[c].ap(), w1C[c * 128:(c + 1) * 128,
                                    ob4 * 1024:(ob4 + 1) * 1024]),
                    reads=[w1Cb], writes=[wA[c].buf], dma=True)
                s.op("pool", lambda e, c=c, ob4=ob4: e.dma_start(
                    wB[c].ap(), w2C[c * 128:(c + 1) * 128,
                                    ob4 * 1024:(ob4 + 1) * 1024]),
                    reads=[w2Cb], writes=[wB[c].buf], dma=True)
            for oi in range(8):
                ot = ob4 * 8 + oi
                for sub in range(2):
                    b1k = (oi * 2 + sub) % 4
                    b2k = 4 + b1k
                    us = usil[b1k]

                    def mku(e, oi=oi, sub=sub, b1k=b1k):
                        last = None
                        for c in range(8):
                            last = e.matmul(
                                P.pb(b1k),
                                wA[c].ap(None, slice(oi * 128, (oi + 1) * 128)),
                                yn[c].ap(None, slice(sub * 512, (sub + 1) * 512)),
                                start=(c == 0), stop=(c == 7))
                        return last

                    def mku2(e, oi=oi, sub=sub, b2k=b2k):
                        last = None
                        for c in range(8):
                            last = e.matmul(
                                P.pb(b2k),
                                wB[c].ap(None, slice(oi * 128, (oi + 1) * 128)),
                                yn[c].ap(None, slice(sub * 512, (sub + 1) * 512)),
                                start=(c == 0), stop=(c == 7))
                        return last
                    s.op("pe", mku, reads=[t.buf for t in wA] + [t.buf for t in yn],
                         writes=[P.pbk[b1k]])
                    s.op("pe", mku2, reads=[t.buf for t in wB] + [t.buf for t in yn],
                         writes=[P.pbk[b2k]])
                    s.op("act", lambda e, ot=ot, b1k=b1k, us=us: e.activation(
                        us.ap(), P.pb(b1k), AF.Silu,
                        scale=s1.ap(None, slice(ot, ot + 1))),
                        reads=[P.pbk[b1k], s1.buf], writes=[us.buf])
                    s.op("dve", lambda e, ot=ot, sub=sub, b2k=b2k, us=us:
                         e.scalar_tensor_tensor(
                             hb.ap(None, slice(ot * 1024 + sub * 512,
                                               ot * 1024 + (sub + 1) * 512)),
                             P.pb(b2k), s2.ap(None, slice(ot, ot + 1)),
                             us.ap(), OP.mult, OP.mult),
                         reads=[P.pbk[b2k], s2.buf, us.buf],
                         writes=[hb.buf] if ot == 0 and sub == 0 else [],
                         pwrites=[hb.buf] if (ot or sub) else [])
        for (wc, wcb, hs, src, dst) in ((h1C, h1Cb, hs1, hb, gb),
                                        (h2C, h2Cb, hs2, gb, hb)):
            for obk in range(8):
                for k2 in range(16):
                    wrs = wr[k2 % 4]
                    s.op("sp" if k2 % 2 == 0 else "pool",
                         lambda e, k2=k2, obk=obk, wrs=wrs, wc=wc: e.dma_start(
                             wrs.ap(None, slice(0, 1024)).rearrange(
                                 "p (k o) -> p k o", k=2),
                             wc[k2 * 256:(k2 + 1) * 256,
                                obk * 512:(obk + 1) * 512].rearrange(
                                 "(k p) o -> p k o", p=128)),
                         reads=[wcb], writes=[wrs.buf], dma=True)

                    def mkh(e, k2=k2, obk=obk, wrs=wrs, src=src):
                        last = None
                        for kk in range(2):
                            kt = k2 * 2 + kk
                            for sub in range(2):
                                for oi in range(4):
                                    last = e.matmul(
                                        P.pb(sub * 4 + oi),
                                        wrs.ap(None, slice(kk * 512 + oi * 128,
                                                           kk * 512 + (oi + 1) * 128)),
                                        src.ap(None, slice(kt * 1024 + sub * 512,
                                                           kt * 1024 + (sub + 1) * 512)),
                                        start=(kt == 0), stop=(kt == 31))
                        return last
                    s.op("pe", mkh, reads=[wrs.buf, src.buf],
                         writes=[P.pbk[i] for i in range(8)] if k2 == 0 else [],
                         pwrites=[P.pbk[i] for i in range(8)] if k2 else [])
                for sub in range(2):
                    for oi in range(4):
                        ot = obk * 4 + oi
                        s.op("act", lambda e, ot=ot, sub=sub, oi=oi, hs=hs, dst=dst:
                             e.activation(
                                 dst.ap(None, slice(ot * 1024 + sub * 512,
                                                    ot * 1024 + (sub + 1) * 512)),
                                 P.pb(sub * 4 + oi), AF.Gelu,
                                 scale=hs.ap(None, slice(ot, ot + 1))),
                             reads=[P.pbk[sub * 4 + oi], hs.buf],
                             writes=[dst.buf] if obk == 0 and ot == 0 and sub == 0 else [],
                             pwrites=[dst.buf] if (obk or ot or sub) else [])
        for obk in range(2):
            for oi in range(4):
                o = obk * 4 + oi
                pb1 = p1b[oi]
                s.op("sp", lambda e, o=o, ts=ts, pb1=pb1: e.dma_start(
                    pb1.ap(), p1D[o * 128:(o + 1) * 128, ts]),
                    reads=[p1B], writes=[pb1.buf], dma=True)
            for k2 in range(16):
                wrs = wr[k2 % 4]
                s.op("sp" if k2 % 2 == 0 else "pool",
                     lambda e, k2=k2, obk=obk, wrs=wrs: e.dma_start(
                         wrs.ap(None, slice(0, 1024)).rearrange(
                             "p (k o) -> p k o", k=2),
                         w3C[k2 * 256:(k2 + 1) * 256,
                             obk * 512:(obk + 1) * 512].rearrange(
                             "(k p) o -> p k o", p=128)),
                     reads=[w3Cb], writes=[wrs.buf], dma=True)

                def mk3(e, k2=k2, obk=obk, wrs=wrs):
                    last = None
                    for kk in range(2):
                        kt = k2 * 2 + kk
                        for sub in range(2):
                            for oi in range(4):
                                last = e.matmul(
                                    P.pb(sub * 4 + oi),
                                    wrs.ap(None, slice(kk * 512 + oi * 128,
                                                       kk * 512 + (oi + 1) * 128)),
                                    hb.ap(None, slice(kt * 1024 + sub * 512,
                                                      kt * 1024 + (sub + 1) * 512)),
                                    start=(kt == 0), stop=(kt == 31))
                    return last
                s.op("pe", mk3, reads=[wrs.buf, hb.buf],
                     writes=[P.pbk[i] for i in range(8)] if k2 == 0 else [],
                     pwrites=[P.pbk[i] for i in range(8)] if k2 else [])
            for oi in range(4):
                o = obk * 4 + oi
                pb1 = p1b[oi]
                obs = ob[oi % 2]
                for sub in range(2):
                    s.op("dve", lambda e, o=o, oi=oi, sub=sub, pb1=pb1, obs=obs:
                         e.scalar_tensor_tensor(
                             obs.ap(None, slice(sub * 512, (sub + 1) * 512)),
                             P.pb(sub * 4 + oi), s3.ap(None, slice(o, o + 1)),
                             pb1.ap(None, slice(sub * 512, (sub + 1) * 512)),
                             OP.mult, OP.add),
                         reads=[P.pbk[sub * 4 + oi], s3.buf, pb1.buf],
                         writes=[obs.buf] if sub == 0 else [],
                         pwrites=[obs.buf] if sub else [])
                s.op("sp", lambda e, o=o, ts=ts, obs=obs: e.dma_start(
                    outD[o * 128:(o + 1) * 128, ts], obs.ap()),
                    reads=[obs.buf], pwrites=[outB], dma=True)


PHASE_FNS = {
    "ln0": ph_ln0, "ln1": ph_ln1,
    "moe": ph_moe, "ff1": ph_ff1,
    "retp": lambda P: attn_prep(P, ACFG["ret"]),
    "reth": lambda P: attn_heads(P, ACFG["ret"]),
    "reto": lambda P: attn_out(P, ACFG["ret"]),
    "fap": lambda P: attn_prep(P, ACFG["fa"]),
    "fah": lambda P: attn_heads(P, ACFG["fa"]),
    "fao": lambda P: attn_out(P, ACFG["fa"]),
    "a1p": lambda P: attn_prep(P, ACFG["at1"]),
    "a1h": lambda P: attn_heads(P, ACFG["at1"]),
    "a1o": lambda P: attn_out(P, ACFG["at1"]),
}


def _run(inputs, phase_groups, taps=()):
    consts = make_consts()
    w = prep_weights(inputs)
    wshapes = {k: v.shape for k, v in w.items()}
    x = inputs["L0_attn_x"]
    xTs = [np.ascontiguousarray(np.asarray(x[i], dtype=np.float32).T) for i in range(8)]

    hbm = [{} for _ in range(8)]
    out = None
    taps_res = {}
    for gi, group in enumerate(phase_groups):
        later = set()
        for g2 in phase_groups[gi + 1:]:
            for p in g2:
                later |= set(PHASE_NEEDS[p])
        gtaps = [t for t in taps
                 if t in sum((PHASE_MAKES[p] for p in group), [])]
        P = Prog(group, wshapes, consts, later_needs=later, taps=gtaps)
        nc = P.build()
        in_maps = []
        for ci in range(8):
            m = {}
            for name in P.ext_in:
                if name == "xT":
                    m["xT"] = xTs[ci]
                elif name in w:
                    m["w_" + name] = w[name]
                else:
                    m[name] = hbm[ci][name]
            in_maps.append(m)
        res = run_bass_kernel_spmd(nc, in_maps, core_ids=list(range(8)))
        for ci in range(8):
            for name, arr in res.results[ci].items():
                if name.startswith("tap_"):
                    taps_res.setdefault(name[4:], [None] * 8)[ci] = arr
                elif name == "outT":
                    if out is None:
                        out = [None] * 8
                    out[ci] = arr
                else:
                    hbm[ci][name] = arr
    return out, taps_res, hbm


def kernel(**inputs):
    groups = [PHASES] if MODE == "fused" else [[p] for p in PHASES]
    out, _, _ = _run(inputs, groups)
    res = np.stack([o.T for o in out], axis=0).astype(np.float32)
    return np.ascontiguousarray(res)


# version 16
# speedup vs baseline: 1.0030x; 1.0030x over previous
import numpy as np
import ml_dtypes

import concourse.bass as bass
import concourse.mybir as mybir
from concourse.bass_utils import run_bass_kernel_spmd

F32 = mybir.dt.float32
BF16 = mybir.dt.bfloat16
AX = mybir.AxisListType.X
OP = mybir.AluOpType
AF = mybir.ActivationFunctionType
BF = ml_dtypes.bfloat16

T = 4096
C = 1024
H = 16
DH = 64
NK = C // 128
NTC = T // 512

MODE = "fused"
PHASES = ["ln0", "retp", "reth", "reto", "fap", "fah", "fao", "moe",
          "ln1", "a1p", "a1h", "a1o", "ff1"]

QUEUES = ("pe", "act", "dve", "pool", "sp")


class Buf:
    __slots__ = ("writers", "readers", "name")

    def __init__(self, name=""):
        self.writers = []
        self.readers = []
        self.name = name


NLANES = 8


class OpH:
    __slots__ = ("q", "fn", "deps", "inc", "sig", "val", "dma", "key", "bg")

    def __init__(self, q, fn, deps, dma):
        self.q = q
        self.fn = fn
        self.deps = deps
        self.inc = 16 if dma else 1
        self.dma = dma
        self.sig = False
        self.val = None
        self.key = None
        self.bg = False


class Sched:
    def __init__(self):
        self.ops = {q: [] for q in QUEUES}
        self.default_deps = []
        self.dma_count = {q: 0 for q in QUEUES}

    def op(self, q, fn, reads=(), writes=(), pwrites=(), deps=(), dma=False,
           dma_key=None):
        dl = list(self.default_deps)
        for b in reads:
            dl.extend(b.writers)
        for b in writes:
            dl.extend(b.writers)
            dl.extend(b.readers)
        for b in pwrites:
            dl.extend(b.readers)
        dl.extend(d for d in deps if d is not None)
        seen, deps2 = set(), []
        for d in dl:
            if id(d) not in seen:
                seen.add(id(d))
                deps2.append(d)
        o = OpH(q, fn, deps2, dma)
        if dma:
            kq = dma_key if dma_key is not None else q
            self.dma_count.setdefault(kq, 0)
            o.key = (kq, self.dma_count[kq] % NLANES)
            self.dma_count[kq] += 1
        else:
            o.key = q
        for d in deps2:
            if not (q == "pe" and d.key == "pe"):
                d.sig = True
        for b in reads:
            b.readers.append(o)
        for b in writes:
            b.writers = [o]
            b.readers = []
        for b in pwrites:
            b.writers.append(o)
        self.ops[q].append(o)
        return o

    def barrier(self):
        lasts = []
        for q in QUEUES:
            for o in reversed(self.ops[q]):
                if not o.bg:
                    lasts.append(o)
                    break
        for q in QUEUES:
            self.op(q, lambda e: e.nop(), deps=lasts)

    def finalize(self, sems):
        totals = {}
        for q in QUEUES:
            for o in self.ops[q]:
                if o.dma:
                    o.sig = True
                if o.sig:
                    v = totals.get(o.key, 0) + o.inc
                    o.val = v
                    totals[o.key] = v

        def mk(q):
            def run(eng):
                waited = {}

                def wait(key, val):
                    if val is None or waited.get(key, 0) >= val:
                        return
                    eng.wait_ge(sems[key], val)
                    waited[key] = val

                for o in self.ops[q]:
                    for d in o.deps:
                        if q == "pe" and d.key == "pe":
                            continue
                        wait(d.key, d.val)
                    if o.dma and o.val is not None and o.val > 16:
                        wait(o.key, o.val - 16)
                    inst = o.fn(eng)
                    if o.sig:
                        assert inst is not None, f"op on {q} needs inc but returned None"
                        inst.then_inc(sems[o.key], o.inc)
                for key, tv in sorted(totals.items(), key=str):
                    if key != q:
                        wait(key, tv)
            return run
        return {q: mk(q) for q in QUEUES}


class SBuf:
    __slots__ = ("t", "c0", "cols", "dt", "buf")

    def __init__(self, t, c0, cols, dt, name):
        self.t = t
        self.c0 = c0
        self.cols = cols
        self.dt = dt
        self.buf = Buf(name)

    def ap(self, rs=None, cs=None):
        r = rs if rs is not None else slice(0, 128)
        c = cs if cs is not None else slice(0, self.cols)
        if self.dt == BF16:
            return self.t[r, self.c0 + c.start: self.c0 + c.stop]
        return self.t[r, self.c0 + 2 * c.start: self.c0 + 2 * c.stop].bitcast(F32)


class Arena:

    def __init__(self, nc, name, cols_bf):
        self.t = nc.alloc_sbuf_tensor(name, [128, cols_bf], BF16)
        self.cols = cols_bf
        self.off = 0
        self.name = name

    def alloc(self, cols, dt=BF16, name=""):
        units = cols * (2 if dt == F32 else 1)
        c0 = (self.off + 15) & ~15
        assert c0 + units <= self.cols, \
            f"arena {self.name} overflow: {c0}+{units}>{self.cols} ({name})"
        self.off = c0 + units
        return SBuf(self.t, c0, cols, dt, name)

    def reset(self):
        self.off = 0


def make_consts():
    t = np.arange(T, dtype=np.float64)
    inv = 1.0 / 10000.0 ** (np.arange(0, DH, 2, dtype=np.float64) / DH)
    fr = t[:, None] * inv[None, :]
    emb = np.concatenate([fr, fr], axis=1)
    cosT = np.cos(emb).T.astype(np.float32)
    sinS = np.sin(emb).T.astype(np.float32)
    sinS[:32] *= -1.0
    cos8 = (cosT * 0.125).astype(np.float32)
    sin8 = (sinS * 0.125).astype(np.float32)

    gam = 1.0 - 2.0 ** (-5.0 - np.arange(H, dtype=np.float64))
    lng = np.log(gam)
    j = np.arange(512, dtype=np.float64)
    colf = np.exp(lng[:, None] * j[None, :])
    colf = np.repeat(colf[:, None, :], DH, axis=1).astype(np.float32)
    d0s = -384.0 + 128.0 * np.arange(35, dtype=np.float64)
    i = np.arange(128, dtype=np.float64)
    ex = d0s[None, :, None] - i[None, None, :]
    with np.errstate(over="ignore", under="ignore"):
        rf = np.exp(lng[:, None, None] * ex)
    rf = np.where(np.isfinite(rf), rf, 0.0).astype(np.float32)
    rf = np.ascontiguousarray(rf.transpose(2, 0, 1))
    jj = np.arange(512)
    ii = np.arange(128)
    masks = np.stack([(jj[None, :] - ii[:, None] + (-384 + 128 * k)) >= 0
                      for k in range(4)], axis=0).astype(BF)
    return {
        "cosT": cosT, "sinS": sinS, "cos8": cos8, "sin8": sin8,
        "colf": colf.reshape(H * DH, 512), "rf": rf,
        "masks": np.ascontiguousarray(masks.transpose(1, 0, 2).reshape(128, 4 * 512)),
        "ones_bf": np.ones((128, 1), dtype=BF),
        "ones_f": np.ones((1, 128), dtype=np.float32),
        "ones_row_bf": np.ones((1, 128), dtype=BF),
        "ident": np.eye(128, dtype=BF),
        "sel8": np.ascontiguousarray(np.kron(np.eye(8), np.ones((1, 128))).astype(BF)),
    }


def prep_weights(inp):
    w = {}
    WqT = np.ascontiguousarray(inp["L0_attn_Wq"].T)
    WkT = np.ascontiguousarray(inp["L0_attn_Wk"].T)
    w["ret_qk"] = np.ascontiguousarray(np.stack(
        [np.concatenate([WqT[:, h * 64:(h + 1) * 64], WkT[:, h * 64:(h + 1) * 64]], 1)
         for h in range(H)]))
    w["ret_v"] = np.ascontiguousarray(inp["L0_attn_Wv"].T)
    w["ret_wo"] = np.ascontiguousarray(inp["L0_attn_Wo"].T)
    w["gnw"] = np.ascontiguousarray(inp["L0_attn_gn_weight"].reshape(16, 64).T)
    w["gnb"] = np.ascontiguousarray(inp["L0_attn_gn_bias"].reshape(16, 64).T)

    qkv = inp["L0_ffn_qkv_w"]
    QT = np.ascontiguousarray(qkv[0:1024].T)
    KTm = np.ascontiguousarray(qkv[1024:2048].T)
    w["fa_qk"] = np.ascontiguousarray(np.stack(
        [np.concatenate([QT[:, h * 64:(h + 1) * 64], KTm[:, h * 64:(h + 1) * 64]], 1)
         for h in range(H)]))
    w["fa_v"] = np.ascontiguousarray(qkv[2048:3072].T)
    b = inp["L0_ffn_qkv_b"]
    w["fa_qkb"] = np.ascontiguousarray(np.stack(
        [np.concatenate([b[h * 64:(h + 1) * 64], b[1024 + h * 64:1024 + (h + 1) * 64]])
         for h in range(H)]).T)
    w["fa_vb"] = np.ascontiguousarray(b[2048:3072].reshape(1, 1024))
    w["fa_out"] = np.ascontiguousarray(inp["L0_ffn_out_w"].T)
    w["fa_outb"] = np.ascontiguousarray(inp["L0_ffn_out_b"].reshape(8, 128).T)
    w["norm1w"] = np.ascontiguousarray(inp["L0_ffn_norm1_w"].reshape(8, 128).T)
    w["norm2w"] = np.ascontiguousarray(inp["L0_ffn_norm2_w"].reshape(8, 128).T)
    w["gateT"] = np.ascontiguousarray(inp["L0_ffn_gate_w"].T)
    w["moe_w1"] = np.ascontiguousarray(inp["L0_ffn_w1"].transpose(0, 2, 1))
    w["moe_w2"] = np.ascontiguousarray(inp["L0_ffn_w2"].transpose(0, 2, 1))
    w["moe_w3"] = np.ascontiguousarray(inp["L0_ffn_w3"].transpose(0, 2, 1))
    w["moe_b1"] = np.ascontiguousarray(
        inp["L0_ffn_b1"].reshape(8, 16, 128).transpose(0, 2, 1))
    w["moe_b2"] = np.ascontiguousarray(
        inp["L0_ffn_b2"].reshape(8, 16, 128).transpose(0, 2, 1))
    w["moe_b3"] = np.ascontiguousarray(inp["L0_ffn_b3"])

    Wq1 = inp["L1_attn_Wqkv"]
    w["at1_qk"] = np.ascontiguousarray(Wq1[:, :, 0:128])
    w["at1_v"] = np.ascontiguousarray(
        Wq1[:, :, 128:192].transpose(1, 0, 2).reshape(1024, 1024))
    b1 = inp["L1_attn_bqkv"]
    w["at1_qkb"] = np.ascontiguousarray(b1[:, 0:128].T)
    w["at1_vb"] = np.ascontiguousarray(b1[:, 128:192].reshape(1, 1024))
    w["at1_out"] = np.ascontiguousarray(inp["L1_attn_Wout"])
    w["at1_outb"] = np.ascontiguousarray(inp["L1_attn_bout"].reshape(8, 128).T)

    w["ff_w1"] = np.ascontiguousarray(inp["L1_ffn_w1_w"].T)
    w["ff_w2"] = np.ascontiguousarray(inp["L1_ffn_w2_w"].T)
    w["ff_h1"] = np.ascontiguousarray(inp["L1_ffn_h1_w"].T)
    w["ff_h2"] = np.ascontiguousarray(inp["L1_ffn_h2_w"].T)
    w["ff_w3"] = np.ascontiguousarray(inp["L1_ffn_w3_w"].T)
    w["ff_s1"] = np.ascontiguousarray(inp["L1_ffn_w1_s"].reshape(32, 128).T)
    w["ff_s2"] = np.ascontiguousarray(inp["L1_ffn_w2_s"].reshape(32, 128).T)
    w["ff_hs1"] = np.ascontiguousarray(inp["L1_ffn_h1_s"].reshape(32, 128).T)
    w["ff_hs2"] = np.ascontiguousarray(inp["L1_ffn_h2_s"].reshape(32, 128).T)
    w["ff_s3"] = np.ascontiguousarray(inp["L1_ffn_w3_s"].reshape(8, 128).T)
    return w


CACHED = {"ret_qk", "ret_v", "ret_wo", "fa_qk", "fa_v", "fa_out", "gateT",
          "moe_w1", "moe_w2", "moe_w3", "moe_b3", "at1_qk", "at1_v", "at1_out",
          "ff_w1", "ff_w2", "ff_h1", "ff_h2", "ff_w3"}

SCRATCH = {
    "xnT": ((C, T), F32, "ln0"),
    "xnTb": ((C, T), BF16, "ln0"),
    "partialT": ((C, T), F32, "reto"),
    "x1T": ((C, T), F32, "fao"),
    "y0T": ((C, T), F32, "moe"),
    "ynT": ((C, T), F32, "ln1"),
    "ynTb": ((C, T), BF16, "ln1"),
    "p1T": ((C, T), F32, "a1o"),
    "r1st": ((1, T), F32, "ln0"),
    "qkf_ret": ((2048, T), F32, "retp"),
    "vt_ret": ((T, C), BF16, "retp"),
    "hd_ret": ((C, T), BF16, "reth"),
    "qkf_fa": ((2048, T), F32, "fap"),
    "vt_fa": ((T, C), BF16, "fap"),
    "hd_fa": ((C, T), BF16, "fah"),
    "qkf_at1": ((2048, T), F32, "a1p"),
    "vt_at1": ((T, C), BF16, "a1p"),
    "hd_at1": ((C, T), BF16, "a1h"),
}

PHASE_NEEDS = {
    "ln0": ["xT"],
    "retp": ["xnTb"], "reth": ["qkf_ret", "vt_ret"], "reto": ["hd_ret", "xT"],
    "fap": ["xnT", "r1st"], "fah": ["qkf_fa", "vt_fa"], "fao": ["hd_fa", "xnT"],
    "moe": ["x1T", "partialT"],
    "ln1": ["y0T"],
    "a1p": ["ynTb"], "a1h": ["qkf_at1", "vt_at1"], "a1o": ["hd_at1", "y0T"],
    "ff1": ["ynTb", "p1T"],
}
PHASE_MAKES = {
    "ln0": ["xnT", "xnTb", "r1st"],
    "retp": ["qkf_ret", "vt_ret"], "reth": ["hd_ret"], "reto": ["partialT"],
    "fap": ["qkf_fa", "vt_fa"], "fah": ["hd_fa"], "fao": ["x1T"],
    "moe": ["y0T"],
    "ln1": ["ynT", "ynTb"],
    "a1p": ["qkf_at1", "vt_at1"], "a1h": ["hd_at1"], "a1o": ["p1T"],
    "ff1": ["outT"],
}
PHASE_WEIGHTS = {
    "ln0": [], "ln1": [],
    "retp": ["ret_qk", "ret_v"], "reth": ["gnw", "gnb"],
    "reto": ["ret_wo"],
    "fap": ["fa_qk", "fa_v", "fa_qkb", "fa_vb", "norm1w"], "fah": [],
    "fao": ["fa_out", "fa_outb"],
    "moe": ["gateT", "moe_w1", "moe_w2", "moe_w3", "moe_b1", "moe_b2",
            "moe_b3", "norm2w"],
    "a1p": ["at1_qk", "at1_v", "at1_qkb", "at1_vb"], "a1h": [],
    "a1o": ["at1_out", "at1_outb"],
    "ff1": ["ff_w1", "ff_w2", "ff_h1", "ff_h2", "ff_w3",
            "ff_s1", "ff_s2", "ff_hs1", "ff_hs2", "ff_s3"],
}

ACFG = {
    "ret": dict(qk="ret_qk", v="ret_v", qkb=None, vb=None, xsrc="xnTb",
                rms=None, rope=True, kind="decay", causal=True,
                qkf="qkf_ret", vt="vt_ret", hd="hd_ret",
                ow="ret_wo", ob=None, res="xT", dst="partialT"),
    "fa": dict(qk="fa_qk", v="fa_v", qkb="fa_qkb", vb="fa_vb", xsrc=None,
               rms=("xnT", "norm1w", 1e-6), rope=True, kind="softmax",
               causal=False, qkf="qkf_fa", vt="vt_fa", hd="hd_fa",
               ow="fa_out", ob="fa_outb", res="xnT", dst="x1T",
               out_stats=True),
    "at1": dict(qk="at1_qk", v="at1_v", qkb="at1_qkb", vb="at1_vb",
                xsrc="ynTb", rms=None, rope=False, kind="softmax",
                causal=True, qkf="qkf_at1", vt="vt_at1", hd="hd_at1",
                ow="at1_out", ob="at1_outb", res="y0T", dst="p1T"),
}


class Prog:
    def __init__(self, phases, wshapes, consts, later_needs=(), taps=()):
        self.nc = bass.Bass("TRN2", target_bir_lowering=False, debug=False)
        self.s = Sched()
        self.phases = phases
        nc = self.nc
        self.ext_in = []
        self.dram = {}
        self.dbuf = {}
        self.cache_done = {}

        needs, makes = set(), set()
        for ph in phases:
            needs |= set(PHASE_NEEDS[ph])
            makes |= set(PHASE_MAKES[ph])

        if "xT" in needs:
            self.dram["xT"] = nc.dram_tensor("xT", [C, T], F32, kind="ExternalInput")
            self.dbuf["xT"] = Buf("xT")
            self.ext_in.append("xT")
        if "outT" in makes:
            self.dram["outT"] = nc.dram_tensor("outT", [C, T], F32, kind="ExternalOutput")
            self.dbuf["outT"] = Buf("outT")
        for name, (shape, dt, prod) in SCRATCH.items():
            if name not in (needs | makes):
                continue
            if name in makes and name in later_needs:
                kind = "ExternalOutput"
            elif name in needs and name not in makes:
                kind = "ExternalInput"
            else:
                kind = "Internal"
            self.dram[name] = nc.dram_tensor(name, list(shape), dt, kind=kind)
            self.dbuf[name] = Buf(name)
            if kind == "ExternalInput":
                self.ext_in.append(name)

        self.wext = {}
        self.wcache = {}
        for ph in phases:
            for wn in PHASE_WEIGHTS[ph]:
                if wn in self.wext:
                    continue
                self.wext[wn] = nc.dram_tensor("w_" + wn, list(wshapes[wn]), F32,
                                               kind="ExternalInput")
                self.dbuf["w_" + wn] = Buf("w_" + wn)
                self.ext_in.append(wn)
                if wn in CACHED:
                    self.wcache[wn] = nc.dram_tensor("bc_" + wn, list(wshapes[wn]),
                                                     BF16, kind="Internal")
                    self.dbuf["bc_" + wn] = Buf("bc_" + wn)

        self.ct = {k: nc.inline_tensor(np.ascontiguousarray(v), name="c_" + k)
                   for k, v in consts.items()}
        self.cbuf = {k: Buf("c_" + k) for k in consts}

        self.ar = Arena(nc, "mega", 100608)
        self.psum = nc.alloc_psum_tensor("ps", [128, 4096], F32)
        self.sems = {q: nc.alloc_semaphore(name="sem_" + q) for q in QUEUES}
        for q in list(QUEUES) + ["bgp"]:
            for ln in range(NLANES):
                self.sems[(q, ln)] = nc.alloc_semaphore(name=f"sem_{q}_d{ln}")

        cvals = [0.0, 1.0, 1e-5, 1e-6, 0.125, -1.0]
        cv_np = np.tile(np.array([cvals], dtype=np.float32), (128, 1))
        cv_t = nc.inline_tensor(cv_np, name="c_scal")
        cv_sb = nc.alloc_sbuf_tensor("cscal", [128, 16], F32)
        boot = self.s.op("sp", lambda e: e.dma_start(cv_sb[0:128, 0:len(cvals)], cv_t.ap()),
                         dma=True)
        for i, v in enumerate(cvals):
            nc.const_aps.register(F32, float(v), cv_sb[0:128, i:i + 1])
        self.s.default_deps = [boot]

        self.taps = list(taps)
        self.tap_out = {}
        for tn in self.taps:
            if tn in SCRATCH:
                shape, dt, _ = SCRATCH[tn]
            else:
                shape, dt = (C, T), F32
            self.tap_out[tn] = nc.dram_tensor("tap_" + tn, list(shape), dt,
                                              kind="ExternalOutput")

    def ab(self, cols, name=""):
        return self.ar.alloc(cols, BF16, name)

    def af(self, cols, name=""):
        return self.ar.alloc(cols, F32, name)

    def cache(self, wn):
        if wn not in self.cache_done:
            src, dst = self.wext[wn], self.wcache[wn]
            o = self.s.op(
                "pool", lambda e: e.dma_start(dst.ap(), src.ap()),
                reads=[self.dbuf["w_" + wn]], writes=[self.dbuf["bc_" + wn]],
                dma=True, dma_key="bgp")
            o.bg = True
            self.cache_done[wn] = o
        return self.wcache[wn], self.dbuf["bc_" + wn]

    def pb(self, bank, nbanks=1, rows=slice(0, 128)):
        return self.psum[rows, bank * 512:(bank + nbanks) * 512]

    def tap(self, name):
        if name not in self.tap_out:
            return
        src, dst = self.dram[name], self.tap_out[name]
        self.s.op("sp", lambda e: e.dma_start(dst.ap(), src.ap()),
                  reads=[self.dbuf[name]], writes=[Buf()], dma=True)

    def build(self):
        for i, ph in enumerate(self.phases):
            self.pbk = [Buf(f"bank{i}") for i in range(8)]
            for ph2 in self.phases[i + 1:i + 3]:
                for wn in PHASE_WEIGHTS[ph2]:
                    if wn in self.wcache:
                        self.cache(wn)
            PHASE_FNS[ph](self)
            for mk in PHASE_MAKES[ph]:
                if mk in self.taps:
                    self.tap(mk)
            self.s.barrier()
            self.ar.reset()
        emitters = self.s.finalize(self.sems)
        with self.nc.Block("main") as blk:
            blk.tensor(emitters["pe"])
            blk.scalar(emitters["act"])
            blk.vector(emitters["dve"])
            blk.gpsimd(emitters["pool"])
            blk.sync(emitters["sp"])
        return self.nc


def load_weight_f32(P, wn, sbuf, src_slice=None, rows=128, cols=None):
    src = P.wext[wn]
    ap = src.ap() if src_slice is None else src_slice(src)
    cols = cols if cols is not None else ap.shape[-1]
    return P.s.op("sp", lambda e: e.dma_start(
        sbuf.ap(slice(0, rows), slice(0, cols)), ap),
        reads=[P.dbuf["w_" + wn]], writes=[sbuf.buf], dma=True)


def load_const(P, cname, sbuf, rows, cols):
    src = P.ct[cname]
    return P.s.op("sp", lambda e: e.dma_start(
        sbuf.ap(slice(0, rows), slice(0, cols)), src.ap()),
        reads=[P.cbuf[cname]], writes=[sbuf.buf], dma=True)


def ph_ln0(P):
    _ln_common(P, "xT", "xnT", "xnTb", sumsq_to="r1st")


def ph_ln1(P):
    _ln_common(P, "y0T", "ynT", "ynTb")


def _ln_common(P, srcn, dstfn, dstbn, sumsq_to=None):
    s = P.s
    src, srcb = P.dram[srcn], P.dbuf[srcn]
    dstf, dstfb = P.dram[dstfn], P.dbuf[dstfn]
    dstb, dstbb = P.dram[dstbn], P.dbuf[dstbn]

    ones_b = P.ab(16, "ln_ones")
    load_const(P, "ones_bf", ones_b, 128, 1)
    onesf = P.af(144, "ln_onesf")
    for r0 in (0, 32, 64):
        P.s.op("sp", lambda e, r0=r0: e.dma_start(
            onesf.ap(slice(r0, r0 + 1), slice(0, 128)), P.ct["ones_f"].ap()),
            reads=[P.cbuf["ones_f"]], pwrites=[onesf.buf], dma=True)

    xin = [P.af(4096, f"ln_x{i}") for i in range(2)]
    xbf = [P.ab(4096, f"ln_xbf{i}") for i in range(2)]
    xsq = [P.ab(4096, f"ln_xsq{i}") for i in range(2)]
    sum_ps, sq_ps = Buf("sum_ps"), Buf("sq_ps")
    for c in range(NK):
        xb, bb, qb = xin[c % 2], xbf[c % 2], xsq[c % 2]
        s.op("sp", lambda e, c=c, xb=xb: e.dma_start(
            xb.ap(), src[c * 128:(c + 1) * 128, :]),
            reads=[srcb], writes=[xb.buf], dma=True)
        s.op("act", lambda e, xb=xb, bb=bb: e.activation(bb.ap(), xb.ap(), AF.Copy),
             reads=[xb.buf], writes=[bb.buf])
        s.op("act", lambda e, xb=xb, qb=qb: e.activation(qb.ap(), xb.ap(), AF.Square),
             reads=[xb.buf], writes=[qb.buf])

        def mk_mm(row, tile, c=c):
            def f(e):
                last = None
                for tc in range(NTC):
                    last = e.matmul(P.psum[row:row + 1, tc * 512:(tc + 1) * 512],
                                    ones_b.ap(slice(0, 128), slice(0, 1)),
                                    tile.ap(slice(0, 128), slice(tc * 512, (tc + 1) * 512)),
                                    start=(c == 0), stop=(c == NK - 1))
                return last
            return f
        s.op("pe", mk_mm(0, bb), reads=[bb.buf, ones_b.buf],
             writes=[sum_ps] if c == 0 else [], pwrites=[sum_ps] if c > 0 else [])
        s.op("pe", mk_mm(32, qb), reads=[qb.buf, ones_b.buf],
             writes=[sq_ps] if c == 0 else [], pwrites=[sq_ps] if c > 0 else [])

    rows = P.af(4096, "ln_rows")
    rM, rR, rTm = slice(0, 1), slice(32, 33), slice(64, 65)
    s.op("act", lambda e: e.activation(rows.ap(rM), P.psum[0:1, :], AF.Copy, scale=1.0 / C),
         reads=[sum_ps], writes=[rows.buf])
    s.op("act", lambda e: e.activation(rows.ap(rTm), rows.ap(rM), AF.Square),
         reads=[rows.buf], pwrites=[rows.buf])
    s.op("dve", lambda e: e.scalar_tensor_tensor(rows.ap(rR), P.psum[32:33, :], 1.0 / C,
                                                 rows.ap(rTm), OP.mult, OP.subtract),
         reads=[sq_ps, rows.buf], pwrites=[rows.buf])
    s.op("act", lambda e: e.activation(rows.ap(rTm), rows.ap(rR), AF.Ln, bias=1e-5),
         reads=[rows.buf], pwrites=[rows.buf])
    s.op("act", lambda e: e.activation(rows.ap(rR), rows.ap(rTm), AF.Exp, scale=-0.5),
         reads=[rows.buf], pwrites=[rows.buf])

    xt2 = [P.af(512, f"ln_t{i}") for i in range(4)]
    xnf = [P.af(512, f"ln_v{i}") for i in range(4)]
    xnb = [P.ab(512, f"ln_b{i}") for i in range(4)]
    meanB, rstdB = Buf("meanB"), Buf("rstdB")
    if sumsq_to is not None:
        sqb = [P.ab(512, f"ln_s{i}") for i in range(4)]
        r1row = P.af(4096, "ln_r1row")
        ssD, ssB = P.dram[sumsq_to], P.dbuf[sumsq_to]
    for tc in range(NTC):
        cs = slice(tc * 512, (tc + 1) * 512)
        s.op("pe", lambda e, cs=cs: e.matmul(
            P.pb(6), onesf.ap(slice(0, 1), slice(0, 128)), rows.ap(rM, cs),
            start=True, stop=True),
            reads=[rows.buf, onesf.buf], writes=[meanB])
        s.op("pe", lambda e, cs=cs: e.matmul(
            P.pb(7), onesf.ap(slice(32, 33), slice(0, 128)), rows.ap(rR, cs),
            start=True, stop=True),
            reads=[rows.buf, onesf.buf], writes=[rstdB])
        for c in range(NK):
            i = c % 4
            xb, xo, xB = xt2[i], xnf[i], xnb[i]
            s.op("sp", lambda e, c=c, cs=cs, xb=xb: e.dma_start(
                xb.ap(), src[c * 128:(c + 1) * 128, cs]),
                reads=[srcb], writes=[xb.buf], dma=True)
            s.op("dve", lambda e, xb=xb, xo=xo: e.tensor_sub(xo.ap(), xb.ap(), P.pb(6)),
                 reads=[xb.buf, meanB], writes=[xo.buf])
            s.op("dve", lambda e, xo=xo: e.tensor_mul(xo.ap(), xo.ap(), P.pb(7)),
                 reads=[rstdB, xo.buf], pwrites=[xo.buf])
            s.op("act", lambda e, xo=xo, xB=xB: e.activation(xB.ap(), xo.ap(), AF.Copy),
                 reads=[xo.buf], writes=[xB.buf])
            s.op("sp", lambda e, c=c, cs=cs, xo=xo: e.dma_start(
                dstf[c * 128:(c + 1) * 128, cs], xo.ap()),
                reads=[xo.buf], pwrites=[dstfb], dma=True)
            s.op("sp", lambda e, c=c, cs=cs, xB=xB: e.dma_start(
                dstb[c * 128:(c + 1) * 128, cs], xB.ap()),
                reads=[xB.buf], pwrites=[dstbb], dma=True)
            if sumsq_to is not None:
                sb2 = sqb[i]
                s.op("act", lambda e, xo=xo, sb2=sb2: e.activation(
                    sb2.ap(), xo.ap(), AF.Square),
                    reads=[xo.buf], writes=[sb2.buf])
                s.op("pe", lambda e, c=c, sb2=sb2: e.matmul(
                    P.psum[0:1, 2560:3072], ones_b.ap(slice(0, 128), slice(0, 1)),
                    sb2.ap(), start=(c == 0), stop=(c == NK - 1)),
                    reads=[sb2.buf, ones_b.buf],
                    writes=[P.pbk[5]] if c == 0 else [],
                    pwrites=[P.pbk[5]] if c else [])
        if sumsq_to is not None:
            s.op("act", lambda e, cs=cs: e.activation(
                r1row.ap(slice(0, 1), cs), P.psum[0:1, 2560:3072], AF.Copy),
                reads=[P.pbk[5]],
                writes=[r1row.buf] if tc == 0 else [],
                pwrites=[r1row.buf] if tc else [])
    if sumsq_to is not None:
        s.op("sp", lambda e: e.dma_start(ssD.ap(), r1row.ap(slice(0, 1))),
             reads=[r1row.buf], writes=[ssB], dma=True)


def _rms_apply(P, rms, xsb):
    s = P.s
    srcn, wn, eps = rms
    src, srcb = P.dram[srcn], P.dbuf[srcn]
    ssD, ssB = P.dram["r1st"], P.dbuf["r1st"]
    nw = P.af(16, "rms_w")
    load_weight_f32(P, wn, nw, cols=8)
    onesf = P.af(144, "rms_onesf")
    P.s.op("sp", lambda e: e.dma_start(onesf.ap(slice(0, 1), slice(0, 128)),
                                       P.ct["ones_f"].ap()),
           reads=[P.cbuf["ones_f"]], writes=[onesf.buf], dma=True)
    xin = [P.af(512, f"rms_x{i}") for i in range(4)]
    rows = P.af(4096, "rms_rows")
    srow = P.af(4096, "rms_srow")
    r1 = slice(0, 1)
    s.op("sp", lambda e: e.dma_start(srow.ap(r1), ssD.ap()),
         reads=[ssB], writes=[srow.buf], dma=True)
    s.op("act", lambda e: e.activation(srow.ap(r1), srow.ap(r1),
                                       AF.Ln, scale=1.0 / C, bias=eps),
         reads=[srow.buf], pwrites=[srow.buf])
    s.op("act", lambda e: e.activation(rows.ap(r1), srow.ap(r1), AF.Exp, scale=-0.5),
         reads=[srow.buf], writes=[rows.buf])
    for tc in range(NTC):
        cs = slice(tc * 512, (tc + 1) * 512)
        s.op("pe", lambda e, cs=cs: e.matmul(
            P.pb(6), onesf.ap(slice(0, 1), slice(0, 128)), rows.ap(r1, cs),
            start=True, stop=True),
            reads=[rows.buf, onesf.buf], writes=[P.pbk[6]])
        for c in range(NK):
            i = c % 4
            xb, qb = xin[i], xin[(i + 2) % 4]
            s.op("sp", lambda e, c=c, cs=cs, xb=xb: e.dma_start(
                xb.ap(), src[c * 128:(c + 1) * 128, cs]),
                reads=[srcb], writes=[xb.buf], dma=True)
            s.op("dve", lambda e, xb=xb, qb=qb: e.tensor_mul(
                qb.ap(), xb.ap(), P.pb(6)),
                reads=[xb.buf, P.pbk[6]], writes=[qb.buf])
            s.op("act", lambda e, c=c, cs=cs, qb=qb: e.activation(
                xsb[c].ap(None, cs), qb.ap(), AF.Copy,
                scale=nw.ap(None, slice(c, c + 1))),
                reads=[qb.buf, nw.buf],
                writes=[xsb[c].buf] if tc == 0 else [],
                pwrites=[] if tc == 0 else [xsb[c].buf])


def attn_prep(P, cfg):
    s = P.s
    xsb = [P.ab(4096, f"px{c}") for c in range(8)]
    if cfg["rms"] is not None:
        _rms_apply(P, cfg["rms"], xsb)
    else:
        src, sbf = P.dram[cfg["xsrc"]], P.dbuf[cfg["xsrc"]]
        for c in range(8):
            s.op("sp", lambda e, c=c, t=xsb[c]: e.dma_start(
                t.ap(), src[c * 128:(c + 1) * 128, :]),
                reads=[sbf], writes=[xsb[c].buf], dma=True)

    vtd, vtb = P.dram[cfg["vt"]], P.dbuf[cfg["vt"]]
    vC, vCb = P.cache(cfg["v"])
    vw = [P.ab(1024, f"pv{c}") for c in range(8)]
    for c in range(8):
        vwc = vw[c]
        s.op("sp", lambda e, c=c, t=vwc: e.dma_start(
            t.ap(), vC[c * 128:(c + 1) * 128, :]),
            reads=[vCb], writes=[vwc.buf], dma=True)
    onesr = P.ab(144, "p_onesr")
    load_const(P, "ones_row_bf", onesr, 1, 128)
    vbrow = None
    if cfg["vb"] is not None:
        vbrow = P.ab(1024, "p_vbrow")
        vbsrc = P.wext[cfg["vb"]]
        s.op("pool", lambda e: e.dma_start(
            vbrow.ap(slice(0, 1), slice(0, 1024)), vbsrc.ap()),
            reads=[P.dbuf["w_" + cfg["vb"]]], writes=[vbrow.buf], dma=True)
    vob = [P.ab(1024, f"pvo{i}") for i in range(2)]
    for tt in range(32):
        ob = vob[tt % 2]
        banks = ((tt % 2) * 2, (tt % 2) * 2 + 1)

        def mkv(tt=tt, banks=banks):
            def f(e):
                last = None
                for hf in range(2):
                    for c in range(8):
                        last = e.matmul(
                            P.pb(banks[hf]),
                            xsb[c].ap(None, slice(tt * 128, (tt + 1) * 128)),
                            vw[c].ap(None, slice(hf * 512, (hf + 1) * 512)),
                            start=(c == 0), stop=(c == 7 and vbrow is None))
                    if vbrow is not None:
                        last = e.matmul(
                            P.pb(banks[hf]), onesr.ap(slice(0, 1), slice(0, 128)),
                            vbrow.ap(slice(0, 1), slice(hf * 512, (hf + 1) * 512)),
                            start=False, stop=True)
                return last
            return f
        rd = [t.buf for t in xsb] + [t.buf for t in vw] + [onesr.buf]
        if vbrow is not None:
            rd.append(vbrow.buf)
        s.op("pe", mkv(), reads=rd, writes=[P.pbk[banks[0]], P.pbk[banks[1]]])
        for hf in range(2):
            s.op("act", lambda e, b=banks[hf], hf=hf, ob=ob: e.activation(
                ob.ap(None, slice(hf * 512, (hf + 1) * 512)), P.pb(b), AF.Copy),
                reads=[P.pbk[banks[hf]]],
                writes=[ob.buf] if hf == 0 else [], pwrites=[ob.buf] if hf else [])
        s.op("sp", lambda e, tt=tt, ob=ob: e.dma_start(
            vtd[tt * 128:(tt + 1) * 128, :], ob.ap()),
            reads=[ob.buf], pwrites=[vtb], dma=True)

    qkC, qkCb = P.cache(cfg["qk"])
    qkbv = None
    if cfg["qkb"] is not None:
        qkbv = P.af(16, "p_qkb")
        load_weight_f32(P, cfg["qkb"], qkbv, cols=16)
    wq = [P.ab(1024, f"pwq{i}") for i in range(2)]
    qf = [P.af(4096, f"pqf{i}") for i in range(2)]
    qkfd, qkfb = P.dram[cfg["qkf"]], P.dbuf[cfg["qkf"]]
    for h in range(H):
        wqs, qfs = wq[h % 2], qf[h % 2]
        s.op("sp", lambda e, h=h, t=wqs: e.dma_start(
            t.ap().rearrange("p (k o) -> p k o", o=128),
            qkC[h].rearrange("(k p) o -> p k o", p=128)),
            reads=[qkCb], writes=[wqs.buf], dma=True)
        for tc in range(NTC):
            bank = 4 + (tc % 4)

            def mkp(wqs=wqs, tc=tc, bank=bank):
                def f(e):
                    last = None
                    for c in range(8):
                        last = e.matmul(
                            P.pb(bank), wqs.ap(None, slice(c * 128, (c + 1) * 128)),
                            xsb[c].ap(None, slice(tc * 512, (tc + 1) * 512)),
                            start=(c == 0), stop=(c == 7))
                    return last
                return f
            s.op("pe", mkp(), reads=[wqs.buf] + [t.buf for t in xsb],
                 writes=[P.pbk[bank]])
            if qkbv is not None:
                s.op("dve", lambda e, tc=tc, bank=bank, qfs=qfs, h=h:
                     e.tensor_scalar_add(
                         qfs.ap(None, slice(tc * 512, (tc + 1) * 512)), P.pb(bank),
                         qkbv.ap(None, slice(h, h + 1))),
                     reads=[P.pbk[bank], qkbv.buf],
                     writes=[qfs.buf] if tc == 0 else [],
                     pwrites=[qfs.buf] if tc else [])
            else:
                s.op("act", lambda e, tc=tc, bank=bank, qfs=qfs: e.activation(
                    qfs.ap(None, slice(tc * 512, (tc + 1) * 512)), P.pb(bank), AF.Copy),
                    reads=[P.pbk[bank]],
                    writes=[qfs.buf] if tc == 0 else [],
                    pwrites=[qfs.buf] if tc else [])
        s.op("sp", lambda e, h=h, qfs=qfs: e.dma_start(
            qkfd[h * 128:(h + 1) * 128, :], qfs.ap()),
            reads=[qfs.buf], pwrites=[qkfb], dma=True)


RETH_STAGE = 99
RETH_GN6 = 3


def attn_heads(P, cfg):
    s = P.s
    decay = cfg["kind"] == "decay"
    soft = not decay
    causal = cfg["causal"]
    rope = cfg["rope"]

    onesf = P.af(144, "h_onesf")
    for r0 in (0, 64):
        s.op("sp", lambda e, r0=r0: e.dma_start(
            onesf.ap(slice(r0, r0 + 1), slice(0, 128)), P.ct["ones_f"].ap()),
            reads=[P.cbuf["ones_f"]], pwrites=[onesf.buf], dma=True)
    ones_b = P.ab(16, "h_onesb")
    load_const(P, "ones_bf", ones_b, 128, 1)
    if rope:
        tabC = P.af(4096, "h_tabC")
        tabS = P.af(4096, "h_tabS")
        for tab, top, bot in ((tabC, "cos8", "cosT"), (tabS, "sin8", "sinS")):
            s.op("sp", lambda e, tab=tab, top=top: e.dma_start(
                tab.ap(slice(0, 64), slice(0, T)), P.ct[top].ap()),
                reads=[P.cbuf[top]], pwrites=[tab.buf], dma=True)
            s.op("sp", lambda e, tab=tab, bot=bot: e.dma_start(
                tab.ap(slice(64, 128), slice(0, T)), P.ct[bot].ap()),
                reads=[P.cbuf[bot]], pwrites=[tab.buf], dma=True)
    if decay:
        rf = P.af(560, "h_rf")
        load_const(P, "rf", rf, 128, 560)
        colf = [P.af(512, f"h_cf{i}") for i in range(2)]
        gv = P.af(16, "h_gnw")
        gb = P.af(16, "h_gnb")
        load_weight_f32(P, "gnw", gv, rows=64, cols=16)
        load_weight_f32(P, "gnb", gb, rows=64, cols=16)
    if causal:
        msk = P.ab(2048, "h_masks")
        load_const(P, "masks", msk, 128, 2048)

    QKF = [P.af(4096, f"h_qkf{i}") for i in range(2)]
    SW = [P.af(4096, f"h_sw{i}") for i in range(2)]
    SWb = []
    for i in range(2):
        b = SBuf(P.ar.t, SW[i].c0, 4096, BF16, f"h_swb{i}")
        b.buf = SW[i].buf
        SWb.append(b)
    QA = [P.ab(4096, f"h_qa{i}") for i in range(2)]
    KA = [P.ab(4096, f"h_ka{i}") for i in range(2)]
    npt = 4
    pt = [P.ab(1024 if soft else 512, f"h_pt{i}") for i in range(npt)]
    vst = 65 if soft else 64
    Vh = [P.ab(32 * vst, f"h_vh{i}") for i in range(2)]
    rows = P.af(4096, "h_rows")
    att = P.ab(4096, "h_att")
    rob = [P.ab(512, f"h_rob{i}") for i in range(2)]
    rosq = [P.ab(512, f"h_rosq{i}") for i in range(2)]

    hdD, hdB = P.dram[cfg["hd"]], P.dbuf[cfg["hd"]]
    qkfD, qkfB = P.dram[cfg["qkf"]], P.dbuf[cfg["qkf"]]
    vtD, vtB = P.dram[cfg["vt"]], P.dbuf[cfg["vt"]]

    if soft:
        for i in range(2):
            s.op("dve", lambda e, i=i: e.memset(KA[i].ap(slice(64, 65)), 1.0),
                 pwrites=[KA[i].buf])
            s.op("dve", lambda e, i=i: e.memset(
                P.ar.t[0:128, Vh[i].c0:Vh[i].c0 + 32 * 65].rearrange(
                    "p (kt d) -> p kt d", d=65)[:, :, 64:65], 1.0),
                 pwrites=[Vh[i].buf])

    def prep_early(h):
        p2 = h % 2
        qkf, qa, ka, vh, sw, swb = QKF[p2], QA[p2], KA[p2], Vh[p2], SW[p2], SWb[p2]
        s.op("sp", lambda e, h=h, qkf=qkf: e.dma_start(
            qkf.ap(), qkfD[h * 128:(h + 1) * 128, :]),
            reads=[qkfB], writes=[qkf.buf], dma=True)
        vdst = P.ar.t[0:128, vh.c0:vh.c0 + 32 * vst].rearrange(
            "p (kt d) -> p kt d", d=vst)[:, :, 0:64]
        for part in range(4):
            ks = slice(part * 8, (part + 1) * 8)
            s.op("sp", lambda e, h=h, ks=ks, part=part, vdst=vdst, vh=vh: e.dma_start(
                vdst[:, ks, :],
                vtD[part * 1024:(part + 1) * 1024, h * 64:(h + 1) * 64]
                .rearrange("(kt p) d -> p kt d", p=128)),
                reads=[vtB], pwrites=[vh.buf], dma=True)
        if decay:
            cf = colf[p2]
            s.op("sp", lambda e, h=h, cf=cf: e.dma_start(
                cf.ap(slice(0, 64), slice(0, 512)),
                P.ct["colf"][h * 64:(h + 1) * 64, :]),
                reads=[P.cbuf["colf"]], writes=[cf.buf], dma=True)
        if rope:
            for (d0, s0) in ((0, 32), (32, 0), (64, 96), (96, 64)):
                s.op("sp", lambda e, d0=d0, s0=s0, qkf=qkf, sw=sw: e.dma_start(
                    sw.ap(slice(d0, d0 + 32)), qkf.ap(slice(s0, s0 + 32))),
                    reads=[qkf.buf], pwrites=[sw.buf], dma=True)
            s.op("pool", lambda e, qkf=qkf: e.tensor_mul(qkf.ap(), qkf.ap(), tabC.ap()),
                 reads=[qkf.buf, tabC.buf], pwrites=[qkf.buf])
            s.op("pool", lambda e, sw=sw: e.tensor_mul(sw.ap(), sw.ap(), tabS.ap()),
                 reads=[sw.buf, tabS.buf], pwrites=[sw.buf])
            s.op("pool", lambda e, qkf=qkf, sw=sw: e.tensor_add(
                qkf.ap(), qkf.ap(), sw.ap()),
                reads=[qkf.buf, sw.buf], pwrites=[qkf.buf])
        if decay:
            cf = colf[p2]
            for tc in range(NTC):
                s.op("pool", lambda e, tc=tc, qa=qa, qkf=qkf, cf=cf: e.tensor_mul(
                    qa.ap(slice(0, 64), slice(tc * 512, (tc + 1) * 512)),
                    qkf.ap(slice(0, 64), slice(tc * 512, (tc + 1) * 512)),
                    cf.ap(slice(0, 64), slice(0, 512))),
                    reads=[qkf.buf, cf.buf],
                    writes=[qa.buf] if tc == 0 else [], pwrites=[qa.buf] if tc else [])
        elif rope:
            s.op("pool", lambda e, qa=qa, qkf=qkf: e.tensor_copy(
                qa.ap(slice(0, 64)), qkf.ap(slice(0, 64))),
                reads=[qkf.buf], writes=[qa.buf])
        else:
            s.op("pool", lambda e, qa=qa, qkf=qkf: e.tensor_scalar_mul(
                qa.ap(slice(0, 64)), qkf.ap(slice(0, 64)), 0.125),
                reads=[qkf.buf], writes=[qa.buf])
        s.op("pool", lambda e, qkf=qkf, swb=swb: e.tensor_copy(
            swb.ap(slice(64, 128)), qkf.ap(slice(64, 128))),
            reads=[qkf.buf], pwrites=[swb.buf])
        s.op("sp", lambda e, ka=ka, swb=swb: e.dma_start(
            ka.ap(slice(0, 64)), swb.ap(slice(64, 128))),
            reads=[swb.buf], pwrites=[ka.buf], dma=True)

    def prep_late(h):
        if not soft:
            return
        p2 = h % 2
        qa, ka, swb = QA[p2], KA[p2], SWb[p2]
        po = p2 * 2048
        r64 = slice(64, 65)
        s.op("act", lambda e, ka=ka, swb=swb: e.activation(
            swb.ap(slice(0, 64)), ka.ap(slice(0, 64)), AF.Square),
            reads=[ka.buf], pwrites=[swb.buf])
        for tc in range(NTC):
            s.op("pe", lambda e, tc=tc, swb=swb: e.matmul(
                P.psum[64:65, 3584:4096], ones_b.ap(slice(0, 64), slice(0, 1)),
                swb.ap(slice(0, 64), slice(tc * 512, (tc + 1) * 512)),
                start=True, stop=True, tile_position=(0, 64)),
                reads=[swb.buf, ones_b.buf], writes=[P.pbk[7]])
            s.op("dve", lambda e, tc=tc: e.reduce_max(
                rows.ap(r64, slice(po + tc, po + tc + 1)),
                P.psum[64:65, 3584:4096], AX),
                reads=[P.pbk[7]], pwrites=[rows.buf])
        s.op("dve", lambda e: e.reduce_max(
            rows.ap(r64, slice(po + 8, po + 9)), rows.ap(r64, slice(po, po + 8)), AX),
            reads=[rows.buf], pwrites=[rows.buf])
        s.op("act", lambda e: e.activation(
            rows.ap(r64, slice(po + 9, po + 10)), rows.ap(r64, slice(po + 8, po + 9)),
            AF.Ln),
            reads=[rows.buf], pwrites=[rows.buf])
        s.op("act", lambda e, qa=qa, swb=swb: e.activation(
            swb.ap(slice(0, 64)), qa.ap(slice(0, 64)), AF.Square),
            reads=[qa.buf], pwrites=[swb.buf])
        for tc in range(NTC):
            s.op("pe", lambda e, tc=tc, swb=swb: e.matmul(
                P.psum[64:65, 3584:4096], ones_b.ap(slice(0, 64), slice(0, 1)),
                swb.ap(slice(0, 64), slice(tc * 512, (tc + 1) * 512)),
                start=True, stop=True, tile_position=(0, 64)),
                reads=[swb.buf, ones_b.buf], writes=[P.pbk[7]])
            s.op("act", lambda e, tc=tc: e.activation(
                rows.ap(r64, slice(po + 512, po + 1024)),
                P.psum[64:65, 3584:4096], AF.Ln),
                reads=[P.pbk[7]], pwrites=[rows.buf])
            s.op("dve", lambda e, tc=tc: e.tensor_scalar_add(
                rows.ap(r64, slice(po + 1024, po + 1536)),
                rows.ap(r64, slice(po + 512, po + 1024)),
                rows.ap(r64, slice(po + 9, po + 10))),
                reads=[rows.buf], pwrites=[rows.buf])
            s.op("act", lambda e, tc=tc: e.activation(
                rows.ap(r64, slice(po + 512, po + 1024)),
                rows.ap(r64, slice(po + 1024, po + 1536)), AF.Exp, scale=0.5),
                reads=[rows.buf], pwrites=[rows.buf])
            s.op("dve", lambda e, tc=tc, qa=qa: e.tensor_scalar_mul(
                qa.ap(r64, slice(tc * 512, (tc + 1) * 512)),
                rows.ap(r64, slice(po + 512, po + 1024)), -1.0),
                reads=[rows.buf], pwrites=[qa.buf])

    prep_early(0)
    prep_late(0)

    for h in range(H):
        p2 = h % 2
        qkf, qa, ka, vh = QKF[p2], QA[p2], KA[p2], Vh[p2]
        sw, swb = SW[p2], SWb[p2]
        po = p2 * 2048

        pend = []

        def flush_pend(pend=pend):
            while pend:
                pend.pop(0)()

        if soft:
            tiles = []
            for qc in range(NTC):
                ng = (2 * qc + 2) if causal else 16
                for g in range(ng):
                    tiles.append((qc, g, ng))

            def em_qk(i, qa=qa, ka=ka):
                qc, g, ng = tiles[i]
                b0 = (i % 3) * 2
                qs = slice(qc * 512, (qc + 1) * 512)

                def mmst(e, g=g, b0=b0, qs=qs):
                    last = None
                    for j in range(2):
                        kt = 2 * g + j
                        last = e.matmul(
                            P.pb(b0 + j),
                            ka.ap(slice(0, 65), slice(kt * 128, (kt + 1) * 128)),
                            qa.ap(slice(0, 65), qs),
                            start=True, stop=True)
                    return last
                s.op("pe", mmst, reads=[qa.buf, ka.buf],
                     writes=[P.pbk[b0], P.pbk[b0 + 1]])

            def em_exp(i):
                qc, g, ng = tiles[i]
                b0 = (i % 3) * 2
                ptg = pt[i % 3]
                s.op("act", lambda e, b0=b0, ptg=ptg: e.activation(
                    ptg.ap(), P.pb(b0, 2), AF.Exp),
                    reads=[P.pbk[b0], P.pbk[b0 + 1]], writes=[ptg.buf])
                if causal:
                    for j in range(2):
                        kt = 2 * g + j
                        if kt >= 4 * qc:
                            mi = 3 - (kt - 4 * qc)
                            s.op("dve", lambda e, j=j, mi=mi, ptg=ptg: e.tensor_mul(
                                ptg.ap(None, slice(j * 512, (j + 1) * 512)),
                                ptg.ap(None, slice(j * 512, (j + 1) * 512)),
                                msk.ap(None, slice(mi * 512, (mi + 1) * 512))),
                                reads=[ptg.buf, msk.buf], pwrites=[ptg.buf])

            def em_pv(i, vh=vh):
                qc, g, ng = tiles[i]
                ptg = pt[i % 3]

                def mmpv(e, g=g, ng=ng, ptg=ptg):
                    last = None
                    for j in range(2):
                        kt = 2 * g + j
                        last = e.matmul(
                            P.psum[0:65, 3072:3584],
                            vh.ap(None, slice(kt * 65, kt * 65 + 65)),
                            ptg.ap(None, slice(j * 512, (j + 1) * 512)),
                            start=(kt == 0), stop=(kt == 2 * ng - 1))
                    return last
                s.op("pe", mmpv, reads=[ptg.buf, vh.buf],
                     writes=[P.pbk[6]] if g == 0 else [],
                     pwrites=[P.pbk[6]] if g else [])

            def mk_epi(qc, sw=sw, po=po):
                qs = slice(qc * 512, (qc + 1) * 512)
                s.op("act", lambda e, qs=qs, sw=sw: e.activation(
                    sw.ap(slice(0, 65), qs), P.psum[0:65, 3072:3584], AF.Copy),
                    reads=[P.pbk[6]], pwrites=[sw.buf])

                def rest(qc=qc, qs=qs, sw=sw, po=po):
                    r64 = slice(64, 65)
                    s.op("act", lambda e, qs=qs, sw=sw, po=po: e.activation(
                        rows.ap(r64, slice(po + 1536, po + 2048)),
                        sw.ap(slice(64, 65), qs), AF.Ln),
                        reads=[sw.buf], pwrites=[rows.buf])
                    s.op("act", lambda e, po=po: e.activation(
                        rows.ap(r64, slice(po + 1024, po + 1536)),
                        rows.ap(r64, slice(po + 1536, po + 2048)), AF.Exp,
                        scale=-1.0),
                        reads=[rows.buf], pwrites=[rows.buf])
                    s.op("pe", lambda e, po=po: e.matmul(
                        P.pb(7, rows=slice(0, 64)),
                        onesf.ap(slice(64, 65), slice(0, 64)),
                        rows.ap(r64, slice(po + 1024, po + 1536)),
                        start=True, stop=True),
                        reads=[rows.buf, onesf.buf], writes=[P.pbk[7]])
                    s.op("dve", lambda e, qs=qs, sw=sw: e.tensor_mul(
                        att.ap(slice(0, 64), qs), sw.ap(slice(0, 64), qs),
                        P.pb(7, rows=slice(0, 64))),
                        reads=[sw.buf, P.pbk[7]],
                        writes=[att.buf] if qc == 0 else [],
                        pwrites=[att.buf] if qc else [])
                pend.append(rest)

            LA = 3
            for j in range(min(LA, len(tiles))):
                em_qk(j)
            for i, (qc, g, ng) in enumerate(tiles):
                em_exp(i)
                if i + LA < len(tiles):
                    em_qk(i + LA)
                em_pv(i)
                if g == min(1, ng - 1) and pend:
                    flush_pend()
                if g == ng - 1:
                    mk_epi(qc)
                if i == max(0, len(tiles) // 8) and h + 1 < H:
                    prep_early(h + 1)
                if i == (len(tiles) * 3) // 4 and h + 1 < H:
                    prep_late(h + 1)
        else:
            tiles = []
            for qc in range(NTC):
                nkt = 4 * qc + 4
                for kt in range(nkt):
                    tiles.append((qc, kt, nkt))

            def em_st(i, qa=qa, ka=ka):
                qc, kt, nkt = tiles[i]
                bank = i % 4
                qs = slice(qc * 512, (qc + 1) * 512)
                s.op("pe", lambda e, kt=kt, bank=bank, qs=qs: e.matmul(
                    P.pb(bank), ka.ap(slice(0, 64), slice(kt * 128, (kt + 1) * 128)),
                    qa.ap(slice(0, 64), qs), start=True, stop=True),
                    reads=[qa.buf, ka.buf], writes=[P.pbk[bank]])

            def em_evac(i, h=h):
                qc, kt, nkt = tiles[i]
                bank = i % 4
                ptg = pt[i % 4]
                d0i = (qc * 512 - kt * 128 + 384) // 128
                rfap = rf.ap(None, slice(h * 35 + d0i, h * 35 + d0i + 1))
                if kt >= 4 * qc:
                    mi = 3 - (kt - 4 * qc)
                    s.op("dve", lambda e, bank=bank, ptg=ptg, rfap=rfap, mi=mi:
                         e.scalar_tensor_tensor(
                             ptg.ap(), P.pb(bank), rfap,
                             msk.ap(None, slice(mi * 512, (mi + 1) * 512)),
                             OP.mult, OP.mult),
                         reads=[P.pbk[bank], rf.buf, msk.buf], writes=[ptg.buf])
                elif kt % 2 == 0:
                    s.op("act", lambda e, bank=bank, ptg=ptg, rfap=rfap:
                         e.activation(ptg.ap(), P.pb(bank), AF.Copy, scale=rfap),
                         reads=[P.pbk[bank], rf.buf], writes=[ptg.buf])
                else:
                    s.op("dve", lambda e, bank=bank, ptg=ptg, rfap=rfap:
                         e.tensor_scalar_mul(ptg.ap(), P.pb(bank), rfap),
                         reads=[P.pbk[bank], rf.buf], writes=[ptg.buf])

            def em_pvd(i, vh=vh):
                qc, kt, nkt = tiles[i]
                ptg = pt[i % 4]
                s.op("pe", lambda e, kt=kt, nkt=nkt, vh=vh, ptg=ptg: e.matmul(
                    P.psum[0:64, 3072:3584],
                    vh.ap(None, slice(kt * 64, (kt + 1) * 64)),
                    ptg.ap(), start=(kt == 0), stop=(kt == nkt - 1)),
                    reads=[ptg.buf, vh.buf],
                    writes=[P.pbk[6]] if kt == 0 else [],
                    pwrites=[P.pbk[6]] if kt else [])

            def mk_epi(qc, h=h, sw=sw):
                qs = slice(qc * 512, (qc + 1) * 512)
                robs, rosqs = rob[qc % 2], rosq[qc % 2]
                s.op("act", lambda e, qs=qs, sw=sw: e.activation(
                    sw.ap(slice(0, 64), qs), P.psum[0:64, 3072:3584], AF.Copy),
                    reads=[P.pbk[6]], pwrites=[sw.buf])

                def rest(qc=qc, qs=qs, h=h, sw=sw, robs=robs, rosqs=rosqs):
                    r1 = slice(0, 1)
                    s.op("dve", lambda e, qs=qs, robs=robs, sw=sw: e.tensor_copy(
                        robs.ap(slice(0, 64)), sw.ap(slice(0, 64), qs)),
                        reads=[sw.buf], writes=[robs.buf])
                    s.op("act", lambda e, qs=qs, rosqs=rosqs, sw=sw: e.activation(
                        rosqs.ap(slice(0, 64)), sw.ap(slice(0, 64), qs), AF.Square),
                        reads=[sw.buf], writes=[rosqs.buf])
                    s.op("pe", lambda e, robs=robs: e.matmul(
                        P.psum[0:1, 3584:4096], ones_b.ap(slice(0, 64), slice(0, 1)),
                        robs.ap(slice(0, 64)), start=True, stop=True),
                        reads=[robs.buf, ones_b.buf], writes=[P.pbk[7]])
                    s.op("pe", lambda e, rosqs=rosqs: e.matmul(
                        P.psum[0:1, 2560:3072], ones_b.ap(slice(0, 64), slice(0, 1)),
                        rosqs.ap(slice(0, 64)), start=True, stop=True),
                        reads=[rosqs.buf, ones_b.buf], writes=[P.pbk[5]])
                    s.op("act", lambda e: e.activation(
                        rows.ap(r1, slice(0, 512)), P.psum[0:1, 3584:4096],
                        AF.Copy, scale=1.0 / 64),
                        reads=[P.pbk[7]], pwrites=[rows.buf])
                    s.op("act", lambda e: e.activation(
                        rows.ap(r1, slice(512, 1024)), rows.ap(r1, slice(0, 512)),
                        AF.Square),
                        reads=[rows.buf], pwrites=[rows.buf])
                    s.op("dve", lambda e: e.scalar_tensor_tensor(
                        rows.ap(r1, slice(1024, 1536)), P.psum[0:1, 2560:3072],
                        1.0 / 64, rows.ap(r1, slice(512, 1024)),
                        OP.mult, OP.subtract),
                        reads=[P.pbk[5], rows.buf], pwrites=[rows.buf])
                    s.op("act", lambda e: e.activation(
                        rows.ap(r1, slice(1536, 2048)), rows.ap(r1, slice(1024, 1536)),
                        AF.Ln, bias=1e-5),
                        reads=[rows.buf], pwrites=[rows.buf])
                    s.op("act", lambda e: e.activation(
                        rows.ap(r1, slice(2048, 2560)), rows.ap(r1, slice(1536, 2048)),
                        AF.Exp, scale=-0.5),
                        reads=[rows.buf], pwrites=[rows.buf])
                    s.op("pe", lambda e: e.matmul(
                        P.pb(4, rows=slice(0, 64)), onesf.ap(slice(0, 1), slice(0, 64)),
                        rows.ap(r1, slice(0, 512)), start=True, stop=True),
                        reads=[rows.buf, onesf.buf], writes=[P.pbk[4]])
                    s.op("pe", lambda e: e.matmul(
                        P.pb(5, rows=slice(0, 64)), onesf.ap(slice(0, 1), slice(0, 64)),
                        rows.ap(r1, slice(2048, 2560)), start=True, stop=True),
                        reads=[rows.buf, onesf.buf], writes=[P.pbk[5]])
                    s.op("dve", lambda e, qs=qs, sw=sw: e.tensor_sub(
                        sw.ap(slice(0, 64), qs), sw.ap(slice(0, 64), qs),
                        P.pb(4, rows=slice(0, 64))),
                        reads=[sw.buf, P.pbk[4]], pwrites=[sw.buf])
                    s.op("dve", lambda e, qs=qs, sw=sw: e.tensor_mul(
                        sw.ap(slice(0, 64), qs), sw.ap(slice(0, 64), qs),
                        P.pb(5, rows=slice(0, 64))),
                        reads=[sw.buf, P.pbk[5]], pwrites=[sw.buf])
                    s.op("dve", lambda e, qs=qs, h=h, sw=sw: e.tensor_scalar(
                        att.ap(slice(0, 64), qs), sw.ap(slice(0, 64), qs),
                        gv.ap(slice(0, 64), slice(h, h + 1)),
                        gb.ap(slice(0, 64), slice(h, h + 1)), OP.mult, OP.add),
                        reads=[sw.buf, gv.buf, gb.buf],
                        writes=[att.buf] if qc == 0 else [],
                        pwrites=[att.buf] if qc else [])
                pend.append(rest)

            LA = 4
            for j in range(min(LA, len(tiles))):
                em_st(j)
            for i, (qc, kt, nkt) in enumerate(tiles):
                em_evac(i)
                if i + LA < len(tiles):
                    em_st(i + LA)
                em_pvd(i)
                if kt == min(1, nkt - 1) and pend:
                    flush_pend()
                if kt == nkt - 1:
                    mk_epi(qc)
                if i == max(0, len(tiles) // 8) and h + 1 < H:
                    prep_early(h + 1)
                if i == (len(tiles) * 3) // 4 and h + 1 < H:
                    prep_late(h + 1)
        flush_pend()
        s.op("sp", lambda e, h=h: e.dma_start(
            hdD[h * 64:(h + 1) * 64, :], att.ap(slice(0, 64))),
            reads=[att.buf], pwrites=[hdB], dma=True)


def attn_out(P, cfg):
    s = P.s
    ostats = cfg.get("out_stats", False)
    hdD, hdB = P.dram[cfg["hd"]], P.dbuf[cfg["hd"]]
    resD, resB = P.dram[cfg["res"]], P.dbuf[cfg["res"]]
    dstD, dstB = P.dram[cfg["dst"]], P.dbuf[cfg["dst"]]
    wC, wCb = P.cache(cfg["ow"])
    hd = [P.ab(4096, f"o_hd{c}") for c in range(8)]
    ww = [P.ab(1024, f"o_w{c}") for c in range(8)]
    for c in range(8):
        hdc, wwc = hd[c], ww[c]
        s.op("sp", lambda e, c=c, t=hdc: e.dma_start(
            t.ap(), hdD[c * 128:(c + 1) * 128, :]),
            reads=[hdB], writes=[hdc.buf], dma=True)
        s.op("sp", lambda e, c=c, t=wwc: e.dma_start(
            t.ap(), wC[c * 128:(c + 1) * 128, :]),
            reads=[wCb], writes=[wwc.buf], dma=True)
    obv = None
    if cfg["ob"] is not None:
        obv = P.af(16, "o_b")
        load_weight_f32(P, cfg["ob"], obv, cols=8)
    xr = [P.af(4096, f"o_x{i}") for i in range(2)]
    ou = [P.af(4096, f"o_o{i}") for i in range(2)]
    if ostats:
        sqb = [P.ab(512, f"o_s{i}") for i in range(2)]
        ones_b = P.ab(16, "o_onesb")
        load_const(P, "ones_bf", ones_b, 128, 1)
    for o in range(8):
        xrs, ous = xr[o % 2], ou[o % 2]
        s.op("sp", lambda e, o=o, xrs=xrs: e.dma_start(
            xrs.ap(), resD[o * 128:(o + 1) * 128, :]),
            reads=[resB], writes=[xrs.buf], dma=True)
        for tc in range(NTC):
            bank = tc % 4

            def mko(o=o, tc=tc, bank=bank):
                def f(e):
                    last = None
                    for c in range(8):
                        last = e.matmul(
                            P.pb(bank), ww[c].ap(None, slice(o * 128, (o + 1) * 128)),
                            hd[c].ap(None, slice(tc * 512, (tc + 1) * 512)),
                            start=(c == 0), stop=(c == 7))
                    return last
                return f
            s.op("pe", mko(), reads=[t.buf for t in ww] + [t.buf for t in hd],
                 writes=[P.pbk[bank]])
            bias = obv.ap(None, slice(o, o + 1)) if obv is not None else 0.0
            s.op("dve", lambda e, tc=tc, bank=bank, ous=ous, xrs=xrs, bias=bias:
                 e.scalar_tensor_tensor(
                     ous.ap(None, slice(tc * 512, (tc + 1) * 512)), P.pb(bank), bias,
                     xrs.ap(None, slice(tc * 512, (tc + 1) * 512)), OP.add, OP.add),
                 reads=[P.pbk[bank], xrs.buf] + ([obv.buf] if obv is not None else []),
                 writes=[ous.buf] if tc == 0 else [], pwrites=[ous.buf] if tc else [])
            if ostats:
                sb2 = sqb[tc % 2]
                sr = 32 * (tc // 4)
                sbk = 4 + (tc % 4)
                s.op("act", lambda e, tc=tc, ous=ous, sb2=sb2: e.activation(
                    sb2.ap(), ous.ap(None, slice(tc * 512, (tc + 1) * 512)), AF.Square),
                    reads=[ous.buf], writes=[sb2.buf])
                s.op("pe", lambda e, o=o, sb2=sb2, sr=sr, sbk=sbk: e.matmul(
                    P.psum[sr:sr + 1, sbk * 512:(sbk + 1) * 512],
                    ones_b.ap(slice(0, 128), slice(0, 1)), sb2.ap(),
                    start=(o == 0), stop=(o == 7), tile_position=(0, sr)),
                    reads=[sb2.buf, ones_b.buf],
                    writes=[P.pbk[sbk]] if o == 0 and tc < 4 else [],
                    pwrites=[P.pbk[sbk]] if not (o == 0 and tc < 4) else [])
        s.op("sp", lambda e, o=o, ous=ous: e.dma_start(
            dstD[o * 128:(o + 1) * 128, :], ous.ap()),
            reads=[ous.buf], pwrites=[dstB], dma=True)


def ph_moe(P):
    s = P.s
    x1D, x1B = P.dram["x1T"], P.dbuf["x1T"]
    paD, paB = P.dram["partialT"], P.dbuf["partialT"]
    y0D, y0B = P.dram["y0T"], P.dbuf["y0T"]
    gC, gCb = P.cache("gateT")
    w1C, w1Cb = P.cache("moe_w1")
    w2C, w2Cb = P.cache("moe_w2")
    w3C, w3Cb = P.cache("moe_w3")
    b3C, b3Cb = P.cache("moe_b3")

    n2 = P.af(16, "m_n2")
    load_weight_f32(P, "norm2w", n2, cols=8)
    b1v = P.af(144, "m_b1")
    b2v = P.af(144, "m_b2")
    for vec, wn in ((b1v, "moe_b1"), (b2v, "moe_b2")):
        src = P.wext[wn]
        s.op("sp", lambda e, vec=vec, src=src: e.dma_start(
            vec.ap(None, slice(0, 128)).rearrange("p (e k) -> p e k", k=16),
            src.rearrange("e p k -> p e k")),
            reads=[P.dbuf["w_" + wn]], writes=[vec.buf], dma=True)
    onesf = P.af(144, "m_onesf")
    for r0 in (0, 32):
        s.op("sp", lambda e, r0=r0: e.dma_start(
            onesf.ap(slice(r0, r0 + 1), slice(0, 128)), P.ct["ones_f"].ap()),
            reads=[P.cbuf["ones_f"]], pwrites=[onesf.buf], dma=True)
    ones_b = P.ab(16, "m_onesb")
    load_const(P, "ones_bf", ones_b, 128, 1)
    ident = P.ab(128, "m_ident")
    load_const(P, "ident", ident, 128, 128)
    sel8 = P.ab(1024, "m_sel8")
    load_const(P, "sel8", sel8, 8, 1024)
    gsb = P.ab(64, "m_gate")
    s.op("sp", lambda e: e.dma_start(
        gsb.ap(None, slice(0, 64)).rearrange("p (c e) -> p c e", e=8),
        gC.rearrange("(c p) e -> p c e", p=128)),
        reads=[gCb], writes=[gsb.buf], dma=True)
    b3sb = P.ab(1024, "m_b3")
    s.op("sp", lambda e: e.dma_start(b3sb.ap(slice(0, 8)), b3C.ap()),
         reads=[b3Cb], writes=[b3sb.buf], dma=True)

    xin = [P.af(512, f"m_x{i}") for i in range(4)]
    sq = [P.ab(512, f"m_q{i}") for i in range(4)]
    r2 = [P.ab(512, f"m_r2{c}") for c in range(8)]
    acc = [P.af(512, f"m_ac{o}") for o in range(8)]
    lg = P.af(16, "m_lg")
    tp1 = P.af(16, "m_t1")
    tp2 = P.af(16, "m_t2")
    cmb = P.ab(16, "m_cmb")
    combT = P.ab(512, "m_combT")
    combB = P.ab(512, "m_combB")
    w1sb = [P.ab(16384, f"m_w1{i}") for i in range(2)]
    w2sb = [P.ab(16384, "m_w20")]
    w3r = [P.ab(1024, f"m_w3{i}") for i in range(2)]
    usil = P.af(512, "m_us")
    hsb = P.ab(8192, "m_h")
    t1s = P.af(512, "m_t1s")
    yo = [P.af(512, f"m_yo{i}") for i in range(2)]
    rows2 = P.af(4096, "m_rows")
    srow = SBuf(P.ar.t, hsb.c0, 4096, F32, "m_srow")
    srow.buf = hsb.buf

    assert MODE == "fused"
    r1 = slice(0, 1)
    for ch in range(8):
        r = 32 * (ch // 4)
        bank = 4 + (ch % 4)
        seg = slice(ch * 512, (ch + 1) * 512)
        s.op("act", lambda e, r=r, bank=bank, seg=seg: e.activation(
            srow.ap(slice(r, r + 1), seg),
            P.psum[r:r + 1, bank * 512:(bank + 1) * 512],
            AF.Ln, scale=1.0 / C, bias=1e-6),
            reads=[P.pbk[bank]],
            writes=[srow.buf] if ch == 0 else [], pwrites=[srow.buf] if ch else [])
        s.op("act", lambda e, r=r, seg=seg: e.activation(
            rows2.ap(slice(r, r + 1), seg), srow.ap(slice(r, r + 1), seg),
            AF.Exp, scale=-0.5),
            reads=[srow.buf],
            writes=[rows2.buf] if ch == 0 else [], pwrites=[rows2.buf] if ch else [])

    def ldw(slot, cache, cb, e):
        for c in range(8):
            s.op("sp", lambda e2, c=c, slot=slot, cache=cache, e=e: e2.dma_start(
                slot.ap(None, slice(c * 2048, (c + 1) * 2048)),
                cache[e, c * 128:(c + 1) * 128, :]),
                reads=[cb], writes=[slot.buf] if c == 0 else [],
                pwrites=[slot.buf] if c else [], dma=True)

    for ch in range(8):
        ts = slice(ch * 512, (ch + 1) * 512)
        rr = 32 * (ch // 4)
        s.op("pe", lambda e, ts=ts, rr=rr: e.matmul(
            P.pb(6), onesf.ap(slice(rr, rr + 1), slice(0, 128)),
            rows2.ap(slice(rr, rr + 1), ts), start=True, stop=True),
            reads=[rows2.buf, onesf.buf], writes=[P.pbk[6]])
        for c in range(8):
            xb, xb2 = xin[c % 2], xin[2 + c % 2]
            s.op("sp", lambda e, c=c, ts=ts, xb=xb: e.dma_start(
                xb.ap(), x1D[c * 128:(c + 1) * 128, ts]),
                reads=[x1B], writes=[xb.buf], dma=True)
            s.op("dve", lambda e, xb=xb, xb2=xb2: e.tensor_mul(
                xb2.ap(), xb.ap(), P.pb(6)),
                reads=[xb.buf, P.pbk[6]], writes=[xb2.buf])
            s.op("act", lambda e, c=c, xb2=xb2: e.activation(
                r2[c].ap(), xb2.ap(), AF.Copy,
                scale=n2.ap(None, slice(c, c + 1))),
                reads=[xb2.buf, n2.buf], writes=[r2[c].buf])
        for tt in range(4):
            def gmm(e, tt=tt):
                last = None
                for c in range(8):
                    last = e.matmul(P.psum[0:128, 3584:3592],
                                    r2[c].ap(None, slice(tt * 128, (tt + 1) * 128)),
                                    gsb.ap(None, slice(c * 8, (c + 1) * 8)),
                                    start=(c == 0), stop=(c == 7))
                return last
            s.op("pe", gmm, reads=[t.buf for t in r2] + [gsb.buf], writes=[P.pbk[7]])
            s.op("act", lambda e: e.activation(lg.ap(None, slice(0, 8)),
                                               P.psum[0:128, 3584:3592], AF.Copy),
                 reads=[P.pbk[7]], writes=[lg.buf])
            s.op("dve", lambda e: e.reduce_max(tp1.ap(None, slice(0, 1)),
                                               lg.ap(None, slice(0, 8)), AX),
                 reads=[lg.buf], writes=[tp1.buf])
            s.op("dve", lambda e: e.tensor_scalar(
                tp2.ap(None, slice(0, 8)), lg.ap(None, slice(0, 8)),
                tp1.ap(None, slice(0, 1)), -1e30, OP.is_equal, OP.mult),
                reads=[lg.buf, tp1.buf], writes=[tp2.buf])
            s.op("dve", lambda e: e.tensor_add(
                tp2.ap(None, slice(8, 16)), tp2.ap(None, slice(0, 8)),
                lg.ap(None, slice(0, 8))),
                reads=[tp2.buf, lg.buf], pwrites=[tp2.buf])
            s.op("dve", lambda e: e.reduce_max(tp1.ap(None, slice(1, 2)),
                                               tp2.ap(None, slice(8, 16)), AX),
                 reads=[tp2.buf], pwrites=[tp1.buf])
            s.op("dve", lambda e: e.tensor_scalar_mul(
                tp1.ap(None, slice(2, 3)), tp1.ap(None, slice(0, 1)), -1.0),
                reads=[tp1.buf], pwrites=[tp1.buf])
            s.op("act", lambda e: e.activation(
                tp1.ap(None, slice(3, 4)), tp1.ap(None, slice(1, 2)), AF.Exp,
                bias=tp1.ap(None, slice(2, 3))),
                reads=[tp1.buf], pwrites=[tp1.buf])
            s.op("dve", lambda e: e.tensor_scalar_add(
                tp1.ap(None, slice(4, 5)), tp1.ap(None, slice(3, 4)), 1.0),
                reads=[tp1.buf], pwrites=[tp1.buf])
            s.op("dve", lambda e: e.reciprocal(
                tp1.ap(None, slice(5, 6)), tp1.ap(None, slice(4, 5))),
                reads=[tp1.buf], pwrites=[tp1.buf])
            s.op("act", lambda e: e.activation(
                tp2.ap(None, slice(0, 8)), lg.ap(None, slice(0, 8)), AF.Exp,
                bias=tp1.ap(None, slice(2, 3))),
                reads=[lg.buf, tp1.buf], pwrites=[tp2.buf])
            s.op("dve", lambda e: e.tensor_scalar(
                tp2.ap(None, slice(8, 16)), lg.ap(None, slice(0, 8)),
                tp1.ap(None, slice(1, 2)), None, OP.is_ge),
                reads=[lg.buf, tp1.buf], pwrites=[tp2.buf])
            s.op("dve", lambda e: e.tensor_mul(
                tp2.ap(None, slice(0, 8)), tp2.ap(None, slice(0, 8)),
                tp2.ap(None, slice(8, 16))),
                reads=[tp2.buf], pwrites=[tp2.buf])
            s.op("dve", lambda e: e.tensor_scalar_mul(
                cmb.ap(None, slice(0, 8)), tp2.ap(None, slice(0, 8)),
                tp1.ap(None, slice(5, 6))),
                reads=[tp2.buf, tp1.buf], writes=[cmb.buf])
            s.op("pe", lambda e: e.transpose(
                P.psum[:, 1536:2048].bitcast(BF16)[0:8, 0:128],
                cmb.ap(None, slice(0, 8)), ident.ap()),
                reads=[cmb.buf, ident.buf], writes=[P.pbk[3]])
            s.op("act", lambda e, tt=tt: e.activation(
                combT.ap(slice(0, 8), slice(tt * 128, (tt + 1) * 128)),
                P.psum[:, 1536:2048].bitcast(BF16)[0:8, 0:128], AF.Copy),
                reads=[P.pbk[3]],
                writes=[combT.buf] if tt == 0 else [],
                pwrites=[combT.buf] if tt else [])
        for o in range(8):
            bank = o % 4
            s.op("pe", lambda e, o=o, bank=bank: e.matmul(
                P.pb(bank), b3sb.ap(slice(0, 8), slice(o * 128, (o + 1) * 128)),
                combT.ap(slice(0, 8)), start=True, stop=True),
                reads=[b3sb.buf, combT.buf], writes=[P.pbk[bank]])
            s.op("act", lambda e, o=o, bank=bank: e.activation(
                acc[o].ap(), P.pb(bank), AF.Copy),
                reads=[P.pbk[bank]], writes=[acc[o].buf])
        for ex in range(8):
            w1s, w2s = w1sb[ex % 2], w2sb[0]
            ldw(w1s, w1C, w1Cb, ex)
            ldw(w2s, w2C, w2Cb, ex)
            s.op("pe", lambda e, ex=ex: e.matmul(
                P.pb(4),
                sel8.ap(slice(0, 8), slice(ex * 128, (ex + 1) * 128)),
                combT.ap(slice(0, 8)), start=True, stop=True),
                reads=[sel8.buf, combT.buf], writes=[P.pbk[4]])
            s.op("dve", lambda e: e.tensor_copy(
                combB.ap(), P.pb(4)),
                reads=[P.pbk[4]], writes=[combB.buf])
            for ht in range(16):
                a = (ht % 2) * 2

                def umm(e, ht=ht, a=a, ws=w1s):
                    last = None
                    for c in range(8):
                        last = e.matmul(
                            P.pb(a),
                            ws.ap(None, slice(c * 2048 + ht * 128,
                                              c * 2048 + (ht + 1) * 128)),
                            r2[c].ap(), start=(c == 0), stop=(c == 7))
                    return last

                def umm2(e, ht=ht, a=a, ws=w2s):
                    last = None
                    for c in range(8):
                        last = e.matmul(
                            P.pb(a + 1),
                            ws.ap(None, slice(c * 2048 + ht * 128,
                                              c * 2048 + (ht + 1) * 128)),
                            r2[c].ap(), start=(c == 0), stop=(c == 7))
                    return last
                s.op("pe", umm, reads=[w1s.buf] + [t.buf for t in r2],
                     writes=[P.pbk[a]])
                s.op("pe", umm2, reads=[w2s.buf] + [t.buf for t in r2],
                     writes=[P.pbk[a + 1]])
                s.op("act", lambda e, ht=ht, a=a, ex=ex: e.activation(
                    usil.ap(), P.pb(a), AF.Silu,
                    bias=b1v.ap(None, slice(ex * 16 + ht, ex * 16 + ht + 1))),
                    reads=[P.pbk[a], b1v.buf], writes=[usil.buf])
                s.op("dve", lambda e, ht=ht, a=a, ex=ex: e.scalar_tensor_tensor(
                    hsb.ap(None, slice(ht * 512, (ht + 1) * 512)), P.pb(a + 1),
                    b2v.ap(None, slice(ex * 16 + ht, ex * 16 + ht + 1)),
                    usil.ap(), OP.add, OP.mult),
                    reads=[P.pbk[a + 1], b2v.buf, usil.buf],
                    writes=[hsb.buf] if ht == 0 else [],
                    pwrites=[hsb.buf] if ht else [])
            for ht in range(16):
                w3s = w3r[ht % 2]
                s.op("sp", lambda e, ht=ht, ex=ex, w3s=w3s: e.dma_start(
                    w3s.ap(), w3C[ex, ht * 128:(ht + 1) * 128, :]),
                    reads=[w3Cb], writes=[w3s.buf], dma=True)

                def w3mm(e, ht=ht, w3s=w3s):
                    last = None
                    for o in range(8):
                        last = e.matmul(
                            P.pb(o), w3s.ap(None, slice(o * 128, (o + 1) * 128)),
                            hsb.ap(None, slice(ht * 512, (ht + 1) * 512)),
                            start=(ht == 0), stop=(ht == 15))
                    return last
                s.op("pe", w3mm, reads=[w3s.buf, hsb.buf],
                     writes=[P.pbk[o2] for o2 in range(8)] if ht == 0 else [],
                     pwrites=[P.pbk[o2] for o2 in range(8)] if ht else [])
            for o in range(8):
                s.op("dve", lambda e, o=o: e.tensor_mul(
                    t1s.ap(), P.pb(o), combB.ap()),
                    reads=[P.pbk[o], combB.buf], writes=[t1s.buf])
                s.op("pool", lambda e, o=o: e.tensor_add(
                    acc[o].ap(), acc[o].ap(), t1s.ap()),
                    reads=[acc[o].buf, t1s.buf], pwrites=[acc[o].buf])
        for o in range(8):
            pi, x2 = xin[o % 2], xin[2 + o % 2]
            s.op("sp", lambda e, o=o, ts=ts, pi=pi: e.dma_start(
                pi.ap(), paD[o * 128:(o + 1) * 128, ts]),
                reads=[paB], writes=[pi.buf], dma=True)
            s.op("sp", lambda e, o=o, ts=ts, x2=x2: e.dma_start(
                x2.ap(), x1D[o * 128:(o + 1) * 128, ts]),
                reads=[x1B], writes=[x2.buf], dma=True)
            yos = yo[o % 2]
            s.op("dve", lambda e, o=o, x2=x2, yos=yos: e.tensor_add(
                yos.ap(), acc[o].ap(), x2.ap()),
                reads=[acc[o].buf, x2.buf], writes=[yos.buf])
            s.op("pool", lambda e, o=o, pi=pi, yos=yos: e.tensor_add(
                yos.ap(), yos.ap(), pi.ap()),
                reads=[yos.buf, pi.buf], pwrites=[yos.buf])
            s.op("sp", lambda e, o=o, ts=ts, yos=yos: e.dma_start(
                y0D[o * 128:(o + 1) * 128, ts], yos.ap()),
                reads=[yos.buf], pwrites=[y0B], dma=True)


def ph_ff1(P):
    s = P.s
    ynD, ynB = P.dram["ynTb"], P.dbuf["ynTb"]
    p1D, p1B = P.dram["p1T"], P.dbuf["p1T"]
    outD, outB = P.dram["outT"], P.dbuf["outT"]
    w1C, w1Cb = P.cache("ff_w1")
    w2C, w2Cb = P.cache("ff_w2")
    h1C, h1Cb = P.cache("ff_h1")
    h2C, h2Cb = P.cache("ff_h2")
    w3C, w3Cb = P.cache("ff_w3")
    s1 = P.af(32, "f_s1")
    s2 = P.af(32, "f_s2")
    hs1 = P.af(32, "f_hs1")
    hs2 = P.af(32, "f_hs2")
    s3 = P.af(16, "f_s3")
    load_weight_f32(P, "ff_s1", s1, cols=32)
    load_weight_f32(P, "ff_s2", s2, cols=32)
    load_weight_f32(P, "ff_hs1", hs1, cols=32)
    load_weight_f32(P, "ff_hs2", hs2, cols=32)
    load_weight_f32(P, "ff_s3", s3, cols=8)

    yn = [P.ab(1024, f"f_y{c}") for c in range(8)]
    hb = P.ab(32768, "f_h")
    gb = P.ab(32768, "f_g")
    wA = [P.ab(1024, f"f_wa{c}") for c in range(8)]
    wB = [P.ab(1024, f"f_wb{c}") for c in range(8)]
    wr = [P.ab(1024, f"f_wr{i}") for i in range(4)]
    usil = [P.af(512, f"f_u{i}") for i in range(4)]
    p1b = []
    for i in range(4):
        b = SBuf(P.ar.t, gb.c0 + i * 2048, 1024, F32, f"f_p{i}")
        b.buf = gb.buf
        p1b.append(b)
    ob = []
    for i in range(2):
        b = SBuf(P.ar.t, gb.c0 + (4 + i) * 2048, 1024, F32, f"f_o{i}")
        b.buf = gb.buf
        ob.append(b)
    TCW = 1024

    for ch in range(4):
        ts = slice(ch * TCW, (ch + 1) * TCW)
        for c in range(8):
            s.op("sp", lambda e, c=c, ts=ts: e.dma_start(
                yn[c].ap(), ynD[c * 128:(c + 1) * 128, ts]),
                reads=[ynB], writes=[yn[c].buf], dma=True)
        for ob4 in range(4):
            for c in range(8):
                s.op("sp", lambda e, c=c, ob4=ob4: e.dma_start(
                    wA[c].ap(), w1C[c * 128:(c + 1) * 128,
                                    ob4 * 1024:(ob4 + 1) * 1024]),
                    reads=[w1Cb], writes=[wA[c].buf], dma=True)
                s.op("pool", lambda e, c=c, ob4=ob4: e.dma_start(
                    wB[c].ap(), w2C[c * 128:(c + 1) * 128,
                                    ob4 * 1024:(ob4 + 1) * 1024]),
                    reads=[w2Cb], writes=[wB[c].buf], dma=True)
            for oi in range(8):
                ot = ob4 * 8 + oi
                for sub in range(2):
                    b1k = (oi * 2 + sub) % 4
                    b2k = 4 + b1k
                    us = usil[b1k]

                    def mku(e, oi=oi, sub=sub, b1k=b1k):
                        last = None
                        for c in range(8):
                            last = e.matmul(
                                P.pb(b1k),
                                wA[c].ap(None, slice(oi * 128, (oi + 1) * 128)),
                                yn[c].ap(None, slice(sub * 512, (sub + 1) * 512)),
                                start=(c == 0), stop=(c == 7))
                        return last

                    def mku2(e, oi=oi, sub=sub, b2k=b2k):
                        last = None
                        for c in range(8):
                            last = e.matmul(
                                P.pb(b2k),
                                wB[c].ap(None, slice(oi * 128, (oi + 1) * 128)),
                                yn[c].ap(None, slice(sub * 512, (sub + 1) * 512)),
                                start=(c == 0), stop=(c == 7))
                        return last
                    s.op("pe", mku, reads=[t.buf for t in wA] + [t.buf for t in yn],
                         writes=[P.pbk[b1k]])
                    s.op("pe", mku2, reads=[t.buf for t in wB] + [t.buf for t in yn],
                         writes=[P.pbk[b2k]])
                    s.op("act", lambda e, ot=ot, b1k=b1k, us=us: e.activation(
                        us.ap(), P.pb(b1k), AF.Silu,
                        scale=s1.ap(None, slice(ot, ot + 1))),
                        reads=[P.pbk[b1k], s1.buf], writes=[us.buf])
                    s.op("dve", lambda e, ot=ot, sub=sub, b2k=b2k, us=us:
                         e.scalar_tensor_tensor(
                             hb.ap(None, slice(ot * 1024 + sub * 512,
                                               ot * 1024 + (sub + 1) * 512)),
                             P.pb(b2k), s2.ap(None, slice(ot, ot + 1)),
                             us.ap(), OP.mult, OP.mult),
                         reads=[P.pbk[b2k], s2.buf, us.buf],
                         writes=[hb.buf] if ot == 0 and sub == 0 else [],
                         pwrites=[hb.buf] if (ot or sub) else [])
        for (wc, wcb, hs, src, dst) in ((h1C, h1Cb, hs1, hb, gb),
                                        (h2C, h2Cb, hs2, gb, hb)):
            for obk in range(8):
                for k2 in range(16):
                    wrs = wr[k2 % 4]
                    s.op("sp" if k2 % 2 == 0 else "pool",
                         lambda e, k2=k2, obk=obk, wrs=wrs, wc=wc: e.dma_start(
                             wrs.ap(None, slice(0, 1024)).rearrange(
                                 "p (k o) -> p k o", k=2),
                             wc[k2 * 256:(k2 + 1) * 256,
                                obk * 512:(obk + 1) * 512].rearrange(
                                 "(k p) o -> p k o", p=128)),
                         reads=[wcb], writes=[wrs.buf], dma=True)

                    def mkh(e, k2=k2, obk=obk, wrs=wrs, src=src):
                        last = None
                        for kk in range(2):
                            kt = k2 * 2 + kk
                            for sub in range(2):
                                for oi in range(4):
                                    last = e.matmul(
                                        P.pb(sub * 4 + oi),
                                        wrs.ap(None, slice(kk * 512 + oi * 128,
                                                           kk * 512 + (oi + 1) * 128)),
                                        src.ap(None, slice(kt * 1024 + sub * 512,
                                                           kt * 1024 + (sub + 1) * 512)),
                                        start=(kt == 0), stop=(kt == 31))
                        return last
                    s.op("pe", mkh, reads=[wrs.buf, src.buf],
                         writes=[P.pbk[i] for i in range(8)] if k2 == 0 else [],
                         pwrites=[P.pbk[i] for i in range(8)] if k2 else [])
                for sub in range(2):
                    for oi in range(4):
                        ot = obk * 4 + oi
                        s.op("act", lambda e, ot=ot, sub=sub, oi=oi, hs=hs, dst=dst:
                             e.activation(
                                 dst.ap(None, slice(ot * 1024 + sub * 512,
                                                    ot * 1024 + (sub + 1) * 512)),
                                 P.pb(sub * 4 + oi), AF.Gelu,
                                 scale=hs.ap(None, slice(ot, ot + 1))),
                             reads=[P.pbk[sub * 4 + oi], hs.buf],
                             writes=[dst.buf] if obk == 0 and ot == 0 and sub == 0 else [],
                             pwrites=[dst.buf] if (obk or ot or sub) else [])
        for obk in range(2):
            for oi in range(4):
                o = obk * 4 + oi
                pb1 = p1b[oi]
                s.op("sp", lambda e, o=o, ts=ts, pb1=pb1: e.dma_start(
                    pb1.ap(), p1D[o * 128:(o + 1) * 128, ts]),
                    reads=[p1B], writes=[pb1.buf], dma=True)
            for k2 in range(16):
                wrs = wr[k2 % 4]
                s.op("sp" if k2 % 2 == 0 else "pool",
                     lambda e, k2=k2, obk=obk, wrs=wrs: e.dma_start(
                         wrs.ap(None, slice(0, 1024)).rearrange(
                             "p (k o) -> p k o", k=2),
                         w3C[k2 * 256:(k2 + 1) * 256,
                             obk * 512:(obk + 1) * 512].rearrange(
                             "(k p) o -> p k o", p=128)),
                     reads=[w3Cb], writes=[wrs.buf], dma=True)

                def mk3(e, k2=k2, obk=obk, wrs=wrs):
                    last = None
                    for kk in range(2):
                        kt = k2 * 2 + kk
                        for sub in range(2):
                            for oi in range(4):
                                last = e.matmul(
                                    P.pb(sub * 4 + oi),
                                    wrs.ap(None, slice(kk * 512 + oi * 128,
                                                       kk * 512 + (oi + 1) * 128)),
                                    hb.ap(None, slice(kt * 1024 + sub * 512,
                                                      kt * 1024 + (sub + 1) * 512)),
                                    start=(kt == 0), stop=(kt == 31))
                    return last
                s.op("pe", mk3, reads=[wrs.buf, hb.buf],
                     writes=[P.pbk[i] for i in range(8)] if k2 == 0 else [],
                     pwrites=[P.pbk[i] for i in range(8)] if k2 else [])
            for oi in range(4):
                o = obk * 4 + oi
                pb1 = p1b[oi]
                obs = ob[oi % 2]
                for sub in range(2):
                    s.op("dve", lambda e, o=o, oi=oi, sub=sub, pb1=pb1, obs=obs:
                         e.scalar_tensor_tensor(
                             obs.ap(None, slice(sub * 512, (sub + 1) * 512)),
                             P.pb(sub * 4 + oi), s3.ap(None, slice(o, o + 1)),
                             pb1.ap(None, slice(sub * 512, (sub + 1) * 512)),
                             OP.mult, OP.add),
                         reads=[P.pbk[sub * 4 + oi], s3.buf, pb1.buf],
                         writes=[obs.buf] if sub == 0 else [],
                         pwrites=[obs.buf] if sub else [])
                s.op("sp", lambda e, o=o, ts=ts, obs=obs: e.dma_start(
                    outD[o * 128:(o + 1) * 128, ts], obs.ap()),
                    reads=[obs.buf], pwrites=[outB], dma=True)


PHASE_FNS = {
    "ln0": ph_ln0, "ln1": ph_ln1,
    "moe": ph_moe, "ff1": ph_ff1,
    "retp": lambda P: attn_prep(P, ACFG["ret"]),
    "reth": lambda P: attn_heads(P, ACFG["ret"]),
    "reto": lambda P: attn_out(P, ACFG["ret"]),
    "fap": lambda P: attn_prep(P, ACFG["fa"]),
    "fah": lambda P: attn_heads(P, ACFG["fa"]),
    "fao": lambda P: attn_out(P, ACFG["fa"]),
    "a1p": lambda P: attn_prep(P, ACFG["at1"]),
    "a1h": lambda P: attn_heads(P, ACFG["at1"]),
    "a1o": lambda P: attn_out(P, ACFG["at1"]),
}


def _run(inputs, phase_groups, taps=()):
    consts = make_consts()
    w = prep_weights(inputs)
    wshapes = {k: v.shape for k, v in w.items()}
    x = inputs["L0_attn_x"]
    xTs = [np.ascontiguousarray(np.asarray(x[i], dtype=np.float32).T) for i in range(8)]

    hbm = [{} for _ in range(8)]
    out = None
    taps_res = {}
    for gi, group in enumerate(phase_groups):
        later = set()
        for g2 in phase_groups[gi + 1:]:
            for p in g2:
                later |= set(PHASE_NEEDS[p])
        gtaps = [t for t in taps
                 if t in sum((PHASE_MAKES[p] for p in group), [])]
        P = Prog(group, wshapes, consts, later_needs=later, taps=gtaps)
        nc = P.build()
        in_maps = []
        for ci in range(8):
            m = {}
            for name in P.ext_in:
                if name == "xT":
                    m["xT"] = xTs[ci]
                elif name in w:
                    m["w_" + name] = w[name]
                else:
                    m[name] = hbm[ci][name]
            in_maps.append(m)
        res = run_bass_kernel_spmd(nc, in_maps, core_ids=list(range(8)))
        for ci in range(8):
            for name, arr in res.results[ci].items():
                if name.startswith("tap_"):
                    taps_res.setdefault(name[4:], [None] * 8)[ci] = arr
                elif name == "outT":
                    if out is None:
                        out = [None] * 8
                    out[ci] = arr
                else:
                    hbm[ci][name] = arr
    return out, taps_res, hbm


def kernel(**inputs):
    groups = [PHASES] if MODE == "fused" else [[p] for p in PHASES]
    out, _, _ = _run(inputs, groups)
    res = np.stack([o.T for o in out], axis=0).astype(np.float32)
    return np.ascontiguousarray(res)


# version 18
# speedup vs baseline: 1.0139x; 1.0109x over previous
import numpy as np
import ml_dtypes

import concourse.bass as bass
import concourse.mybir as mybir
from concourse.bass_utils import run_bass_kernel_spmd

F32 = mybir.dt.float32
BF16 = mybir.dt.bfloat16
AX = mybir.AxisListType.X
OP = mybir.AluOpType
AF = mybir.ActivationFunctionType
BF = ml_dtypes.bfloat16

T = 4096
C = 1024
H = 16
DH = 64
NK = C // 128
NTC = T // 512

MODE = "fused"
PHASES = ["ln0", "retp", "reth", "reto", "fap", "fah", "fao", "moe",
          "ln1", "a1p", "a1h", "a1o", "ff1"]

QUEUES = ("pe", "act", "dve", "pool", "sp")


class Buf:
    __slots__ = ("writers", "readers", "name")

    def __init__(self, name=""):
        self.writers = []
        self.readers = []
        self.name = name


NLANES = 8


class OpH:
    __slots__ = ("q", "fn", "deps", "inc", "sig", "val", "dma", "key", "bg")

    def __init__(self, q, fn, deps, dma):
        self.q = q
        self.fn = fn
        self.deps = deps
        self.inc = 16 if dma else 1
        self.dma = dma
        self.sig = False
        self.val = None
        self.key = None
        self.bg = False


class Sched:
    def __init__(self):
        self.ops = {q: [] for q in QUEUES}
        self.default_deps = []
        self.dma_count = {q: 0 for q in QUEUES}

    def op(self, q, fn, reads=(), writes=(), pwrites=(), deps=(), dma=False,
           dma_key=None):
        dl = list(self.default_deps)
        for b in reads:
            dl.extend(b.writers)
        for b in writes:
            dl.extend(b.writers)
            dl.extend(b.readers)
        for b in pwrites:
            dl.extend(b.readers)
        dl.extend(d for d in deps if d is not None)
        seen, deps2 = set(), []
        for d in dl:
            if id(d) not in seen:
                seen.add(id(d))
                deps2.append(d)
        o = OpH(q, fn, deps2, dma)
        if dma:
            kq = dma_key if dma_key is not None else q
            self.dma_count.setdefault(kq, 0)
            o.key = (kq, self.dma_count[kq] % NLANES)
            self.dma_count[kq] += 1
        else:
            o.key = q
        for d in deps2:
            if not (q == "pe" and d.key == "pe"):
                d.sig = True
        for b in reads:
            b.readers.append(o)
        for b in writes:
            b.writers = [o]
            b.readers = []
        for b in pwrites:
            b.writers.append(o)
        self.ops[q].append(o)
        return o

    def barrier(self):
        lasts = []
        for q in QUEUES:
            for o in reversed(self.ops[q]):
                if not o.bg:
                    lasts.append(o)
                    break
        for q in QUEUES:
            self.op(q, lambda e: e.nop(), deps=lasts)

    def finalize(self, sems):
        totals = {}
        for q in QUEUES:
            for o in self.ops[q]:
                if o.dma:
                    o.sig = True
                if o.sig:
                    v = totals.get(o.key, 0) + o.inc
                    o.val = v
                    totals[o.key] = v

        def mk(q):
            def run(eng):
                waited = {}

                def wait(key, val):
                    if val is None or waited.get(key, 0) >= val:
                        return
                    eng.wait_ge(sems[key], val)
                    waited[key] = val

                for o in self.ops[q]:
                    for d in o.deps:
                        if q == "pe" and d.key == "pe":
                            continue
                        wait(d.key, d.val)
                    if o.dma and o.val is not None and o.val > 16:
                        wait(o.key, o.val - 16)
                    inst = o.fn(eng)
                    if o.sig:
                        assert inst is not None, f"op on {q} needs inc but returned None"
                        inst.then_inc(sems[o.key], o.inc)
                for key, tv in sorted(totals.items(), key=str):
                    if key != q:
                        wait(key, tv)
            return run
        return {q: mk(q) for q in QUEUES}


class SBuf:
    __slots__ = ("t", "c0", "cols", "dt", "buf")

    def __init__(self, t, c0, cols, dt, name):
        self.t = t
        self.c0 = c0
        self.cols = cols
        self.dt = dt
        self.buf = Buf(name)

    def ap(self, rs=None, cs=None):
        r = rs if rs is not None else slice(0, 128)
        c = cs if cs is not None else slice(0, self.cols)
        if self.dt == BF16:
            return self.t[r, self.c0 + c.start: self.c0 + c.stop]
        return self.t[r, self.c0 + 2 * c.start: self.c0 + 2 * c.stop].bitcast(F32)


class Arena:

    def __init__(self, nc, name, cols_bf):
        self.t = nc.alloc_sbuf_tensor(name, [128, cols_bf], BF16)
        self.cols = cols_bf
        self.off = 0
        self.name = name

    def alloc(self, cols, dt=BF16, name=""):
        units = cols * (2 if dt == F32 else 1)
        c0 = (self.off + 15) & ~15
        assert c0 + units <= self.cols, \
            f"arena {self.name} overflow: {c0}+{units}>{self.cols} ({name})"
        self.off = c0 + units
        return SBuf(self.t, c0, cols, dt, name)

    def reset(self):
        self.off = 0


def make_consts():
    t = np.arange(T, dtype=np.float64)
    inv = 1.0 / 10000.0 ** (np.arange(0, DH, 2, dtype=np.float64) / DH)
    fr = t[:, None] * inv[None, :]
    emb = np.concatenate([fr, fr], axis=1)
    cosT = np.cos(emb).T.astype(np.float32)
    sinS = np.sin(emb).T.astype(np.float32)
    sinS[:32] *= -1.0
    cos8 = (cosT * 0.125).astype(np.float32)
    sin8 = (sinS * 0.125).astype(np.float32)

    gam = 1.0 - 2.0 ** (-5.0 - np.arange(H, dtype=np.float64))
    lng = np.log(gam)
    j = np.arange(512, dtype=np.float64)
    colf = np.exp(lng[:, None] * j[None, :])
    colf = np.repeat(colf[:, None, :], DH, axis=1).astype(np.float32)
    d0s = -384.0 + 128.0 * np.arange(35, dtype=np.float64)
    i = np.arange(128, dtype=np.float64)
    ex = d0s[None, :, None] - i[None, None, :]
    with np.errstate(over="ignore", under="ignore"):
        rf = np.exp(lng[:, None, None] * ex)
    rf = np.where(np.isfinite(rf), rf, 0.0).astype(np.float32)
    rf = np.ascontiguousarray(rf.transpose(2, 0, 1))
    jj = np.arange(512)
    ii = np.arange(128)
    masks = np.stack([(jj[None, :] - ii[:, None] + (-384 + 128 * k)) >= 0
                      for k in range(4)], axis=0).astype(BF)
    return {
        "cosT": cosT, "sinS": sinS, "cos8": cos8, "sin8": sin8,
        "colf": colf.reshape(H * DH, 512), "rf": rf,
        "masks": np.ascontiguousarray(masks.transpose(1, 0, 2).reshape(128, 4 * 512)),
        "ones_bf": np.ones((128, 1), dtype=BF),
        "ones_f": np.ones((1, 128), dtype=np.float32),
        "ones_row_bf": np.ones((1, 128), dtype=BF),
        "ident": np.eye(128, dtype=BF),
        "sel8": np.ascontiguousarray(np.kron(np.eye(8), np.ones((1, 128))).astype(BF)),
    }


def prep_weights(inp):
    w = {}
    WqT = np.ascontiguousarray(inp["L0_attn_Wq"].T)
    WkT = np.ascontiguousarray(inp["L0_attn_Wk"].T)
    w["ret_qk"] = np.ascontiguousarray(np.stack(
        [np.concatenate([WqT[:, h * 64:(h + 1) * 64], WkT[:, h * 64:(h + 1) * 64]], 1)
         for h in range(H)]))
    w["ret_v"] = np.ascontiguousarray(inp["L0_attn_Wv"].T)
    w["ret_wo"] = np.ascontiguousarray(inp["L0_attn_Wo"].T)
    w["gnw"] = np.ascontiguousarray(inp["L0_attn_gn_weight"].reshape(16, 64).T)
    w["gnb"] = np.ascontiguousarray(inp["L0_attn_gn_bias"].reshape(16, 64).T)

    qkv = inp["L0_ffn_qkv_w"]
    QT = np.ascontiguousarray(qkv[0:1024].T)
    KTm = np.ascontiguousarray(qkv[1024:2048].T)
    w["fa_qk"] = np.ascontiguousarray(np.stack(
        [np.concatenate([QT[:, h * 64:(h + 1) * 64], KTm[:, h * 64:(h + 1) * 64]], 1)
         for h in range(H)]))
    w["fa_v"] = np.ascontiguousarray(qkv[2048:3072].T)
    b = inp["L0_ffn_qkv_b"]
    w["fa_qkb"] = np.ascontiguousarray(np.stack(
        [np.concatenate([b[h * 64:(h + 1) * 64], b[1024 + h * 64:1024 + (h + 1) * 64]])
         for h in range(H)]).T)
    w["fa_vb"] = np.ascontiguousarray(b[2048:3072].reshape(1, 1024))
    w["fa_out"] = np.ascontiguousarray(inp["L0_ffn_out_w"].T)
    w["fa_outb"] = np.ascontiguousarray(inp["L0_ffn_out_b"].reshape(8, 128).T)
    w["norm1w"] = np.ascontiguousarray(inp["L0_ffn_norm1_w"].reshape(8, 128).T)
    w["norm2w"] = np.ascontiguousarray(inp["L0_ffn_norm2_w"].reshape(8, 128).T)
    w["gateT"] = np.ascontiguousarray(inp["L0_ffn_gate_w"].T)
    w["moe_w1"] = np.ascontiguousarray(inp["L0_ffn_w1"].transpose(0, 2, 1))
    w["moe_w2"] = np.ascontiguousarray(inp["L0_ffn_w2"].transpose(0, 2, 1))
    w["moe_w3"] = np.ascontiguousarray(inp["L0_ffn_w3"].transpose(0, 2, 1))
    w["moe_b1"] = np.ascontiguousarray(
        inp["L0_ffn_b1"].reshape(8, 16, 128).transpose(0, 2, 1))
    w["moe_b2"] = np.ascontiguousarray(
        inp["L0_ffn_b2"].reshape(8, 16, 128).transpose(0, 2, 1))
    w["moe_b3"] = np.ascontiguousarray(inp["L0_ffn_b3"])

    Wq1 = inp["L1_attn_Wqkv"]
    w["at1_qk"] = np.ascontiguousarray(Wq1[:, :, 0:128])
    w["at1_v"] = np.ascontiguousarray(
        Wq1[:, :, 128:192].transpose(1, 0, 2).reshape(1024, 1024))
    b1 = inp["L1_attn_bqkv"]
    w["at1_qkb"] = np.ascontiguousarray(b1[:, 0:128].T)
    w["at1_vb"] = np.ascontiguousarray(b1[:, 128:192].reshape(1, 1024))
    w["at1_out"] = np.ascontiguousarray(inp["L1_attn_Wout"])
    w["at1_outb"] = np.ascontiguousarray(inp["L1_attn_bout"].reshape(8, 128).T)

    w["ff_w1"] = np.ascontiguousarray(inp["L1_ffn_w1_w"].T)
    w["ff_w2"] = np.ascontiguousarray(inp["L1_ffn_w2_w"].T)
    w["ff_h1"] = np.ascontiguousarray(inp["L1_ffn_h1_w"].T)
    w["ff_h2"] = np.ascontiguousarray(inp["L1_ffn_h2_w"].T)
    w["ff_w3"] = np.ascontiguousarray(inp["L1_ffn_w3_w"].T)
    w["ff_s1"] = np.ascontiguousarray(inp["L1_ffn_w1_s"].reshape(32, 128).T)
    w["ff_s2"] = np.ascontiguousarray(inp["L1_ffn_w2_s"].reshape(32, 128).T)
    w["ff_hs1"] = np.ascontiguousarray(inp["L1_ffn_h1_s"].reshape(32, 128).T)
    w["ff_hs2"] = np.ascontiguousarray(inp["L1_ffn_h2_s"].reshape(32, 128).T)
    w["ff_s3"] = np.ascontiguousarray(inp["L1_ffn_w3_s"].reshape(8, 128).T)
    return w


CACHED = {"ret_qk", "ret_v", "ret_wo", "fa_qk", "fa_v", "fa_out", "gateT",
          "moe_w1", "moe_w2", "moe_w3", "moe_b3", "at1_qk", "at1_v", "at1_out",
          "ff_w1", "ff_w2", "ff_h1", "ff_h2", "ff_w3"}

SCRATCH = {
    "xnT": ((C, T), F32, "ln0"),
    "xnTb": ((C, T), BF16, "ln0"),
    "partialT": ((C, T), F32, "reto"),
    "x1T": ((C, T), F32, "fao"),
    "y0T": ((C, T), F32, "moe"),
    "ynT": ((C, T), F32, "ln1"),
    "ynTb": ((C, T), BF16, "ln1"),
    "p1T": ((C, T), F32, "a1o"),
    "r1st": ((1, T), F32, "ln0"),
    "qkf_ret": ((2048, T), F32, "retp"),
    "vt_ret": ((T, C), BF16, "retp"),
    "hd_ret": ((C, T), BF16, "reth"),
    "qkf_fa": ((2048, T), F32, "fap"),
    "vt_fa": ((T, C), BF16, "fap"),
    "hd_fa": ((C, T), BF16, "fah"),
    "qkf_at1": ((2048, T), BF16, "a1p"),
    "vt_at1": ((T, C), BF16, "a1p"),
    "hd_at1": ((C, T), BF16, "a1h"),
}

PHASE_NEEDS = {
    "ln0": ["xT"],
    "retp": ["xnTb"], "reth": ["qkf_ret", "vt_ret"], "reto": ["hd_ret", "xT"],
    "fap": ["xnT", "r1st"], "fah": ["qkf_fa", "vt_fa"], "fao": ["hd_fa", "xnT"],
    "moe": ["x1T", "partialT"],
    "ln1": ["y0T"],
    "a1p": ["ynTb"], "a1h": ["qkf_at1", "vt_at1"], "a1o": ["hd_at1", "y0T"],
    "ff1": ["ynTb", "p1T"],
}
PHASE_MAKES = {
    "ln0": ["xnT", "xnTb", "r1st"],
    "retp": ["qkf_ret", "vt_ret"], "reth": ["hd_ret"], "reto": ["partialT"],
    "fap": ["qkf_fa", "vt_fa"], "fah": ["hd_fa"], "fao": ["x1T"],
    "moe": ["y0T"],
    "ln1": ["ynT", "ynTb"],
    "a1p": ["qkf_at1", "vt_at1"], "a1h": ["hd_at1"], "a1o": ["p1T"],
    "ff1": ["outT"],
}
PHASE_WEIGHTS = {
    "ln0": [], "ln1": [],
    "retp": ["ret_qk", "ret_v"], "reth": ["gnw", "gnb"],
    "reto": ["ret_wo"],
    "fap": ["fa_qk", "fa_v", "fa_qkb", "fa_vb", "norm1w"], "fah": [],
    "fao": ["fa_out", "fa_outb"],
    "moe": ["gateT", "moe_w1", "moe_w2", "moe_w3", "moe_b1", "moe_b2",
            "moe_b3", "norm2w"],
    "a1p": ["at1_qk", "at1_v", "at1_qkb", "at1_vb"], "a1h": [],
    "a1o": ["at1_out", "at1_outb"],
    "ff1": ["ff_w1", "ff_w2", "ff_h1", "ff_h2", "ff_w3",
            "ff_s1", "ff_s2", "ff_hs1", "ff_hs2", "ff_s3"],
}

ACFG = {
    "ret": dict(qk="ret_qk", v="ret_v", qkb=None, vb=None, xsrc="xnTb",
                rms=None, rope=True, kind="decay", causal=True,
                qkf="qkf_ret", vt="vt_ret", hd="hd_ret",
                ow="ret_wo", ob=None, res="xT", dst="partialT"),
    "fa": dict(qk="fa_qk", v="fa_v", qkb="fa_qkb", vb="fa_vb", xsrc=None,
               rms=("xnT", "norm1w", 1e-6), rope=True, kind="softmax",
               causal=False, qkf="qkf_fa", vt="vt_fa", hd="hd_fa",
               ow="fa_out", ob="fa_outb", res="xnT", dst="x1T",
               out_stats=True),
    "at1": dict(qk="at1_qk", v="at1_v", qkb="at1_qkb", vb="at1_vb",
                xsrc="ynTb", rms=None, rope=False, kind="softmax",
                causal=True, qkf="qkf_at1", vt="vt_at1", hd="hd_at1",
                ow="at1_out", ob="at1_outb", res="y0T", dst="p1T"),
}


class Prog:
    def __init__(self, phases, wshapes, consts, later_needs=(), taps=()):
        self.nc = bass.Bass("TRN2", target_bir_lowering=False, debug=False)
        self.s = Sched()
        self.phases = phases
        nc = self.nc
        self.ext_in = []
        self.dram = {}
        self.dbuf = {}
        self.cache_done = {}

        needs, makes = set(), set()
        for ph in phases:
            needs |= set(PHASE_NEEDS[ph])
            makes |= set(PHASE_MAKES[ph])

        if "xT" in needs:
            self.dram["xT"] = nc.dram_tensor("xT", [C, T], F32, kind="ExternalInput")
            self.dbuf["xT"] = Buf("xT")
            self.ext_in.append("xT")
        if "outT" in makes:
            self.dram["outT"] = nc.dram_tensor("outT", [C, T], F32, kind="ExternalOutput")
            self.dbuf["outT"] = Buf("outT")
        for name, (shape, dt, prod) in SCRATCH.items():
            if name not in (needs | makes):
                continue
            if name in makes and name in later_needs:
                kind = "ExternalOutput"
            elif name in needs and name not in makes:
                kind = "ExternalInput"
            else:
                kind = "Internal"
            self.dram[name] = nc.dram_tensor(name, list(shape), dt, kind=kind)
            self.dbuf[name] = Buf(name)
            if kind == "ExternalInput":
                self.ext_in.append(name)

        self.wext = {}
        self.wcache = {}
        for ph in phases:
            for wn in PHASE_WEIGHTS[ph]:
                if wn in self.wext:
                    continue
                self.wext[wn] = nc.dram_tensor("w_" + wn, list(wshapes[wn]), F32,
                                               kind="ExternalInput")
                self.dbuf["w_" + wn] = Buf("w_" + wn)
                self.ext_in.append(wn)
                if wn in CACHED:
                    self.wcache[wn] = nc.dram_tensor("bc_" + wn, list(wshapes[wn]),
                                                     BF16, kind="Internal")
                    self.dbuf["bc_" + wn] = Buf("bc_" + wn)

        self.ct = {k: nc.inline_tensor(np.ascontiguousarray(v), name="c_" + k)
                   for k, v in consts.items()}
        self.cbuf = {k: Buf("c_" + k) for k in consts}

        self.ar = Arena(nc, "mega", 100608)
        self.psum = nc.alloc_psum_tensor("ps", [128, 4096], F32)
        self.sems = {q: nc.alloc_semaphore(name="sem_" + q) for q in QUEUES}
        for q in list(QUEUES) + ["bgp"]:
            for ln in range(NLANES):
                self.sems[(q, ln)] = nc.alloc_semaphore(name=f"sem_{q}_d{ln}")

        cvals = [0.0, 1.0, 1e-5, 1e-6, 0.125, -1.0]
        cv_np = np.tile(np.array([cvals], dtype=np.float32), (128, 1))
        cv_t = nc.inline_tensor(cv_np, name="c_scal")
        cv_sb = nc.alloc_sbuf_tensor("cscal", [128, 16], F32)
        boot = self.s.op("sp", lambda e: e.dma_start(cv_sb[0:128, 0:len(cvals)], cv_t.ap()),
                         dma=True)
        for i, v in enumerate(cvals):
            nc.const_aps.register(F32, float(v), cv_sb[0:128, i:i + 1])
        self.s.default_deps = [boot]

        self.taps = list(taps)
        self.tap_out = {}
        for tn in self.taps:
            if tn in SCRATCH:
                shape, dt, _ = SCRATCH[tn]
            else:
                shape, dt = (C, T), F32
            self.tap_out[tn] = nc.dram_tensor("tap_" + tn, list(shape), dt,
                                              kind="ExternalOutput")

    def ab(self, cols, name=""):
        return self.ar.alloc(cols, BF16, name)

    def af(self, cols, name=""):
        return self.ar.alloc(cols, F32, name)

    def cache(self, wn):
        if wn not in self.cache_done:
            src, dst = self.wext[wn], self.wcache[wn]
            o = self.s.op(
                "pool", lambda e: e.dma_start(dst.ap(), src.ap()),
                reads=[self.dbuf["w_" + wn]], writes=[self.dbuf["bc_" + wn]],
                dma=True, dma_key="bgp")
            o.bg = True
            self.cache_done[wn] = o
        return self.wcache[wn], self.dbuf["bc_" + wn]

    def pb(self, bank, nbanks=1, rows=slice(0, 128)):
        return self.psum[rows, bank * 512:(bank + nbanks) * 512]

    def tap(self, name):
        if name not in self.tap_out:
            return
        src, dst = self.dram[name], self.tap_out[name]
        self.s.op("sp", lambda e: e.dma_start(dst.ap(), src.ap()),
                  reads=[self.dbuf[name]], writes=[Buf()], dma=True)

    def build(self):
        for i, ph in enumerate(self.phases):
            self.pbk = [Buf(f"bank{i}") for i in range(8)]
            for ph2 in self.phases[i + 1:i + 3]:
                for wn in PHASE_WEIGHTS[ph2]:
                    if wn in self.wcache:
                        self.cache(wn)
            PHASE_FNS[ph](self)
            for mk in PHASE_MAKES[ph]:
                if mk in self.taps:
                    self.tap(mk)
            self.s.barrier()
            self.ar.reset()
        emitters = self.s.finalize(self.sems)
        with self.nc.Block("main") as blk:
            blk.tensor(emitters["pe"])
            blk.scalar(emitters["act"])
            blk.vector(emitters["dve"])
            blk.gpsimd(emitters["pool"])
            blk.sync(emitters["sp"])
        return self.nc


def load_weight_f32(P, wn, sbuf, src_slice=None, rows=128, cols=None):
    src = P.wext[wn]
    ap = src.ap() if src_slice is None else src_slice(src)
    cols = cols if cols is not None else ap.shape[-1]
    return P.s.op("sp", lambda e: e.dma_start(
        sbuf.ap(slice(0, rows), slice(0, cols)), ap),
        reads=[P.dbuf["w_" + wn]], writes=[sbuf.buf], dma=True)


def load_const(P, cname, sbuf, rows, cols):
    src = P.ct[cname]
    return P.s.op("sp", lambda e: e.dma_start(
        sbuf.ap(slice(0, rows), slice(0, cols)), src.ap()),
        reads=[P.cbuf[cname]], writes=[sbuf.buf], dma=True)


def ph_ln0(P):
    _ln_common(P, "xT", "xnT", "xnTb", sumsq_to="r1st")


def ph_ln1(P):
    _ln_common(P, "y0T", "ynT", "ynTb")


def _ln_common(P, srcn, dstfn, dstbn, sumsq_to=None):
    s = P.s
    src, srcb = P.dram[srcn], P.dbuf[srcn]
    dstf, dstfb = P.dram[dstfn], P.dbuf[dstfn]
    dstb, dstbb = P.dram[dstbn], P.dbuf[dstbn]

    ones_b = P.ab(16, "ln_ones")
    load_const(P, "ones_bf", ones_b, 128, 1)
    onesf = P.af(144, "ln_onesf")
    for r0 in (0, 32, 64):
        P.s.op("sp", lambda e, r0=r0: e.dma_start(
            onesf.ap(slice(r0, r0 + 1), slice(0, 128)), P.ct["ones_f"].ap()),
            reads=[P.cbuf["ones_f"]], pwrites=[onesf.buf], dma=True)

    xin = [P.af(4096, f"ln_x{i}") for i in range(2)]
    xbf = [P.ab(4096, f"ln_xbf{i}") for i in range(2)]
    xsq = [P.ab(4096, f"ln_xsq{i}") for i in range(2)]
    sum_ps, sq_ps = Buf("sum_ps"), Buf("sq_ps")
    for c in range(NK):
        xb, bb, qb = xin[c % 2], xbf[c % 2], xsq[c % 2]
        s.op("sp", lambda e, c=c, xb=xb: e.dma_start(
            xb.ap(), src[c * 128:(c + 1) * 128, :]),
            reads=[srcb], writes=[xb.buf], dma=True)
        s.op("act", lambda e, xb=xb, bb=bb: e.activation(bb.ap(), xb.ap(), AF.Copy),
             reads=[xb.buf], writes=[bb.buf])
        s.op("act", lambda e, xb=xb, qb=qb: e.activation(qb.ap(), xb.ap(), AF.Square),
             reads=[xb.buf], writes=[qb.buf])

        def mk_mm(row, tile, c=c):
            def f(e):
                last = None
                for tc in range(NTC):
                    last = e.matmul(P.psum[row:row + 1, tc * 512:(tc + 1) * 512],
                                    ones_b.ap(slice(0, 128), slice(0, 1)),
                                    tile.ap(slice(0, 128), slice(tc * 512, (tc + 1) * 512)),
                                    start=(c == 0), stop=(c == NK - 1))
                return last
            return f
        s.op("pe", mk_mm(0, bb), reads=[bb.buf, ones_b.buf],
             writes=[sum_ps] if c == 0 else [], pwrites=[sum_ps] if c > 0 else [])
        s.op("pe", mk_mm(32, qb), reads=[qb.buf, ones_b.buf],
             writes=[sq_ps] if c == 0 else [], pwrites=[sq_ps] if c > 0 else [])

    rows = P.af(4096, "ln_rows")
    rM, rR, rTm = slice(0, 1), slice(32, 33), slice(64, 65)
    s.op("act", lambda e: e.activation(rows.ap(rM), P.psum[0:1, :], AF.Copy, scale=1.0 / C),
         reads=[sum_ps], writes=[rows.buf])
    s.op("act", lambda e: e.activation(rows.ap(rTm), rows.ap(rM), AF.Square),
         reads=[rows.buf], pwrites=[rows.buf])
    s.op("dve", lambda e: e.scalar_tensor_tensor(rows.ap(rR), P.psum[32:33, :], 1.0 / C,
                                                 rows.ap(rTm), OP.mult, OP.subtract),
         reads=[sq_ps, rows.buf], pwrites=[rows.buf])
    s.op("act", lambda e: e.activation(rows.ap(rTm), rows.ap(rR), AF.Ln, bias=1e-5),
         reads=[rows.buf], pwrites=[rows.buf])
    s.op("act", lambda e: e.activation(rows.ap(rR), rows.ap(rTm), AF.Exp, scale=-0.5),
         reads=[rows.buf], pwrites=[rows.buf])

    xt2 = [P.af(512, f"ln_t{i}") for i in range(4)]
    xnf = [P.af(512, f"ln_v{i}") for i in range(4)]
    xnb = [P.ab(512, f"ln_b{i}") for i in range(4)]
    meanB, rstdB = Buf("meanB"), Buf("rstdB")
    if sumsq_to is not None:
        sqb = [P.ab(512, f"ln_s{i}") for i in range(4)]
        r1row = P.af(4096, "ln_r1row")
        ssD, ssB = P.dram[sumsq_to], P.dbuf[sumsq_to]
    for tc in range(NTC):
        cs = slice(tc * 512, (tc + 1) * 512)
        s.op("pe", lambda e, cs=cs: e.matmul(
            P.pb(6), onesf.ap(slice(0, 1), slice(0, 128)), rows.ap(rM, cs),
            start=True, stop=True),
            reads=[rows.buf, onesf.buf], writes=[meanB])
        s.op("pe", lambda e, cs=cs: e.matmul(
            P.pb(7), onesf.ap(slice(32, 33), slice(0, 128)), rows.ap(rR, cs),
            start=True, stop=True),
            reads=[rows.buf, onesf.buf], writes=[rstdB])
        for c in range(NK):
            i = c % 4
            xb, xo, xB = xt2[i], xnf[i], xnb[i]
            s.op("sp", lambda e, c=c, cs=cs, xb=xb: e.dma_start(
                xb.ap(), src[c * 128:(c + 1) * 128, cs]),
                reads=[srcb], writes=[xb.buf], dma=True)
            s.op("dve", lambda e, xb=xb, xo=xo: e.tensor_sub(xo.ap(), xb.ap(), P.pb(6)),
                 reads=[xb.buf, meanB], writes=[xo.buf])
            s.op("dve", lambda e, xo=xo: e.tensor_mul(xo.ap(), xo.ap(), P.pb(7)),
                 reads=[rstdB, xo.buf], pwrites=[xo.buf])
            s.op("act", lambda e, xo=xo, xB=xB: e.activation(xB.ap(), xo.ap(), AF.Copy),
                 reads=[xo.buf], writes=[xB.buf])
            s.op("sp", lambda e, c=c, cs=cs, xo=xo: e.dma_start(
                dstf[c * 128:(c + 1) * 128, cs], xo.ap()),
                reads=[xo.buf], pwrites=[dstfb], dma=True)
            s.op("sp", lambda e, c=c, cs=cs, xB=xB: e.dma_start(
                dstb[c * 128:(c + 1) * 128, cs], xB.ap()),
                reads=[xB.buf], pwrites=[dstbb], dma=True)
            if sumsq_to is not None:
                sb2 = sqb[i]
                s.op("act", lambda e, xo=xo, sb2=sb2: e.activation(
                    sb2.ap(), xo.ap(), AF.Square),
                    reads=[xo.buf], writes=[sb2.buf])
                s.op("pe", lambda e, c=c, sb2=sb2: e.matmul(
                    P.psum[0:1, 2560:3072], ones_b.ap(slice(0, 128), slice(0, 1)),
                    sb2.ap(), start=(c == 0), stop=(c == NK - 1)),
                    reads=[sb2.buf, ones_b.buf],
                    writes=[P.pbk[5]] if c == 0 else [],
                    pwrites=[P.pbk[5]] if c else [])
        if sumsq_to is not None:
            s.op("act", lambda e, cs=cs: e.activation(
                r1row.ap(slice(0, 1), cs), P.psum[0:1, 2560:3072], AF.Copy),
                reads=[P.pbk[5]],
                writes=[r1row.buf] if tc == 0 else [],
                pwrites=[r1row.buf] if tc else [])
    if sumsq_to is not None:
        s.op("sp", lambda e: e.dma_start(ssD.ap(), r1row.ap(slice(0, 1))),
             reads=[r1row.buf], writes=[ssB], dma=True)


def _rms_apply(P, rms, xsb):
    s = P.s
    srcn, wn, eps = rms
    src, srcb = P.dram[srcn], P.dbuf[srcn]
    ssD, ssB = P.dram["r1st"], P.dbuf["r1st"]
    nw = P.af(16, "rms_w")
    load_weight_f32(P, wn, nw, cols=8)
    onesf = P.af(144, "rms_onesf")
    P.s.op("sp", lambda e: e.dma_start(onesf.ap(slice(0, 1), slice(0, 128)),
                                       P.ct["ones_f"].ap()),
           reads=[P.cbuf["ones_f"]], writes=[onesf.buf], dma=True)
    xin = [P.af(512, f"rms_x{i}") for i in range(4)]
    rows = P.af(4096, "rms_rows")
    srow = P.af(4096, "rms_srow")
    r1 = slice(0, 1)
    s.op("sp", lambda e: e.dma_start(srow.ap(r1), ssD.ap()),
         reads=[ssB], writes=[srow.buf], dma=True)
    s.op("act", lambda e: e.activation(srow.ap(r1), srow.ap(r1),
                                       AF.Ln, scale=1.0 / C, bias=eps),
         reads=[srow.buf], pwrites=[srow.buf])
    s.op("act", lambda e: e.activation(rows.ap(r1), srow.ap(r1), AF.Exp, scale=-0.5),
         reads=[srow.buf], writes=[rows.buf])
    for tc in range(NTC):
        cs = slice(tc * 512, (tc + 1) * 512)
        s.op("pe", lambda e, cs=cs: e.matmul(
            P.pb(6), onesf.ap(slice(0, 1), slice(0, 128)), rows.ap(r1, cs),
            start=True, stop=True),
            reads=[rows.buf, onesf.buf], writes=[P.pbk[6]])
        for c in range(NK):
            i = c % 4
            xb, qb = xin[i], xin[(i + 2) % 4]
            s.op("sp", lambda e, c=c, cs=cs, xb=xb: e.dma_start(
                xb.ap(), src[c * 128:(c + 1) * 128, cs]),
                reads=[srcb], writes=[xb.buf], dma=True)
            s.op("dve", lambda e, xb=xb, qb=qb: e.tensor_mul(
                qb.ap(), xb.ap(), P.pb(6)),
                reads=[xb.buf, P.pbk[6]], writes=[qb.buf])
            s.op("act", lambda e, c=c, cs=cs, qb=qb: e.activation(
                xsb[c].ap(None, cs), qb.ap(), AF.Copy,
                scale=nw.ap(None, slice(c, c + 1))),
                reads=[qb.buf, nw.buf],
                writes=[xsb[c].buf] if tc == 0 else [],
                pwrites=[] if tc == 0 else [xsb[c].buf])


def attn_prep(P, cfg):
    s = P.s
    xsb = [P.ab(4096, f"px{c}") for c in range(8)]
    if cfg["rms"] is not None:
        _rms_apply(P, cfg["rms"], xsb)
    else:
        src, sbf = P.dram[cfg["xsrc"]], P.dbuf[cfg["xsrc"]]
        for c in range(8):
            s.op("sp", lambda e, c=c, t=xsb[c]: e.dma_start(
                t.ap(), src[c * 128:(c + 1) * 128, :]),
                reads=[sbf], writes=[xsb[c].buf], dma=True)

    vtd, vtb = P.dram[cfg["vt"]], P.dbuf[cfg["vt"]]
    vC, vCb = P.cache(cfg["v"])
    vw = [P.ab(1024, f"pv{c}") for c in range(8)]
    for c in range(8):
        vwc = vw[c]
        s.op("sp", lambda e, c=c, t=vwc: e.dma_start(
            t.ap(), vC[c * 128:(c + 1) * 128, :]),
            reads=[vCb], writes=[vwc.buf], dma=True)
    onesr = P.ab(144, "p_onesr")
    load_const(P, "ones_row_bf", onesr, 1, 128)
    vbrow = None
    if cfg["vb"] is not None:
        vbrow = P.ab(1024, "p_vbrow")
        vbsrc = P.wext[cfg["vb"]]
        s.op("pool", lambda e: e.dma_start(
            vbrow.ap(slice(0, 1), slice(0, 1024)), vbsrc.ap()),
            reads=[P.dbuf["w_" + cfg["vb"]]], writes=[vbrow.buf], dma=True)
    vob = [P.ab(1024, f"pvo{i}") for i in range(2)]
    for tt in range(32):
        ob = vob[tt % 2]
        banks = ((tt % 2) * 2, (tt % 2) * 2 + 1)

        def mkv(tt=tt, banks=banks):
            def f(e):
                last = None
                for hf in range(2):
                    for c in range(8):
                        last = e.matmul(
                            P.pb(banks[hf]),
                            xsb[c].ap(None, slice(tt * 128, (tt + 1) * 128)),
                            vw[c].ap(None, slice(hf * 512, (hf + 1) * 512)),
                            start=(c == 0), stop=(c == 7 and vbrow is None))
                    if vbrow is not None:
                        last = e.matmul(
                            P.pb(banks[hf]), onesr.ap(slice(0, 1), slice(0, 128)),
                            vbrow.ap(slice(0, 1), slice(hf * 512, (hf + 1) * 512)),
                            start=False, stop=True)
                return last
            return f
        rd = [t.buf for t in xsb] + [t.buf for t in vw] + [onesr.buf]
        if vbrow is not None:
            rd.append(vbrow.buf)
        s.op("pe", mkv(), reads=rd, writes=[P.pbk[banks[0]], P.pbk[banks[1]]])
        for hf in range(2):
            s.op("act", lambda e, b=banks[hf], hf=hf, ob=ob: e.activation(
                ob.ap(None, slice(hf * 512, (hf + 1) * 512)), P.pb(b), AF.Copy),
                reads=[P.pbk[banks[hf]]],
                writes=[ob.buf] if hf == 0 else [], pwrites=[ob.buf] if hf else [])
        s.op("sp", lambda e, tt=tt, ob=ob: e.dma_start(
            vtd[tt * 128:(tt + 1) * 128, :], ob.ap()),
            reads=[ob.buf], pwrites=[vtb], dma=True)

    qkC, qkCb = P.cache(cfg["qk"])
    qkbv = None
    if cfg["qkb"] is not None:
        qkbv = P.af(16, "p_qkb")
        load_weight_f32(P, cfg["qkb"], qkbv, cols=16)
    wq = [P.ab(1024, f"pwq{i}") for i in range(2)]
    if cfg["kind"] == "softmax" and not cfg["rope"]:
        qf = [P.ab(4096, f"pqf{i}") for i in range(2)]
    else:
        qf = [P.af(4096, f"pqf{i}") for i in range(2)]
    qkfd, qkfb = P.dram[cfg["qkf"]], P.dbuf[cfg["qkf"]]
    for h in range(H):
        wqs, qfs = wq[h % 2], qf[h % 2]
        s.op("sp", lambda e, h=h, t=wqs: e.dma_start(
            t.ap().rearrange("p (k o) -> p k o", o=128),
            qkC[h].rearrange("(k p) o -> p k o", p=128)),
            reads=[qkCb], writes=[wqs.buf], dma=True)
        for tc in range(NTC):
            bank = 4 + (tc % 4)

            def mkp(wqs=wqs, tc=tc, bank=bank):
                def f(e):
                    last = None
                    for c in range(8):
                        last = e.matmul(
                            P.pb(bank), wqs.ap(None, slice(c * 128, (c + 1) * 128)),
                            xsb[c].ap(None, slice(tc * 512, (tc + 1) * 512)),
                            start=(c == 0), stop=(c == 7))
                    return last
                return f
            s.op("pe", mkp(), reads=[wqs.buf] + [t.buf for t in xsb],
                 writes=[P.pbk[bank]])
            if cfg["kind"] == "softmax" and not cfg["rope"]:
                s.op("dve", lambda e, tc=tc, bank=bank, qfs=qfs, h=h:
                     e.tensor_scalar(
                         qfs.ap(slice(0, 64), slice(tc * 512, (tc + 1) * 512)),
                         P.psum[0:64, bank * 512:(bank + 1) * 512],
                         qkbv.ap(slice(0, 64), slice(h, h + 1)), 0.125,
                         OP.add, OP.mult),
                     reads=[P.pbk[bank], qkbv.buf],
                     writes=[qfs.buf] if tc == 0 else [],
                     pwrites=[qfs.buf] if tc else [])
                s.op("act", lambda e, tc=tc, bank=bank, qfs=qfs, h=h: e.activation(
                    qfs.ap(slice(64, 128), slice(tc * 512, (tc + 1) * 512)),
                    P.psum[64:128, bank * 512:(bank + 1) * 512], AF.Identity,
                    bias=qkbv.ap(slice(64, 128), slice(h, h + 1))),
                    reads=[P.pbk[bank], qkbv.buf], pwrites=[qfs.buf])
            elif qkbv is not None:
                s.op("dve", lambda e, tc=tc, bank=bank, qfs=qfs, h=h:
                     e.tensor_scalar_add(
                         qfs.ap(None, slice(tc * 512, (tc + 1) * 512)), P.pb(bank),
                         qkbv.ap(None, slice(h, h + 1))),
                     reads=[P.pbk[bank], qkbv.buf],
                     writes=[qfs.buf] if tc == 0 else [],
                     pwrites=[qfs.buf] if tc else [])
            else:
                s.op("act", lambda e, tc=tc, bank=bank, qfs=qfs: e.activation(
                    qfs.ap(None, slice(tc * 512, (tc + 1) * 512)), P.pb(bank), AF.Copy),
                    reads=[P.pbk[bank]],
                    writes=[qfs.buf] if tc == 0 else [],
                    pwrites=[qfs.buf] if tc else [])
        s.op("sp", lambda e, h=h, qfs=qfs: e.dma_start(
            qkfd[h * 128:(h + 1) * 128, :], qfs.ap()),
            reads=[qfs.buf], pwrites=[qkfb], dma=True)


RETH_STAGE = 99
RETH_GN6 = 3


def attn_heads(P, cfg):
    s = P.s
    decay = cfg["kind"] == "decay"
    soft = not decay
    causal = cfg["causal"]
    rope = cfg["rope"]

    onesf = P.af(144, "h_onesf")
    for r0 in (0, 64):
        s.op("sp", lambda e, r0=r0: e.dma_start(
            onesf.ap(slice(r0, r0 + 1), slice(0, 128)), P.ct["ones_f"].ap()),
            reads=[P.cbuf["ones_f"]], pwrites=[onesf.buf], dma=True)
    ones_b = P.ab(16, "h_onesb")
    load_const(P, "ones_bf", ones_b, 128, 1)
    if rope:
        tabC = P.af(4096, "h_tabC")
        tabS = P.af(4096, "h_tabS")
        for tab, top, bot in ((tabC, "cos8", "cosT"), (tabS, "sin8", "sinS")):
            s.op("sp", lambda e, tab=tab, top=top: e.dma_start(
                tab.ap(slice(0, 64), slice(0, T)), P.ct[top].ap()),
                reads=[P.cbuf[top]], pwrites=[tab.buf], dma=True)
            s.op("sp", lambda e, tab=tab, bot=bot: e.dma_start(
                tab.ap(slice(64, 128), slice(0, T)), P.ct[bot].ap()),
                reads=[P.cbuf[bot]], pwrites=[tab.buf], dma=True)
    if decay:
        rf = P.af(560, "h_rf")
        load_const(P, "rf", rf, 128, 560)
        colf = [P.af(512, f"h_cf{i}") for i in range(2)]
        gv = P.af(16, "h_gnw")
        gb = P.af(16, "h_gnb")
        load_weight_f32(P, "gnw", gv, rows=64, cols=16)
        load_weight_f32(P, "gnb", gb, rows=64, cols=16)
    if causal:
        msk = P.ab(2048, "h_masks")
        load_const(P, "masks", msk, 128, 2048)

    QKF = [P.af(4096, f"h_qkf{i}") for i in range(2)]
    SW = [P.af(4096, f"h_sw{i}") for i in range(2)]
    SWb = []
    for i in range(2):
        b = SBuf(P.ar.t, SW[i].c0, 4096, BF16, f"h_swb{i}")
        b.buf = SW[i].buf
        SWb.append(b)
    QA = [P.ab(4096, f"h_qa{i}") for i in range(2)]
    KA = [P.ab(4096, f"h_ka{i}") for i in range(2)]
    npt = 4
    pt = [P.ab(1024 if soft else 512, f"h_pt{i}") for i in range(npt)]
    vst = 65 if soft else 64
    Vh = [P.ab(32 * vst, f"h_vh{i}") for i in range(2)]
    rows = P.af(4096, "h_rows")
    att = P.ab(4096, "h_att")
    rob = [P.ab(512, f"h_rob{i}") for i in range(2)]
    rosq = [P.ab(512, f"h_rosq{i}") for i in range(2)]

    hdD, hdB = P.dram[cfg["hd"]], P.dbuf[cfg["hd"]]
    qkfD, qkfB = P.dram[cfg["qkf"]], P.dbuf[cfg["qkf"]]
    vtD, vtB = P.dram[cfg["vt"]], P.dbuf[cfg["vt"]]

    if soft:
        for i in range(2):
            s.op("dve", lambda e, i=i: e.memset(KA[i].ap(slice(64, 65)), 1.0),
                 pwrites=[KA[i].buf])
            s.op("dve", lambda e, i=i: e.memset(
                P.ar.t[0:128, Vh[i].c0:Vh[i].c0 + 32 * 65].rearrange(
                    "p (kt d) -> p kt d", d=65)[:, :, 64:65], 1.0),
                 pwrites=[Vh[i].buf])

    def prep_early(h):
        p2 = h % 2
        qkf, qa, ka, vh, sw, swb = QKF[p2], QA[p2], KA[p2], Vh[p2], SW[p2], SWb[p2]
        if not (soft and not rope):
            s.op("sp", lambda e, h=h, qkf=qkf: e.dma_start(
                qkf.ap(), qkfD[h * 128:(h + 1) * 128, :]),
                reads=[qkfB], writes=[qkf.buf], dma=True)
        vdst = P.ar.t[0:128, vh.c0:vh.c0 + 32 * vst].rearrange(
            "p (kt d) -> p kt d", d=vst)[:, :, 0:64]
        for part in range(4):
            ks = slice(part * 8, (part + 1) * 8)
            s.op("sp", lambda e, h=h, ks=ks, part=part, vdst=vdst, vh=vh: e.dma_start(
                vdst[:, ks, :],
                vtD[part * 1024:(part + 1) * 1024, h * 64:(h + 1) * 64]
                .rearrange("(kt p) d -> p kt d", p=128)),
                reads=[vtB], pwrites=[vh.buf], dma=True)
        if decay:
            cf = colf[p2]
            s.op("sp", lambda e, h=h, cf=cf: e.dma_start(
                cf.ap(slice(0, 64), slice(0, 512)),
                P.ct["colf"][h * 64:(h + 1) * 64, :]),
                reads=[P.cbuf["colf"]], writes=[cf.buf], dma=True)
        if rope:
            for (d0, s0) in ((0, 32), (32, 0), (64, 96), (96, 64)):
                s.op("sp", lambda e, d0=d0, s0=s0, qkf=qkf, sw=sw: e.dma_start(
                    sw.ap(slice(d0, d0 + 32)), qkf.ap(slice(s0, s0 + 32))),
                    reads=[qkf.buf], pwrites=[sw.buf], dma=True)
            s.op("pool", lambda e, qkf=qkf: e.tensor_mul(qkf.ap(), qkf.ap(), tabC.ap()),
                 reads=[qkf.buf, tabC.buf], pwrites=[qkf.buf])
            s.op("pool", lambda e, sw=sw: e.tensor_mul(sw.ap(), sw.ap(), tabS.ap()),
                 reads=[sw.buf, tabS.buf], pwrites=[sw.buf])
            s.op("pool", lambda e, qkf=qkf, sw=sw: e.tensor_add(
                qkf.ap(), qkf.ap(), sw.ap()),
                reads=[qkf.buf, sw.buf], pwrites=[qkf.buf])
        if soft and not rope:
            s.op("sp", lambda e, h=h, qa=qa: e.dma_start(
                qa.ap(slice(0, 64)), qkfD[h * 128:h * 128 + 64, :]),
                reads=[qkfB], pwrites=[qa.buf], dma=True)
            s.op("sp", lambda e, h=h, ka=ka: e.dma_start(
                ka.ap(slice(0, 64)), qkfD[h * 128 + 64:(h + 1) * 128, :]),
                reads=[qkfB], pwrites=[ka.buf], dma=True)
            return
        if decay:
            cf = colf[p2]
            for tc in range(NTC):
                s.op("pool", lambda e, tc=tc, qa=qa, qkf=qkf, cf=cf: e.tensor_mul(
                    qa.ap(slice(0, 64), slice(tc * 512, (tc + 1) * 512)),
                    qkf.ap(slice(0, 64), slice(tc * 512, (tc + 1) * 512)),
                    cf.ap(slice(0, 64), slice(0, 512))),
                    reads=[qkf.buf, cf.buf],
                    writes=[qa.buf] if tc == 0 else [], pwrites=[qa.buf] if tc else [])
        elif rope:
            s.op("pool", lambda e, qa=qa, qkf=qkf: e.tensor_copy(
                qa.ap(slice(0, 64)), qkf.ap(slice(0, 64))),
                reads=[qkf.buf], writes=[qa.buf])
        else:
            s.op("pool", lambda e, qa=qa, qkf=qkf: e.tensor_scalar_mul(
                qa.ap(slice(0, 64)), qkf.ap(slice(0, 64)), 0.125),
                reads=[qkf.buf], writes=[qa.buf])
        s.op("pool", lambda e, qkf=qkf, swb=swb: e.tensor_copy(
            swb.ap(slice(64, 128)), qkf.ap(slice(64, 128))),
            reads=[qkf.buf], pwrites=[swb.buf])
        s.op("sp", lambda e, ka=ka, swb=swb: e.dma_start(
            ka.ap(slice(0, 64)), swb.ap(slice(64, 128))),
            reads=[swb.buf], pwrites=[ka.buf], dma=True)

    def prep_late(h):
        if not soft:
            return
        p2 = h % 2
        qa, ka, swb = QA[p2], KA[p2], SWb[p2]
        po = p2 * 2048
        r64 = slice(64, 65)
        s.op("act", lambda e, ka=ka, swb=swb: e.activation(
            swb.ap(slice(0, 64)), ka.ap(slice(0, 64)), AF.Square),
            reads=[ka.buf], pwrites=[swb.buf])
        for tc in range(NTC):
            s.op("pe", lambda e, tc=tc, swb=swb: e.matmul(
                P.psum[64:65, 3584:4096], ones_b.ap(slice(0, 64), slice(0, 1)),
                swb.ap(slice(0, 64), slice(tc * 512, (tc + 1) * 512)),
                start=True, stop=True, tile_position=(0, 64)),
                reads=[swb.buf, ones_b.buf], writes=[P.pbk[7]])
            s.op("dve", lambda e, tc=tc: e.reduce_max(
                rows.ap(r64, slice(po + tc, po + tc + 1)),
                P.psum[64:65, 3584:4096], AX),
                reads=[P.pbk[7]], pwrites=[rows.buf])
        s.op("dve", lambda e: e.reduce_max(
            rows.ap(r64, slice(po + 8, po + 9)), rows.ap(r64, slice(po, po + 8)), AX),
            reads=[rows.buf], pwrites=[rows.buf])
        s.op("act", lambda e: e.activation(
            rows.ap(r64, slice(po + 9, po + 10)), rows.ap(r64, slice(po + 8, po + 9)),
            AF.Ln),
            reads=[rows.buf], pwrites=[rows.buf])
        s.op("act", lambda e, qa=qa, swb=swb: e.activation(
            swb.ap(slice(0, 64)), qa.ap(slice(0, 64)), AF.Square),
            reads=[qa.buf], pwrites=[swb.buf])
        for tc in range(NTC):
            s.op("pe", lambda e, tc=tc, swb=swb: e.matmul(
                P.psum[64:65, 3584:4096], ones_b.ap(slice(0, 64), slice(0, 1)),
                swb.ap(slice(0, 64), slice(tc * 512, (tc + 1) * 512)),
                start=True, stop=True, tile_position=(0, 64)),
                reads=[swb.buf, ones_b.buf], writes=[P.pbk[7]])
            s.op("act", lambda e, tc=tc: e.activation(
                rows.ap(r64, slice(po + 512, po + 1024)),
                P.psum[64:65, 3584:4096], AF.Ln),
                reads=[P.pbk[7]], pwrites=[rows.buf])
            s.op("dve", lambda e, tc=tc: e.tensor_scalar_add(
                rows.ap(r64, slice(po + 1024, po + 1536)),
                rows.ap(r64, slice(po + 512, po + 1024)),
                rows.ap(r64, slice(po + 9, po + 10))),
                reads=[rows.buf], pwrites=[rows.buf])
            s.op("act", lambda e, tc=tc: e.activation(
                rows.ap(r64, slice(po + 512, po + 1024)),
                rows.ap(r64, slice(po + 1024, po + 1536)), AF.Exp, scale=0.5),
                reads=[rows.buf], pwrites=[rows.buf])
            s.op("dve", lambda e, tc=tc, qa=qa: e.tensor_scalar_mul(
                qa.ap(r64, slice(tc * 512, (tc + 1) * 512)),
                rows.ap(r64, slice(po + 512, po + 1024)), -1.0),
                reads=[rows.buf], pwrites=[qa.buf])

    prep_early(0)
    prep_late(0)

    for h in range(H):
        p2 = h % 2
        qkf, qa, ka, vh = QKF[p2], QA[p2], KA[p2], Vh[p2]
        sw, swb = SW[p2], SWb[p2]
        po = p2 * 2048

        pend = []

        def flush_pend(pend=pend):
            while pend:
                pend.pop(0)()

        if soft:
            tiles = []
            for qc in range(NTC):
                ng = (2 * qc + 2) if causal else 16
                for g in range(ng):
                    tiles.append((qc, g, ng))

            def em_qk(i, qa=qa, ka=ka):
                qc, g, ng = tiles[i]
                b0 = (i % 3) * 2
                qs = slice(qc * 512, (qc + 1) * 512)

                def mmst(e, g=g, b0=b0, qs=qs):
                    last = None
                    for j in range(2):
                        kt = 2 * g + j
                        last = e.matmul(
                            P.pb(b0 + j),
                            ka.ap(slice(0, 65), slice(kt * 128, (kt + 1) * 128)),
                            qa.ap(slice(0, 65), qs),
                            start=True, stop=True)
                    return last
                s.op("pe", mmst, reads=[qa.buf, ka.buf],
                     writes=[P.pbk[b0], P.pbk[b0 + 1]])

            def em_exp(i):
                qc, g, ng = tiles[i]
                b0 = (i % 3) * 2
                ptg = pt[i % 3]
                s.op("act", lambda e, b0=b0, ptg=ptg: e.activation(
                    ptg.ap(), P.pb(b0, 2), AF.Exp),
                    reads=[P.pbk[b0], P.pbk[b0 + 1]], writes=[ptg.buf])
                if causal:
                    for j in range(2):
                        kt = 2 * g + j
                        if kt >= 4 * qc:
                            mi = 3 - (kt - 4 * qc)
                            s.op("dve", lambda e, j=j, mi=mi, ptg=ptg: e.tensor_mul(
                                ptg.ap(None, slice(j * 512, (j + 1) * 512)),
                                ptg.ap(None, slice(j * 512, (j + 1) * 512)),
                                msk.ap(None, slice(mi * 512, (mi + 1) * 512))),
                                reads=[ptg.buf, msk.buf], pwrites=[ptg.buf])

            def em_pv(i, vh=vh):
                qc, g, ng = tiles[i]
                ptg = pt[i % 3]

                def mmpv(e, g=g, ng=ng, ptg=ptg):
                    last = None
                    for j in range(2):
                        kt = 2 * g + j
                        last = e.matmul(
                            P.psum[0:65, 3072:3584],
                            vh.ap(None, slice(kt * 65, kt * 65 + 65)),
                            ptg.ap(None, slice(j * 512, (j + 1) * 512)),
                            start=(kt == 0), stop=(kt == 2 * ng - 1))
                    return last
                s.op("pe", mmpv, reads=[ptg.buf, vh.buf],
                     writes=[P.pbk[6]] if g == 0 else [],
                     pwrites=[P.pbk[6]] if g else [])

            def mk_epi(qc, sw=sw, po=po):
                qs = slice(qc * 512, (qc + 1) * 512)
                s.op("act", lambda e, qs=qs, sw=sw: e.activation(
                    sw.ap(slice(0, 65), qs), P.psum[0:65, 3072:3584], AF.Copy),
                    reads=[P.pbk[6]], pwrites=[sw.buf])

                def rest(qc=qc, qs=qs, sw=sw, po=po):
                    r64 = slice(64, 65)
                    s.op("act", lambda e, qs=qs, sw=sw, po=po: e.activation(
                        rows.ap(r64, slice(po + 1536, po + 2048)),
                        sw.ap(slice(64, 65), qs), AF.Ln),
                        reads=[sw.buf], pwrites=[rows.buf])
                    s.op("act", lambda e, po=po: e.activation(
                        rows.ap(r64, slice(po + 1024, po + 1536)),
                        rows.ap(r64, slice(po + 1536, po + 2048)), AF.Exp,
                        scale=-1.0),
                        reads=[rows.buf], pwrites=[rows.buf])
                    s.op("pe", lambda e, po=po: e.matmul(
                        P.pb(7, rows=slice(0, 64)),
                        onesf.ap(slice(64, 65), slice(0, 64)),
                        rows.ap(r64, slice(po + 1024, po + 1536)),
                        start=True, stop=True),
                        reads=[rows.buf, onesf.buf], writes=[P.pbk[7]])
                    s.op("dve", lambda e, qs=qs, sw=sw: e.tensor_mul(
                        att.ap(slice(0, 64), qs), sw.ap(slice(0, 64), qs),
                        P.pb(7, rows=slice(0, 64))),
                        reads=[sw.buf, P.pbk[7]],
                        writes=[att.buf] if qc == 0 else [],
                        pwrites=[att.buf] if qc else [])
                pend.append(rest)

            LA = 3
            for j in range(min(LA, len(tiles))):
                em_qk(j)
            for i, (qc, g, ng) in enumerate(tiles):
                em_exp(i)
                if i + LA < len(tiles):
                    em_qk(i + LA)
                em_pv(i)
                if g == min(1, ng - 1) and pend:
                    flush_pend()
                if g == ng - 1:
                    mk_epi(qc)
                if i == max(0, len(tiles) // 8) and h + 1 < H:
                    prep_early(h + 1)
                if i == (len(tiles) * 3) // 4 and h + 1 < H:
                    prep_late(h + 1)
        else:
            tiles = []
            for qc in range(NTC):
                nkt = 4 * qc + 4
                for kt in range(nkt):
                    tiles.append((qc, kt, nkt))

            def em_st(i, qa=qa, ka=ka):
                qc, kt, nkt = tiles[i]
                bank = i % 4
                qs = slice(qc * 512, (qc + 1) * 512)
                s.op("pe", lambda e, kt=kt, bank=bank, qs=qs: e.matmul(
                    P.pb(bank), ka.ap(slice(0, 64), slice(kt * 128, (kt + 1) * 128)),
                    qa.ap(slice(0, 64), qs), start=True, stop=True),
                    reads=[qa.buf, ka.buf], writes=[P.pbk[bank]])

            def em_evac(i, h=h):
                qc, kt, nkt = tiles[i]
                bank = i % 4
                ptg = pt[i % 4]
                d0i = (qc * 512 - kt * 128 + 384) // 128
                rfap = rf.ap(None, slice(h * 35 + d0i, h * 35 + d0i + 1))
                if kt >= 4 * qc:
                    mi = 3 - (kt - 4 * qc)
                    s.op("dve", lambda e, bank=bank, ptg=ptg, rfap=rfap, mi=mi:
                         e.scalar_tensor_tensor(
                             ptg.ap(), P.pb(bank), rfap,
                             msk.ap(None, slice(mi * 512, (mi + 1) * 512)),
                             OP.mult, OP.mult),
                         reads=[P.pbk[bank], rf.buf, msk.buf], writes=[ptg.buf])
                elif kt % 2 == 0:
                    s.op("act", lambda e, bank=bank, ptg=ptg, rfap=rfap:
                         e.activation(ptg.ap(), P.pb(bank), AF.Copy, scale=rfap),
                         reads=[P.pbk[bank], rf.buf], writes=[ptg.buf])
                else:
                    s.op("dve", lambda e, bank=bank, ptg=ptg, rfap=rfap:
                         e.tensor_scalar_mul(ptg.ap(), P.pb(bank), rfap),
                         reads=[P.pbk[bank], rf.buf], writes=[ptg.buf])

            def em_pvd(i, vh=vh):
                qc, kt, nkt = tiles[i]
                ptg = pt[i % 4]
                s.op("pe", lambda e, kt=kt, nkt=nkt, vh=vh, ptg=ptg: e.matmul(
                    P.psum[0:64, 3072:3584],
                    vh.ap(None, slice(kt * 64, (kt + 1) * 64)),
                    ptg.ap(), start=(kt == 0), stop=(kt == nkt - 1)),
                    reads=[ptg.buf, vh.buf],
                    writes=[P.pbk[6]] if kt == 0 else [],
                    pwrites=[P.pbk[6]] if kt else [])

            def mk_epi(qc, h=h, sw=sw):
                qs = slice(qc * 512, (qc + 1) * 512)
                robs, rosqs = rob[qc % 2], rosq[qc % 2]
                s.op("act", lambda e, qs=qs, sw=sw: e.activation(
                    sw.ap(slice(0, 64), qs), P.psum[0:64, 3072:3584], AF.Copy),
                    reads=[P.pbk[6]], pwrites=[sw.buf])

                def rest(qc=qc, qs=qs, h=h, sw=sw, robs=robs, rosqs=rosqs):
                    r1 = slice(0, 1)
                    s.op("dve", lambda e, qs=qs, robs=robs, sw=sw: e.tensor_copy(
                        robs.ap(slice(0, 64)), sw.ap(slice(0, 64), qs)),
                        reads=[sw.buf], writes=[robs.buf])
                    s.op("act", lambda e, qs=qs, rosqs=rosqs, sw=sw: e.activation(
                        rosqs.ap(slice(0, 64)), sw.ap(slice(0, 64), qs), AF.Square),
                        reads=[sw.buf], writes=[rosqs.buf])
                    s.op("pe", lambda e, robs=robs: e.matmul(
                        P.psum[0:1, 3584:4096], ones_b.ap(slice(0, 64), slice(0, 1)),
                        robs.ap(slice(0, 64)), start=True, stop=True),
                        reads=[robs.buf, ones_b.buf], writes=[P.pbk[7]])
                    s.op("pe", lambda e, rosqs=rosqs: e.matmul(
                        P.psum[0:1, 2560:3072], ones_b.ap(slice(0, 64), slice(0, 1)),
                        rosqs.ap(slice(0, 64)), start=True, stop=True),
                        reads=[rosqs.buf, ones_b.buf], writes=[P.pbk[5]])
                    s.op("act", lambda e: e.activation(
                        rows.ap(r1, slice(0, 512)), P.psum[0:1, 3584:4096],
                        AF.Copy, scale=1.0 / 64),
                        reads=[P.pbk[7]], pwrites=[rows.buf])
                    s.op("act", lambda e: e.activation(
                        rows.ap(r1, slice(512, 1024)), rows.ap(r1, slice(0, 512)),
                        AF.Square),
                        reads=[rows.buf], pwrites=[rows.buf])
                    s.op("dve", lambda e: e.scalar_tensor_tensor(
                        rows.ap(r1, slice(1024, 1536)), P.psum[0:1, 2560:3072],
                        1.0 / 64, rows.ap(r1, slice(512, 1024)),
                        OP.mult, OP.subtract),
                        reads=[P.pbk[5], rows.buf], pwrites=[rows.buf])
                    s.op("act", lambda e: e.activation(
                        rows.ap(r1, slice(1536, 2048)), rows.ap(r1, slice(1024, 1536)),
                        AF.Ln, bias=1e-5),
                        reads=[rows.buf], pwrites=[rows.buf])
                    s.op("act", lambda e: e.activation(
                        rows.ap(r1, slice(2048, 2560)), rows.ap(r1, slice(1536, 2048)),
                        AF.Exp, scale=-0.5),
                        reads=[rows.buf], pwrites=[rows.buf])
                    s.op("pe", lambda e: e.matmul(
                        P.pb(4, rows=slice(0, 64)), onesf.ap(slice(0, 1), slice(0, 64)),
                        rows.ap(r1, slice(0, 512)), start=True, stop=True),
                        reads=[rows.buf, onesf.buf], writes=[P.pbk[4]])
                    s.op("pe", lambda e: e.matmul(
                        P.pb(5, rows=slice(0, 64)), onesf.ap(slice(0, 1), slice(0, 64)),
                        rows.ap(r1, slice(2048, 2560)), start=True, stop=True),
                        reads=[rows.buf, onesf.buf], writes=[P.pbk[5]])
                    s.op("dve", lambda e, qs=qs, sw=sw: e.tensor_sub(
                        sw.ap(slice(0, 64), qs), sw.ap(slice(0, 64), qs),
                        P.pb(4, rows=slice(0, 64))),
                        reads=[sw.buf, P.pbk[4]], pwrites=[sw.buf])
                    s.op("dve", lambda e, qs=qs, sw=sw: e.tensor_mul(
                        sw.ap(slice(0, 64), qs), sw.ap(slice(0, 64), qs),
                        P.pb(5, rows=slice(0, 64))),
                        reads=[sw.buf, P.pbk[5]], pwrites=[sw.buf])
                    s.op("dve", lambda e, qs=qs, h=h, sw=sw: e.tensor_scalar(
                        att.ap(slice(0, 64), qs), sw.ap(slice(0, 64), qs),
                        gv.ap(slice(0, 64), slice(h, h + 1)),
                        gb.ap(slice(0, 64), slice(h, h + 1)), OP.mult, OP.add),
                        reads=[sw.buf, gv.buf, gb.buf],
                        writes=[att.buf] if qc == 0 else [],
                        pwrites=[att.buf] if qc else [])
                pend.append(rest)

            LA = 4
            for j in range(min(LA, len(tiles))):
                em_st(j)
            for i, (qc, kt, nkt) in enumerate(tiles):
                em_evac(i)
                if i + LA < len(tiles):
                    em_st(i + LA)
                em_pvd(i)
                if kt == min(1, nkt - 1) and pend:
                    flush_pend()
                if kt == nkt - 1:
                    mk_epi(qc)
                if i == max(0, len(tiles) // 8) and h + 1 < H:
                    prep_early(h + 1)
                if i == (len(tiles) * 3) // 4 and h + 1 < H:
                    prep_late(h + 1)
        flush_pend()
        s.op("sp", lambda e, h=h: e.dma_start(
            hdD[h * 64:(h + 1) * 64, :], att.ap(slice(0, 64))),
            reads=[att.buf], pwrites=[hdB], dma=True)


def attn_out(P, cfg):
    s = P.s
    ostats = cfg.get("out_stats", False)
    hdD, hdB = P.dram[cfg["hd"]], P.dbuf[cfg["hd"]]
    resD, resB = P.dram[cfg["res"]], P.dbuf[cfg["res"]]
    dstD, dstB = P.dram[cfg["dst"]], P.dbuf[cfg["dst"]]
    wC, wCb = P.cache(cfg["ow"])
    hd = [P.ab(4096, f"o_hd{c}") for c in range(8)]
    ww = [P.ab(1024, f"o_w{c}") for c in range(8)]
    for c in range(8):
        hdc, wwc = hd[c], ww[c]
        s.op("sp", lambda e, c=c, t=hdc: e.dma_start(
            t.ap(), hdD[c * 128:(c + 1) * 128, :]),
            reads=[hdB], writes=[hdc.buf], dma=True)
        s.op("sp", lambda e, c=c, t=wwc: e.dma_start(
            t.ap(), wC[c * 128:(c + 1) * 128, :]),
            reads=[wCb], writes=[wwc.buf], dma=True)
    obv = None
    if cfg["ob"] is not None:
        obv = P.af(16, "o_b")
        load_weight_f32(P, cfg["ob"], obv, cols=8)
    xr = [P.af(4096, f"o_x{i}") for i in range(2)]
    ou = [P.af(4096, f"o_o{i}") for i in range(2)]
    if ostats:
        sqb = [P.ab(512, f"o_s{i}") for i in range(2)]
        ones_b = P.ab(16, "o_onesb")
        load_const(P, "ones_bf", ones_b, 128, 1)
    for o in range(8):
        xrs, ous = xr[o % 2], ou[o % 2]
        s.op("sp", lambda e, o=o, xrs=xrs: e.dma_start(
            xrs.ap(), resD[o * 128:(o + 1) * 128, :]),
            reads=[resB], writes=[xrs.buf], dma=True)
        for tc in range(NTC):
            bank = tc % 4

            def mko(o=o, tc=tc, bank=bank):
                def f(e):
                    last = None
                    for c in range(8):
                        last = e.matmul(
                            P.pb(bank), ww[c].ap(None, slice(o * 128, (o + 1) * 128)),
                            hd[c].ap(None, slice(tc * 512, (tc + 1) * 512)),
                            start=(c == 0), stop=(c == 7))
                    return last
                return f
            s.op("pe", mko(), reads=[t.buf for t in ww] + [t.buf for t in hd],
                 writes=[P.pbk[bank]])
            bias = obv.ap(None, slice(o, o + 1)) if obv is not None else 0.0
            s.op("dve", lambda e, tc=tc, bank=bank, ous=ous, xrs=xrs, bias=bias:
                 e.scalar_tensor_tensor(
                     ous.ap(None, slice(tc * 512, (tc + 1) * 512)), P.pb(bank), bias,
                     xrs.ap(None, slice(tc * 512, (tc + 1) * 512)), OP.add, OP.add),
                 reads=[P.pbk[bank], xrs.buf] + ([obv.buf] if obv is not None else []),
                 writes=[ous.buf] if tc == 0 else [], pwrites=[ous.buf] if tc else [])
            if ostats:
                sb2 = sqb[tc % 2]
                sr = 32 * (tc // 4)
                sbk = 4 + (tc % 4)
                s.op("act", lambda e, tc=tc, ous=ous, sb2=sb2: e.activation(
                    sb2.ap(), ous.ap(None, slice(tc * 512, (tc + 1) * 512)), AF.Square),
                    reads=[ous.buf], writes=[sb2.buf])
                s.op("pe", lambda e, o=o, sb2=sb2, sr=sr, sbk=sbk: e.matmul(
                    P.psum[sr:sr + 1, sbk * 512:(sbk + 1) * 512],
                    ones_b.ap(slice(0, 128), slice(0, 1)), sb2.ap(),
                    start=(o == 0), stop=(o == 7), tile_position=(0, sr)),
                    reads=[sb2.buf, ones_b.buf],
                    writes=[P.pbk[sbk]] if o == 0 and tc < 4 else [],
                    pwrites=[P.pbk[sbk]] if not (o == 0 and tc < 4) else [])
        s.op("sp", lambda e, o=o, ous=ous: e.dma_start(
            dstD[o * 128:(o + 1) * 128, :], ous.ap()),
            reads=[ous.buf], pwrites=[dstB], dma=True)


def ph_moe(P):
    s = P.s
    x1D, x1B = P.dram["x1T"], P.dbuf["x1T"]
    paD, paB = P.dram["partialT"], P.dbuf["partialT"]
    y0D, y0B = P.dram["y0T"], P.dbuf["y0T"]
    gC, gCb = P.cache("gateT")
    w1C, w1Cb = P.cache("moe_w1")
    w2C, w2Cb = P.cache("moe_w2")
    w3C, w3Cb = P.cache("moe_w3")
    b3C, b3Cb = P.cache("moe_b3")

    n2 = P.af(16, "m_n2")
    load_weight_f32(P, "norm2w", n2, cols=8)
    b1v = P.af(144, "m_b1")
    b2v = P.af(144, "m_b2")
    for vec, wn in ((b1v, "moe_b1"), (b2v, "moe_b2")):
        src = P.wext[wn]
        s.op("sp", lambda e, vec=vec, src=src: e.dma_start(
            vec.ap(None, slice(0, 128)).rearrange("p (e k) -> p e k", k=16),
            src.rearrange("e p k -> p e k")),
            reads=[P.dbuf["w_" + wn]], writes=[vec.buf], dma=True)
    onesf = P.af(144, "m_onesf")
    for r0 in (0, 32):
        s.op("sp", lambda e, r0=r0: e.dma_start(
            onesf.ap(slice(r0, r0 + 1), slice(0, 128)), P.ct["ones_f"].ap()),
            reads=[P.cbuf["ones_f"]], pwrites=[onesf.buf], dma=True)
    ones_b = P.ab(16, "m_onesb")
    load_const(P, "ones_bf", ones_b, 128, 1)
    ident = P.ab(128, "m_ident")
    load_const(P, "ident", ident, 128, 128)
    sel8 = P.ab(1024, "m_sel8")
    load_const(P, "sel8", sel8, 8, 1024)
    gsb = P.ab(64, "m_gate")
    s.op("sp", lambda e: e.dma_start(
        gsb.ap(None, slice(0, 64)).rearrange("p (c e) -> p c e", e=8),
        gC.rearrange("(c p) e -> p c e", p=128)),
        reads=[gCb], writes=[gsb.buf], dma=True)
    b3sb = P.ab(1024, "m_b3")
    s.op("sp", lambda e: e.dma_start(b3sb.ap(slice(0, 8)), b3C.ap()),
         reads=[b3Cb], writes=[b3sb.buf], dma=True)

    xin = [P.af(512, f"m_x{i}") for i in range(4)]
    sq = [P.ab(512, f"m_q{i}") for i in range(4)]
    r2 = [P.ab(512, f"m_r2{c}") for c in range(8)]
    acc = [P.af(512, f"m_ac{o}") for o in range(8)]
    lg = P.af(16, "m_lg")
    tp1 = P.af(16, "m_t1")
    tp2 = P.af(16, "m_t2")
    cmb = P.ab(16, "m_cmb")
    combT = P.ab(512, "m_combT")
    combB = P.ab(512, "m_combB")
    w1sb = [P.ab(16384, f"m_w1{i}") for i in range(2)]
    w2sb = [P.ab(16384, "m_w20")]
    w3r = [P.ab(1024, f"m_w3{i}") for i in range(2)]
    usil = P.af(512, "m_us")
    hsb = P.ab(8192, "m_h")
    t1s = P.af(512, "m_t1s")
    yo = [P.af(512, f"m_yo{i}") for i in range(2)]
    rows2 = P.af(4096, "m_rows")
    srow = SBuf(P.ar.t, hsb.c0, 4096, F32, "m_srow")
    srow.buf = hsb.buf

    assert MODE == "fused"
    r1 = slice(0, 1)
    for ch in range(8):
        r = 32 * (ch // 4)
        bank = 4 + (ch % 4)
        seg = slice(ch * 512, (ch + 1) * 512)
        s.op("act", lambda e, r=r, bank=bank, seg=seg: e.activation(
            srow.ap(slice(r, r + 1), seg),
            P.psum[r:r + 1, bank * 512:(bank + 1) * 512],
            AF.Ln, scale=1.0 / C, bias=1e-6),
            reads=[P.pbk[bank]],
            writes=[srow.buf] if ch == 0 else [], pwrites=[srow.buf] if ch else [])
        s.op("act", lambda e, r=r, seg=seg: e.activation(
            rows2.ap(slice(r, r + 1), seg), srow.ap(slice(r, r + 1), seg),
            AF.Exp, scale=-0.5),
            reads=[srow.buf],
            writes=[rows2.buf] if ch == 0 else [], pwrites=[rows2.buf] if ch else [])

    def ldw(slot, cache, cb, e):
        for c in range(8):
            s.op("sp", lambda e2, c=c, slot=slot, cache=cache, e=e: e2.dma_start(
                slot.ap(None, slice(c * 2048, (c + 1) * 2048)),
                cache[e, c * 128:(c + 1) * 128, :]),
                reads=[cb], writes=[slot.buf] if c == 0 else [],
                pwrites=[slot.buf] if c else [], dma=True)

    for ch in range(8):
        ts = slice(ch * 512, (ch + 1) * 512)
        rr = 32 * (ch // 4)
        s.op("pe", lambda e, ts=ts, rr=rr: e.matmul(
            P.pb(6), onesf.ap(slice(rr, rr + 1), slice(0, 128)),
            rows2.ap(slice(rr, rr + 1), ts), start=True, stop=True),
            reads=[rows2.buf, onesf.buf], writes=[P.pbk[6]])
        for c in range(8):
            xb, xb2 = xin[c % 2], xin[2 + c % 2]
            s.op("sp", lambda e, c=c, ts=ts, xb=xb: e.dma_start(
                xb.ap(), x1D[c * 128:(c + 1) * 128, ts]),
                reads=[x1B], writes=[xb.buf], dma=True)
            s.op("dve", lambda e, xb=xb, xb2=xb2: e.tensor_mul(
                xb2.ap(), xb.ap(), P.pb(6)),
                reads=[xb.buf, P.pbk[6]], writes=[xb2.buf])
            s.op("act", lambda e, c=c, xb2=xb2: e.activation(
                r2[c].ap(), xb2.ap(), AF.Copy,
                scale=n2.ap(None, slice(c, c + 1))),
                reads=[xb2.buf, n2.buf], writes=[r2[c].buf])
        for tt in range(4):
            def gmm(e, tt=tt):
                last = None
                for c in range(8):
                    last = e.matmul(P.psum[0:128, 3584:3592],
                                    r2[c].ap(None, slice(tt * 128, (tt + 1) * 128)),
                                    gsb.ap(None, slice(c * 8, (c + 1) * 8)),
                                    start=(c == 0), stop=(c == 7))
                return last
            s.op("pe", gmm, reads=[t.buf for t in r2] + [gsb.buf], writes=[P.pbk[7]])
            s.op("act", lambda e: e.activation(lg.ap(None, slice(0, 8)),
                                               P.psum[0:128, 3584:3592], AF.Copy),
                 reads=[P.pbk[7]], writes=[lg.buf])
            s.op("dve", lambda e: e.reduce_max(tp1.ap(None, slice(0, 1)),
                                               lg.ap(None, slice(0, 8)), AX),
                 reads=[lg.buf], writes=[tp1.buf])
            s.op("dve", lambda e: e.tensor_scalar(
                tp2.ap(None, slice(0, 8)), lg.ap(None, slice(0, 8)),
                tp1.ap(None, slice(0, 1)), -1e30, OP.is_equal, OP.mult),
                reads=[lg.buf, tp1.buf], writes=[tp2.buf])
            s.op("dve", lambda e: e.tensor_add(
                tp2.ap(None, slice(8, 16)), tp2.ap(None, slice(0, 8)),
                lg.ap(None, slice(0, 8))),
                reads=[tp2.buf, lg.buf], pwrites=[tp2.buf])
            s.op("dve", lambda e: e.reduce_max(tp1.ap(None, slice(1, 2)),
                                               tp2.ap(None, slice(8, 16)), AX),
                 reads=[tp2.buf], pwrites=[tp1.buf])
            s.op("dve", lambda e: e.tensor_scalar_mul(
                tp1.ap(None, slice(2, 3)), tp1.ap(None, slice(0, 1)), -1.0),
                reads=[tp1.buf], pwrites=[tp1.buf])
            s.op("act", lambda e: e.activation(
                tp1.ap(None, slice(3, 4)), tp1.ap(None, slice(1, 2)), AF.Exp,
                bias=tp1.ap(None, slice(2, 3))),
                reads=[tp1.buf], pwrites=[tp1.buf])
            s.op("dve", lambda e: e.tensor_scalar_add(
                tp1.ap(None, slice(4, 5)), tp1.ap(None, slice(3, 4)), 1.0),
                reads=[tp1.buf], pwrites=[tp1.buf])
            s.op("dve", lambda e: e.reciprocal(
                tp1.ap(None, slice(5, 6)), tp1.ap(None, slice(4, 5))),
                reads=[tp1.buf], pwrites=[tp1.buf])
            s.op("act", lambda e: e.activation(
                tp2.ap(None, slice(0, 8)), lg.ap(None, slice(0, 8)), AF.Exp,
                bias=tp1.ap(None, slice(2, 3))),
                reads=[lg.buf, tp1.buf], pwrites=[tp2.buf])
            s.op("dve", lambda e: e.tensor_scalar(
                tp2.ap(None, slice(8, 16)), lg.ap(None, slice(0, 8)),
                tp1.ap(None, slice(1, 2)), None, OP.is_ge),
                reads=[lg.buf, tp1.buf], pwrites=[tp2.buf])
            s.op("dve", lambda e: e.tensor_mul(
                tp2.ap(None, slice(0, 8)), tp2.ap(None, slice(0, 8)),
                tp2.ap(None, slice(8, 16))),
                reads=[tp2.buf], pwrites=[tp2.buf])
            s.op("dve", lambda e: e.tensor_scalar_mul(
                cmb.ap(None, slice(0, 8)), tp2.ap(None, slice(0, 8)),
                tp1.ap(None, slice(5, 6))),
                reads=[tp2.buf, tp1.buf], writes=[cmb.buf])
            s.op("pe", lambda e: e.transpose(
                P.psum[:, 1536:2048].bitcast(BF16)[0:8, 0:128],
                cmb.ap(None, slice(0, 8)), ident.ap()),
                reads=[cmb.buf, ident.buf], writes=[P.pbk[3]])
            s.op("act", lambda e, tt=tt: e.activation(
                combT.ap(slice(0, 8), slice(tt * 128, (tt + 1) * 128)),
                P.psum[:, 1536:2048].bitcast(BF16)[0:8, 0:128], AF.Copy),
                reads=[P.pbk[3]],
                writes=[combT.buf] if tt == 0 else [],
                pwrites=[combT.buf] if tt else [])
        for o in range(8):
            bank = o % 4
            s.op("pe", lambda e, o=o, bank=bank: e.matmul(
                P.pb(bank), b3sb.ap(slice(0, 8), slice(o * 128, (o + 1) * 128)),
                combT.ap(slice(0, 8)), start=True, stop=True),
                reads=[b3sb.buf, combT.buf], writes=[P.pbk[bank]])
            s.op("act", lambda e, o=o, bank=bank: e.activation(
                acc[o].ap(), P.pb(bank), AF.Copy),
                reads=[P.pbk[bank]], writes=[acc[o].buf])
        for ex in range(8):
            w1s, w2s = w1sb[ex % 2], w2sb[0]
            ldw(w1s, w1C, w1Cb, ex)
            ldw(w2s, w2C, w2Cb, ex)
            s.op("pe", lambda e, ex=ex: e.matmul(
                P.pb(4),
                sel8.ap(slice(0, 8), slice(ex * 128, (ex + 1) * 128)),
                combT.ap(slice(0, 8)), start=True, stop=True),
                reads=[sel8.buf, combT.buf], writes=[P.pbk[4]])
            s.op("dve", lambda e: e.tensor_copy(
                combB.ap(), P.pb(4)),
                reads=[P.pbk[4]], writes=[combB.buf])
            for ht in range(16):
                a = (ht % 2) * 2

                def umm(e, ht=ht, a=a, ws=w1s):
                    last = None
                    for c in range(8):
                        last = e.matmul(
                            P.pb(a),
                            ws.ap(None, slice(c * 2048 + ht * 128,
                                              c * 2048 + (ht + 1) * 128)),
                            r2[c].ap(), start=(c == 0), stop=(c == 7))
                    return last

                def umm2(e, ht=ht, a=a, ws=w2s):
                    last = None
                    for c in range(8):
                        last = e.matmul(
                            P.pb(a + 1),
                            ws.ap(None, slice(c * 2048 + ht * 128,
                                              c * 2048 + (ht + 1) * 128)),
                            r2[c].ap(), start=(c == 0), stop=(c == 7))
                    return last
                s.op("pe", umm, reads=[w1s.buf] + [t.buf for t in r2],
                     writes=[P.pbk[a]])
                s.op("pe", umm2, reads=[w2s.buf] + [t.buf for t in r2],
                     writes=[P.pbk[a + 1]])
                s.op("act", lambda e, ht=ht, a=a, ex=ex: e.activation(
                    usil.ap(), P.pb(a), AF.Silu,
                    bias=b1v.ap(None, slice(ex * 16 + ht, ex * 16 + ht + 1))),
                    reads=[P.pbk[a], b1v.buf], writes=[usil.buf])
                s.op("dve", lambda e, ht=ht, a=a, ex=ex: e.scalar_tensor_tensor(
                    hsb.ap(None, slice(ht * 512, (ht + 1) * 512)), P.pb(a + 1),
                    b2v.ap(None, slice(ex * 16 + ht, ex * 16 + ht + 1)),
                    usil.ap(), OP.add, OP.mult),
                    reads=[P.pbk[a + 1], b2v.buf, usil.buf],
                    writes=[hsb.buf] if ht == 0 else [],
                    pwrites=[hsb.buf] if ht else [])
            for ht in range(16):
                w3s = w3r[ht % 2]
                s.op("sp", lambda e, ht=ht, ex=ex, w3s=w3s: e.dma_start(
                    w3s.ap(), w3C[ex, ht * 128:(ht + 1) * 128, :]),
                    reads=[w3Cb], writes=[w3s.buf], dma=True)

                def w3mm(e, ht=ht, w3s=w3s):
                    last = None
                    for o in range(8):
                        last = e.matmul(
                            P.pb(o), w3s.ap(None, slice(o * 128, (o + 1) * 128)),
                            hsb.ap(None, slice(ht * 512, (ht + 1) * 512)),
                            start=(ht == 0), stop=(ht == 15))
                    return last
                s.op("pe", w3mm, reads=[w3s.buf, hsb.buf],
                     writes=[P.pbk[o2] for o2 in range(8)] if ht == 0 else [],
                     pwrites=[P.pbk[o2] for o2 in range(8)] if ht else [])
            for o in range(8):
                s.op("dve", lambda e, o=o: e.tensor_mul(
                    t1s.ap(), P.pb(o), combB.ap()),
                    reads=[P.pbk[o], combB.buf], writes=[t1s.buf])
                s.op("pool", lambda e, o=o: e.tensor_add(
                    acc[o].ap(), acc[o].ap(), t1s.ap()),
                    reads=[acc[o].buf, t1s.buf], pwrites=[acc[o].buf])
        for o in range(8):
            pi, x2 = xin[o % 2], xin[2 + o % 2]
            s.op("sp", lambda e, o=o, ts=ts, pi=pi: e.dma_start(
                pi.ap(), paD[o * 128:(o + 1) * 128, ts]),
                reads=[paB], writes=[pi.buf], dma=True)
            s.op("sp", lambda e, o=o, ts=ts, x2=x2: e.dma_start(
                x2.ap(), x1D[o * 128:(o + 1) * 128, ts]),
                reads=[x1B], writes=[x2.buf], dma=True)
            yos = yo[o % 2]
            s.op("dve", lambda e, o=o, x2=x2, yos=yos: e.tensor_add(
                yos.ap(), acc[o].ap(), x2.ap()),
                reads=[acc[o].buf, x2.buf], writes=[yos.buf])
            s.op("pool", lambda e, o=o, pi=pi, yos=yos: e.tensor_add(
                yos.ap(), yos.ap(), pi.ap()),
                reads=[yos.buf, pi.buf], pwrites=[yos.buf])
            s.op("sp", lambda e, o=o, ts=ts, yos=yos: e.dma_start(
                y0D[o * 128:(o + 1) * 128, ts], yos.ap()),
                reads=[yos.buf], pwrites=[y0B], dma=True)


def ph_ff1(P):
    s = P.s
    ynD, ynB = P.dram["ynTb"], P.dbuf["ynTb"]
    p1D, p1B = P.dram["p1T"], P.dbuf["p1T"]
    outD, outB = P.dram["outT"], P.dbuf["outT"]
    w1C, w1Cb = P.cache("ff_w1")
    w2C, w2Cb = P.cache("ff_w2")
    h1C, h1Cb = P.cache("ff_h1")
    h2C, h2Cb = P.cache("ff_h2")
    w3C, w3Cb = P.cache("ff_w3")
    s1 = P.af(32, "f_s1")
    s2 = P.af(32, "f_s2")
    hs1 = P.af(32, "f_hs1")
    hs2 = P.af(32, "f_hs2")
    s3 = P.af(16, "f_s3")
    load_weight_f32(P, "ff_s1", s1, cols=32)
    load_weight_f32(P, "ff_s2", s2, cols=32)
    load_weight_f32(P, "ff_hs1", hs1, cols=32)
    load_weight_f32(P, "ff_hs2", hs2, cols=32)
    load_weight_f32(P, "ff_s3", s3, cols=8)

    yn = [P.ab(1024, f"f_y{c}") for c in range(8)]
    hb = P.ab(32768, "f_h")
    gb = P.ab(32768, "f_g")
    wA = [P.ab(1024, f"f_wa{c}") for c in range(8)]
    wB = [P.ab(1024, f"f_wb{c}") for c in range(8)]
    wr = [P.ab(1024, f"f_wr{i}") for i in range(4)]
    usil = [P.af(512, f"f_u{i}") for i in range(4)]
    p1b = []
    for i in range(4):
        b = SBuf(P.ar.t, gb.c0 + i * 2048, 1024, F32, f"f_p{i}")
        b.buf = gb.buf
        p1b.append(b)
    ob = []
    for i in range(2):
        b = SBuf(P.ar.t, gb.c0 + (4 + i) * 2048, 1024, F32, f"f_o{i}")
        b.buf = gb.buf
        ob.append(b)
    TCW = 1024

    for ch in range(4):
        ts = slice(ch * TCW, (ch + 1) * TCW)
        for c in range(8):
            s.op("sp", lambda e, c=c, ts=ts: e.dma_start(
                yn[c].ap(), ynD[c * 128:(c + 1) * 128, ts]),
                reads=[ynB], writes=[yn[c].buf], dma=True)
        for ob4 in range(4):
            for c in range(8):
                s.op("sp", lambda e, c=c, ob4=ob4: e.dma_start(
                    wA[c].ap(), w1C[c * 128:(c + 1) * 128,
                                    ob4 * 1024:(ob4 + 1) * 1024]),
                    reads=[w1Cb], writes=[wA[c].buf], dma=True)
                s.op("pool", lambda e, c=c, ob4=ob4: e.dma_start(
                    wB[c].ap(), w2C[c * 128:(c + 1) * 128,
                                    ob4 * 1024:(ob4 + 1) * 1024]),
                    reads=[w2Cb], writes=[wB[c].buf], dma=True)
            for oi in range(8):
                ot = ob4 * 8 + oi
                for sub in range(2):
                    b1k = (oi * 2 + sub) % 4
                    b2k = 4 + b1k
                    us = usil[b1k]

                    def mku(e, oi=oi, sub=sub, b1k=b1k):
                        last = None
                        for c in range(8):
                            last = e.matmul(
                                P.pb(b1k),
                                wA[c].ap(None, slice(oi * 128, (oi + 1) * 128)),
                                yn[c].ap(None, slice(sub * 512, (sub + 1) * 512)),
                                start=(c == 0), stop=(c == 7))
                        return last

                    def mku2(e, oi=oi, sub=sub, b2k=b2k):
                        last = None
                        for c in range(8):
                            last = e.matmul(
                                P.pb(b2k),
                                wB[c].ap(None, slice(oi * 128, (oi + 1) * 128)),
                                yn[c].ap(None, slice(sub * 512, (sub + 1) * 512)),
                                start=(c == 0), stop=(c == 7))
                        return last
                    s.op("pe", mku, reads=[t.buf for t in wA] + [t.buf for t in yn],
                         writes=[P.pbk[b1k]])
                    s.op("pe", mku2, reads=[t.buf for t in wB] + [t.buf for t in yn],
                         writes=[P.pbk[b2k]])
                    s.op("act", lambda e, ot=ot, b1k=b1k, us=us: e.activation(
                        us.ap(), P.pb(b1k), AF.Silu,
                        scale=s1.ap(None, slice(ot, ot + 1))),
                        reads=[P.pbk[b1k], s1.buf], writes=[us.buf])
                    s.op("dve", lambda e, ot=ot, sub=sub, b2k=b2k, us=us:
                         e.scalar_tensor_tensor(
                             hb.ap(None, slice(ot * 1024 + sub * 512,
                                               ot * 1024 + (sub + 1) * 512)),
                             P.pb(b2k), s2.ap(None, slice(ot, ot + 1)),
                             us.ap(), OP.mult, OP.mult),
                         reads=[P.pbk[b2k], s2.buf, us.buf],
                         writes=[hb.buf] if ot == 0 and sub == 0 else [],
                         pwrites=[hb.buf] if (ot or sub) else [])
        for (wc, wcb, hs, src, dst) in ((h1C, h1Cb, hs1, hb, gb),
                                        (h2C, h2Cb, hs2, gb, hb)):
            for obk in range(8):
                for k2 in range(16):
                    wrs = wr[k2 % 4]
                    s.op("sp" if k2 % 2 == 0 else "pool",
                         lambda e, k2=k2, obk=obk, wrs=wrs, wc=wc: e.dma_start(
                             wrs.ap(None, slice(0, 1024)).rearrange(
                                 "p (k o) -> p k o", k=2),
                             wc[k2 * 256:(k2 + 1) * 256,
                                obk * 512:(obk + 1) * 512].rearrange(
                                 "(k p) o -> p k o", p=128)),
                         reads=[wcb], writes=[wrs.buf], dma=True)

                    def mkh(e, k2=k2, obk=obk, wrs=wrs, src=src):
                        last = None
                        for kk in range(2):
                            kt = k2 * 2 + kk
                            for sub in range(2):
                                for oi in range(4):
                                    last = e.matmul(
                                        P.pb(sub * 4 + oi),
                                        wrs.ap(None, slice(kk * 512 + oi * 128,
                                                           kk * 512 + (oi + 1) * 128)),
                                        src.ap(None, slice(kt * 1024 + sub * 512,
                                                           kt * 1024 + (sub + 1) * 512)),
                                        start=(kt == 0), stop=(kt == 31))
                        return last
                    s.op("pe", mkh, reads=[wrs.buf, src.buf],
                         writes=[P.pbk[i] for i in range(8)] if k2 == 0 else [],
                         pwrites=[P.pbk[i] for i in range(8)] if k2 else [])
                for sub in range(2):
                    for oi in range(4):
                        ot = obk * 4 + oi
                        s.op("act", lambda e, ot=ot, sub=sub, oi=oi, hs=hs, dst=dst:
                             e.activation(
                                 dst.ap(None, slice(ot * 1024 + sub * 512,
                                                    ot * 1024 + (sub + 1) * 512)),
                                 P.pb(sub * 4 + oi), AF.Gelu,
                                 scale=hs.ap(None, slice(ot, ot + 1))),
                             reads=[P.pbk[sub * 4 + oi], hs.buf],
                             writes=[dst.buf] if obk == 0 and ot == 0 and sub == 0 else [],
                             pwrites=[dst.buf] if (obk or ot or sub) else [])
        for obk in range(2):
            for oi in range(4):
                o = obk * 4 + oi
                pb1 = p1b[oi]
                s.op("sp", lambda e, o=o, ts=ts, pb1=pb1: e.dma_start(
                    pb1.ap(), p1D[o * 128:(o + 1) * 128, ts]),
                    reads=[p1B], writes=[pb1.buf], dma=True)
            for k2 in range(16):
                wrs = wr[k2 % 4]
                s.op("sp" if k2 % 2 == 0 else "pool",
                     lambda e, k2=k2, obk=obk, wrs=wrs: e.dma_start(
                         wrs.ap(None, slice(0, 1024)).rearrange(
                             "p (k o) -> p k o", k=2),
                         w3C[k2 * 256:(k2 + 1) * 256,
                             obk * 512:(obk + 1) * 512].rearrange(
                             "(k p) o -> p k o", p=128)),
                     reads=[w3Cb], writes=[wrs.buf], dma=True)

                def mk3(e, k2=k2, obk=obk, wrs=wrs):
                    last = None
                    for kk in range(2):
                        kt = k2 * 2 + kk
                        for sub in range(2):
                            for oi in range(4):
                                last = e.matmul(
                                    P.pb(sub * 4 + oi),
                                    wrs.ap(None, slice(kk * 512 + oi * 128,
                                                       kk * 512 + (oi + 1) * 128)),
                                    hb.ap(None, slice(kt * 1024 + sub * 512,
                                                      kt * 1024 + (sub + 1) * 512)),
                                    start=(kt == 0), stop=(kt == 31))
                    return last
                s.op("pe", mk3, reads=[wrs.buf, hb.buf],
                     writes=[P.pbk[i] for i in range(8)] if k2 == 0 else [],
                     pwrites=[P.pbk[i] for i in range(8)] if k2 else [])
            for oi in range(4):
                o = obk * 4 + oi
                pb1 = p1b[oi]
                obs = ob[oi % 2]
                for sub in range(2):
                    s.op("dve", lambda e, o=o, oi=oi, sub=sub, pb1=pb1, obs=obs:
                         e.scalar_tensor_tensor(
                             obs.ap(None, slice(sub * 512, (sub + 1) * 512)),
                             P.pb(sub * 4 + oi), s3.ap(None, slice(o, o + 1)),
                             pb1.ap(None, slice(sub * 512, (sub + 1) * 512)),
                             OP.mult, OP.add),
                         reads=[P.pbk[sub * 4 + oi], s3.buf, pb1.buf],
                         writes=[obs.buf] if sub == 0 else [],
                         pwrites=[obs.buf] if sub else [])
                s.op("sp", lambda e, o=o, ts=ts, obs=obs: e.dma_start(
                    outD[o * 128:(o + 1) * 128, ts], obs.ap()),
                    reads=[obs.buf], pwrites=[outB], dma=True)


PHASE_FNS = {
    "ln0": ph_ln0, "ln1": ph_ln1,
    "moe": ph_moe, "ff1": ph_ff1,
    "retp": lambda P: attn_prep(P, ACFG["ret"]),
    "reth": lambda P: attn_heads(P, ACFG["ret"]),
    "reto": lambda P: attn_out(P, ACFG["ret"]),
    "fap": lambda P: attn_prep(P, ACFG["fa"]),
    "fah": lambda P: attn_heads(P, ACFG["fa"]),
    "fao": lambda P: attn_out(P, ACFG["fa"]),
    "a1p": lambda P: attn_prep(P, ACFG["at1"]),
    "a1h": lambda P: attn_heads(P, ACFG["at1"]),
    "a1o": lambda P: attn_out(P, ACFG["at1"]),
}


def _run(inputs, phase_groups, taps=()):
    consts = make_consts()
    w = prep_weights(inputs)
    wshapes = {k: v.shape for k, v in w.items()}
    x = inputs["L0_attn_x"]
    xTs = [np.ascontiguousarray(np.asarray(x[i], dtype=np.float32).T) for i in range(8)]

    hbm = [{} for _ in range(8)]
    out = None
    taps_res = {}
    for gi, group in enumerate(phase_groups):
        later = set()
        for g2 in phase_groups[gi + 1:]:
            for p in g2:
                later |= set(PHASE_NEEDS[p])
        gtaps = [t for t in taps
                 if t in sum((PHASE_MAKES[p] for p in group), [])]
        P = Prog(group, wshapes, consts, later_needs=later, taps=gtaps)
        nc = P.build()
        in_maps = []
        for ci in range(8):
            m = {}
            for name in P.ext_in:
                if name == "xT":
                    m["xT"] = xTs[ci]
                elif name in w:
                    m["w_" + name] = w[name]
                else:
                    m[name] = hbm[ci][name]
            in_maps.append(m)
        res = run_bass_kernel_spmd(nc, in_maps, core_ids=list(range(8)))
        for ci in range(8):
            for name, arr in res.results[ci].items():
                if name.startswith("tap_"):
                    taps_res.setdefault(name[4:], [None] * 8)[ci] = arr
                elif name == "outT":
                    if out is None:
                        out = [None] * 8
                    out[ci] = arr
                else:
                    hbm[ci][name] = arr
    return out, taps_res, hbm


def kernel(**inputs):
    groups = [PHASES] if MODE == "fused" else [[p] for p in PHASES]
    out, _, _ = _run(inputs, groups)
    res = np.stack([o.T for o in out], axis=0).astype(np.float32)
    return np.ascontiguousarray(res)


# version 20 (final)
# speedup vs baseline: 1.0646x; 1.0500x over previous
import numpy as np
import ml_dtypes

import concourse.bass as bass
import concourse.mybir as mybir
from concourse.bass_utils import run_bass_kernel_spmd

F32 = mybir.dt.float32
BF16 = mybir.dt.bfloat16
AX = mybir.AxisListType.X
OP = mybir.AluOpType
AF = mybir.ActivationFunctionType
BF = ml_dtypes.bfloat16

T = 4096
C = 1024
H = 16
DH = 64
NK = C // 128
NTC = T // 512

MODE = "fused"
PHASES = ["ln0", "retp", "reth", "reto", "fap", "fah", "fao", "moe",
          "ln1", "a1p", "a1h", "a1o", "ff1"]

QUEUES = ("pe", "act", "dve", "pool", "sp")


class Buf:
    __slots__ = ("writers", "readers", "name")

    def __init__(self, name=""):
        self.writers = []
        self.readers = []
        self.name = name


NLANES = 8


class OpH:
    __slots__ = ("q", "fn", "deps", "inc", "sig", "val", "dma", "key", "bg")

    def __init__(self, q, fn, deps, dma):
        self.q = q
        self.fn = fn
        self.deps = deps
        self.inc = 16 if dma else 1
        self.dma = dma
        self.sig = False
        self.val = None
        self.key = None
        self.bg = False


class Sched:
    def __init__(self):
        self.ops = {q: [] for q in QUEUES}
        self.default_deps = []
        self.dma_count = {q: 0 for q in QUEUES}

    def op(self, q, fn, reads=(), writes=(), pwrites=(), deps=(), dma=False,
           dma_key=None):
        dl = list(self.default_deps)
        for b in reads:
            dl.extend(b.writers)
        for b in writes:
            dl.extend(b.writers)
            dl.extend(b.readers)
        for b in pwrites:
            dl.extend(b.readers)
        dl.extend(d for d in deps if d is not None)
        seen, deps2 = set(), []
        for d in dl:
            if id(d) not in seen:
                seen.add(id(d))
                deps2.append(d)
        o = OpH(q, fn, deps2, dma)
        if dma:
            kq = dma_key if dma_key is not None else q
            self.dma_count.setdefault(kq, 0)
            o.key = (kq, self.dma_count[kq] % NLANES)
            self.dma_count[kq] += 1
        else:
            o.key = q
        for d in deps2:
            if not (q == "pe" and d.key == "pe"):
                d.sig = True
        for b in reads:
            b.readers.append(o)
        for b in writes:
            b.writers = [o]
            b.readers = []
        for b in pwrites:
            b.writers.append(o)
        self.ops[q].append(o)
        return o

    def barrier(self):
        lasts = []
        for q in QUEUES:
            for o in reversed(self.ops[q]):
                if not o.bg:
                    lasts.append(o)
                    break
        for q in QUEUES:
            self.op(q, lambda e: e.nop(), deps=lasts)

    def finalize(self, sems):
        totals = {}
        for q in QUEUES:
            for o in self.ops[q]:
                if o.dma:
                    o.sig = True
                if o.sig:
                    v = totals.get(o.key, 0) + o.inc
                    o.val = v
                    totals[o.key] = v

        def mk(q):
            def run(eng):
                waited = {}

                def wait(key, val):
                    if val is None or waited.get(key, 0) >= val:
                        return
                    eng.wait_ge(sems[key], val)
                    waited[key] = val

                for o in self.ops[q]:
                    for d in o.deps:
                        if q == "pe" and d.key == "pe":
                            continue
                        wait(d.key, d.val)
                    if o.dma and o.val is not None and o.val > 16:
                        wait(o.key, o.val - 16)
                    inst = o.fn(eng)
                    if o.sig:
                        assert inst is not None, f"op on {q} needs inc but returned None"
                        inst.then_inc(sems[o.key], o.inc)
                for key, tv in sorted(totals.items(), key=str):
                    if key != q:
                        wait(key, tv)
            return run
        return {q: mk(q) for q in QUEUES}


class SBuf:
    __slots__ = ("t", "c0", "cols", "dt", "buf")

    def __init__(self, t, c0, cols, dt, name):
        self.t = t
        self.c0 = c0
        self.cols = cols
        self.dt = dt
        self.buf = Buf(name)

    def ap(self, rs=None, cs=None):
        r = rs if rs is not None else slice(0, 128)
        c = cs if cs is not None else slice(0, self.cols)
        if self.dt == BF16:
            return self.t[r, self.c0 + c.start: self.c0 + c.stop]
        return self.t[r, self.c0 + 2 * c.start: self.c0 + 2 * c.stop].bitcast(F32)


class Arena:

    def __init__(self, nc, name, cols_bf):
        self.t = nc.alloc_sbuf_tensor(name, [128, cols_bf], BF16)
        self.cols = cols_bf
        self.off = 0
        self.name = name

    def alloc(self, cols, dt=BF16, name=""):
        units = cols * (2 if dt == F32 else 1)
        c0 = (self.off + 15) & ~15
        assert c0 + units <= self.cols, \
            f"arena {self.name} overflow: {c0}+{units}>{self.cols} ({name})"
        self.off = c0 + units
        return SBuf(self.t, c0, cols, dt, name)

    def reset(self):
        self.off = 0


def make_consts():
    t = np.arange(T, dtype=np.float64)
    inv = 1.0 / 10000.0 ** (np.arange(0, DH, 2, dtype=np.float64) / DH)
    fr = t[:, None] * inv[None, :]
    emb = np.concatenate([fr, fr], axis=1)
    cosT = np.cos(emb).T.astype(np.float32)
    sinS = np.sin(emb).T.astype(np.float32)
    sinS[:32] *= -1.0
    cos8 = (cosT * 0.125).astype(np.float32)
    sin8 = (sinS * 0.125).astype(np.float32)

    gam = 1.0 - 2.0 ** (-5.0 - np.arange(H, dtype=np.float64))
    lng = np.log(gam)
    j = np.arange(512, dtype=np.float64)
    colf = np.exp(lng[:, None] * j[None, :])
    colf = np.repeat(colf[:, None, :], DH, axis=1).astype(np.float32)
    d0s = -384.0 + 128.0 * np.arange(35, dtype=np.float64)
    i = np.arange(128, dtype=np.float64)
    ex = d0s[None, :, None] - i[None, None, :]
    with np.errstate(over="ignore", under="ignore"):
        rf = np.exp(lng[:, None, None] * ex)
    rf = np.where(np.isfinite(rf), rf, 0.0).astype(np.float32)
    rf = np.ascontiguousarray(rf.transpose(2, 0, 1))
    jj = np.arange(512)
    ii = np.arange(128)
    masks = np.stack([(jj[None, :] - ii[:, None] + (-384 + 128 * k)) >= 0
                      for k in range(4)], axis=0).astype(BF)
    return {
        "cosT": cosT, "sinS": sinS, "cos8": cos8, "sin8": sin8,
        "colf": colf.reshape(H * DH, 512), "rf": rf,
        "masks": np.ascontiguousarray(masks.transpose(1, 0, 2).reshape(128, 4 * 512)),
        "ones_bf": np.ones((128, 1), dtype=BF),
        "ones_f": np.ones((1, 128), dtype=np.float32),
        "ones_row_bf": np.ones((1, 128), dtype=BF),
        "ident": np.eye(128, dtype=BF),
        "sel8": np.ascontiguousarray(np.kron(np.eye(8), np.ones((1, 128))).astype(BF)),
    }


def prep_weights(inp):
    w = {}
    WqT = np.ascontiguousarray(inp["L0_attn_Wq"].T)
    WkT = np.ascontiguousarray(inp["L0_attn_Wk"].T)
    w["ret_qk"] = np.ascontiguousarray(np.stack(
        [np.concatenate([WqT[:, h * 64:(h + 1) * 64], WkT[:, h * 64:(h + 1) * 64]], 1)
         for h in range(H)]))
    w["ret_v"] = np.ascontiguousarray(inp["L0_attn_Wv"].T)
    w["ret_wo"] = np.ascontiguousarray(inp["L0_attn_Wo"].T)
    w["gnw"] = np.ascontiguousarray(inp["L0_attn_gn_weight"].reshape(16, 64).T)
    w["gnb"] = np.ascontiguousarray(inp["L0_attn_gn_bias"].reshape(16, 64).T)

    qkv = inp["L0_ffn_qkv_w"]
    QT = np.ascontiguousarray(qkv[0:1024].T)
    KTm = np.ascontiguousarray(qkv[1024:2048].T)
    w["fa_qk"] = np.ascontiguousarray(np.stack(
        [np.concatenate([QT[:, h * 64:(h + 1) * 64], KTm[:, h * 64:(h + 1) * 64]], 1)
         for h in range(H)]))
    w["fa_v"] = np.ascontiguousarray(qkv[2048:3072].T)
    b = inp["L0_ffn_qkv_b"]
    w["fa_qkb"] = np.ascontiguousarray(np.stack(
        [np.concatenate([b[h * 64:(h + 1) * 64], b[1024 + h * 64:1024 + (h + 1) * 64]])
         for h in range(H)]).T)
    w["fa_vb"] = np.ascontiguousarray(b[2048:3072].reshape(1, 1024))
    w["fa_out"] = np.ascontiguousarray(inp["L0_ffn_out_w"].T)
    w["fa_outb"] = np.ascontiguousarray(inp["L0_ffn_out_b"].reshape(8, 128).T)
    w["norm1w"] = np.ascontiguousarray(inp["L0_ffn_norm1_w"].reshape(8, 128).T)
    w["norm2w"] = np.ascontiguousarray(inp["L0_ffn_norm2_w"].reshape(8, 128).T)
    w["gateT"] = np.ascontiguousarray(inp["L0_ffn_gate_w"].T)
    w["moe_w1"] = np.ascontiguousarray(inp["L0_ffn_w1"].transpose(0, 2, 1))
    w["moe_w2"] = np.ascontiguousarray(inp["L0_ffn_w2"].transpose(0, 2, 1))
    w["moe_w3"] = np.ascontiguousarray(inp["L0_ffn_w3"].transpose(0, 2, 1))
    w["moe_b1"] = np.ascontiguousarray(
        inp["L0_ffn_b1"].reshape(8, 16, 128).transpose(0, 2, 1))
    w["moe_b2"] = np.ascontiguousarray(
        inp["L0_ffn_b2"].reshape(8, 16, 128).transpose(0, 2, 1))
    w["moe_b3"] = np.ascontiguousarray(inp["L0_ffn_b3"])

    Wq1 = inp["L1_attn_Wqkv"]
    w["at1_qk"] = np.ascontiguousarray(Wq1[:, :, 0:128])
    w["at1_v"] = np.ascontiguousarray(
        Wq1[:, :, 128:192].transpose(1, 0, 2).reshape(1024, 1024))
    b1 = inp["L1_attn_bqkv"]
    w["at1_qkb"] = np.ascontiguousarray(b1[:, 0:128].T)
    w["at1_vb"] = np.ascontiguousarray(b1[:, 128:192].reshape(1, 1024))
    w["at1_out"] = np.ascontiguousarray(inp["L1_attn_Wout"])
    w["at1_outb"] = np.ascontiguousarray(inp["L1_attn_bout"].reshape(8, 128).T)

    w["ff_w1"] = np.ascontiguousarray(inp["L1_ffn_w1_w"].T)
    w["ff_w2"] = np.ascontiguousarray(inp["L1_ffn_w2_w"].T)
    w["ff_h1"] = np.ascontiguousarray(inp["L1_ffn_h1_w"].T)
    w["ff_h2"] = np.ascontiguousarray(inp["L1_ffn_h2_w"].T)
    w["ff_w3"] = np.ascontiguousarray(inp["L1_ffn_w3_w"].T)
    w["ff_s1"] = np.ascontiguousarray(inp["L1_ffn_w1_s"].reshape(32, 128).T)
    w["ff_s2"] = np.ascontiguousarray(inp["L1_ffn_w2_s"].reshape(32, 128).T)
    w["ff_hs1"] = np.ascontiguousarray(inp["L1_ffn_h1_s"].reshape(32, 128).T)
    w["ff_hs2"] = np.ascontiguousarray(inp["L1_ffn_h2_s"].reshape(32, 128).T)
    w["ff_s3"] = np.ascontiguousarray(inp["L1_ffn_w3_s"].reshape(8, 128).T)
    return w


CACHED = {"ret_qk", "ret_v", "ret_wo", "fa_qk", "fa_v", "fa_out", "gateT",
          "moe_w1", "moe_w2", "moe_w3", "moe_b3", "at1_qk", "at1_v", "at1_out",
          "ff_w1", "ff_w2", "ff_h1", "ff_h2", "ff_w3"}

SCRATCH = {
    "xnT": ((C, T), F32, "ln0"),
    "xnTb": ((C, T), BF16, "ln0"),
    "partialT": ((C, T), F32, "reto"),
    "x1T": ((C, T), F32, "fao"),
    "y0T": ((C, T), F32, "moe"),
    "ynT": ((C, T), F32, "ln1"),
    "ynTb": ((C, T), BF16, "ln1"),
    "p1T": ((C, T), F32, "a1o"),
    "r1st": ((1, T), F32, "ln0"),
    "qkf_ret": ((2048, T), F32, "retp"),
    "vt_ret": ((T, C), BF16, "retp"),
    "hd_ret": ((C, T), BF16, "reth"),
    "qkf_fa": ((2048, T), F32, "fap"),
    "vt_fa": ((T, C), BF16, "fap"),
    "hd_fa": ((C, T), BF16, "fah"),
    "qkf_at1": ((2048, T), BF16, "a1p"),
    "vt_at1": ((T, C), BF16, "a1p"),
    "hd_at1": ((C, T), BF16, "a1h"),
}

PHASE_NEEDS = {
    "ln0": ["xT"],
    "retp": ["xnTb"], "reth": ["qkf_ret", "vt_ret"], "reto": ["hd_ret", "xT"],
    "fap": ["xnT", "r1st"], "fah": ["qkf_fa", "vt_fa"], "fao": ["hd_fa", "xnT"],
    "moe": ["x1T", "partialT"],
    "ln1": ["y0T"],
    "a1p": ["ynTb"], "a1h": ["qkf_at1", "vt_at1"], "a1o": ["hd_at1", "y0T"],
    "ff1": ["ynTb", "p1T"],
}
PHASE_MAKES = {
    "ln0": ["xnT", "xnTb", "r1st"],
    "retp": ["qkf_ret", "vt_ret"], "reth": ["hd_ret"], "reto": ["partialT"],
    "fap": ["qkf_fa", "vt_fa"], "fah": ["hd_fa"], "fao": ["x1T"],
    "moe": ["y0T"],
    "ln1": ["ynT", "ynTb"],
    "a1p": ["qkf_at1", "vt_at1"], "a1h": ["hd_at1"], "a1o": ["p1T"],
    "ff1": ["outT"],
}
PHASE_WEIGHTS = {
    "ln0": [], "ln1": [],
    "retp": ["ret_qk", "ret_v"], "reth": ["gnw", "gnb"],
    "reto": ["ret_wo"],
    "fap": ["fa_qk", "fa_v", "fa_qkb", "fa_vb", "norm1w"], "fah": [],
    "fao": ["fa_out", "fa_outb"],
    "moe": ["gateT", "moe_w1", "moe_w2", "moe_w3", "moe_b1", "moe_b2",
            "moe_b3", "norm2w"],
    "a1p": ["at1_qk", "at1_v", "at1_qkb", "at1_vb"], "a1h": [],
    "a1o": ["at1_out", "at1_outb"],
    "ff1": ["ff_w1", "ff_w2", "ff_h1", "ff_h2", "ff_w3",
            "ff_s1", "ff_s2", "ff_hs1", "ff_hs2", "ff_s3"],
}

ACFG = {
    "ret": dict(qk="ret_qk", v="ret_v", qkb=None, vb=None, xsrc="xnTb",
                rms=None, rope=True, kind="decay", causal=True,
                qkf="qkf_ret", vt="vt_ret", hd="hd_ret",
                ow="ret_wo", ob=None, res="xT", dst="partialT"),
    "fa": dict(qk="fa_qk", v="fa_v", qkb="fa_qkb", vb="fa_vb", xsrc=None,
               rms=("xnT", "norm1w", 1e-6), rope=True, kind="softmax",
               causal=False, qkf="qkf_fa", vt="vt_fa", hd="hd_fa",
               ow="fa_out", ob="fa_outb", res="xnT", dst="x1T",
               out_stats=True),
    "at1": dict(qk="at1_qk", v="at1_v", qkb="at1_qkb", vb="at1_vb",
                xsrc="ynTb", rms=None, rope=False, kind="softmax",
                causal=True, qkf="qkf_at1", vt="vt_at1", hd="hd_at1",
                ow="at1_out", ob="at1_outb", res="y0T", dst="p1T"),
}


class Prog:
    def __init__(self, phases, wshapes, consts, later_needs=(), taps=()):
        self.nc = bass.Bass("TRN2", target_bir_lowering=False, debug=False)
        self.s = Sched()
        self.phases = phases
        nc = self.nc
        self.ext_in = []
        self.dram = {}
        self.dbuf = {}
        self.cache_done = {}

        needs, makes = set(), set()
        for ph in phases:
            needs |= set(PHASE_NEEDS[ph])
            makes |= set(PHASE_MAKES[ph])

        if "xT" in needs:
            self.dram["xT"] = nc.dram_tensor("xT", [C, T], F32, kind="ExternalInput")
            self.dbuf["xT"] = Buf("xT")
            self.ext_in.append("xT")
        if "outT" in makes:
            self.dram["outT"] = nc.dram_tensor("outT", [C, T], F32, kind="ExternalOutput")
            self.dbuf["outT"] = Buf("outT")
        for name, (shape, dt, prod) in SCRATCH.items():
            if name not in (needs | makes):
                continue
            if name in makes and name in later_needs:
                kind = "ExternalOutput"
            elif name in needs and name not in makes:
                kind = "ExternalInput"
            else:
                kind = "Internal"
            self.dram[name] = nc.dram_tensor(name, list(shape), dt, kind=kind)
            self.dbuf[name] = Buf(name)
            if kind == "ExternalInput":
                self.ext_in.append(name)

        self.wext = {}
        self.wcache = {}
        for ph in phases:
            for wn in PHASE_WEIGHTS[ph]:
                if wn in self.wext:
                    continue
                self.wext[wn] = nc.dram_tensor("w_" + wn, list(wshapes[wn]), F32,
                                               kind="ExternalInput")
                self.dbuf["w_" + wn] = Buf("w_" + wn)
                self.ext_in.append(wn)
                if wn in CACHED:
                    self.wcache[wn] = nc.dram_tensor("bc_" + wn, list(wshapes[wn]),
                                                     BF16, kind="Internal")
                    self.dbuf["bc_" + wn] = Buf("bc_" + wn)

        self.ct = {k: nc.inline_tensor(np.ascontiguousarray(v), name="c_" + k)
                   for k, v in consts.items()}
        self.cbuf = {k: Buf("c_" + k) for k in consts}

        self.ar = Arena(nc, "mega", 100608)
        self.psum = nc.alloc_psum_tensor("ps", [128, 4096], F32)
        self.sems = {q: nc.alloc_semaphore(name="sem_" + q) for q in QUEUES}
        for q in list(QUEUES) + ["bgp"]:
            for ln in range(NLANES):
                self.sems[(q, ln)] = nc.alloc_semaphore(name=f"sem_{q}_d{ln}")

        cvals = [0.0, 1.0, 1e-5, 1e-6, 0.125, -1.0]
        cv_np = np.tile(np.array([cvals], dtype=np.float32), (128, 1))
        cv_t = nc.inline_tensor(cv_np, name="c_scal")
        cv_sb = nc.alloc_sbuf_tensor("cscal", [128, 16], F32)
        boot = self.s.op("sp", lambda e: e.dma_start(cv_sb[0:128, 0:len(cvals)], cv_t.ap()),
                         dma=True)
        for i, v in enumerate(cvals):
            nc.const_aps.register(F32, float(v), cv_sb[0:128, i:i + 1])
        self.s.default_deps = [boot]

        self.taps = list(taps)
        self.tap_out = {}
        for tn in self.taps:
            if tn in SCRATCH:
                shape, dt, _ = SCRATCH[tn]
            else:
                shape, dt = (C, T), F32
            self.tap_out[tn] = nc.dram_tensor("tap_" + tn, list(shape), dt,
                                              kind="ExternalOutput")

    def ab(self, cols, name=""):
        return self.ar.alloc(cols, BF16, name)

    def af(self, cols, name=""):
        return self.ar.alloc(cols, F32, name)

    def cache(self, wn):
        if wn not in self.cache_done:
            src, dst = self.wext[wn], self.wcache[wn]
            o = self.s.op(
                "pool", lambda e: e.dma_start(dst.ap(), src.ap()),
                reads=[self.dbuf["w_" + wn]], writes=[self.dbuf["bc_" + wn]],
                dma=True, dma_key="bgp")
            o.bg = True
            self.cache_done[wn] = o
        return self.wcache[wn], self.dbuf["bc_" + wn]

    def pb(self, bank, nbanks=1, rows=slice(0, 128)):
        return self.psum[rows, bank * 512:(bank + nbanks) * 512]

    def tap(self, name):
        if name not in self.tap_out:
            return
        src, dst = self.dram[name], self.tap_out[name]
        self.s.op("sp", lambda e: e.dma_start(dst.ap(), src.ap()),
                  reads=[self.dbuf[name]], writes=[Buf()], dma=True)

    def build(self):
        for i, ph in enumerate(self.phases):
            self.pbk = [Buf(f"bank{i}") for i in range(8)]
            for ph2 in self.phases[i + 1:i + 3]:
                for wn in PHASE_WEIGHTS[ph2]:
                    if wn in self.wcache:
                        self.cache(wn)
            PHASE_FNS[ph](self)
            for mk in PHASE_MAKES[ph]:
                if mk in self.taps:
                    self.tap(mk)
            self.s.barrier()
            self.ar.reset()
        emitters = self.s.finalize(self.sems)
        with self.nc.Block("main") as blk:
            blk.tensor(emitters["pe"])
            blk.scalar(emitters["act"])
            blk.vector(emitters["dve"])
            blk.gpsimd(emitters["pool"])
            blk.sync(emitters["sp"])
        return self.nc


def load_weight_f32(P, wn, sbuf, src_slice=None, rows=128, cols=None):
    src = P.wext[wn]
    ap = src.ap() if src_slice is None else src_slice(src)
    cols = cols if cols is not None else ap.shape[-1]
    return P.s.op("sp", lambda e: e.dma_start(
        sbuf.ap(slice(0, rows), slice(0, cols)), ap),
        reads=[P.dbuf["w_" + wn]], writes=[sbuf.buf], dma=True)


def load_const(P, cname, sbuf, rows, cols):
    src = P.ct[cname]
    return P.s.op("sp", lambda e: e.dma_start(
        sbuf.ap(slice(0, rows), slice(0, cols)), src.ap()),
        reads=[P.cbuf[cname]], writes=[sbuf.buf], dma=True)


def ph_ln0(P):
    _ln_common(P, "xT", "xnT", "xnTb", sumsq_to="r1st")


def ph_ln1(P):
    _ln_common(P, "y0T", "ynT", "ynTb")


def _ln_common(P, srcn, dstfn, dstbn, sumsq_to=None):
    s = P.s
    src, srcb = P.dram[srcn], P.dbuf[srcn]
    dstf, dstfb = P.dram[dstfn], P.dbuf[dstfn]
    dstb, dstbb = P.dram[dstbn], P.dbuf[dstbn]

    ones_b = P.ab(16, "ln_ones")
    load_const(P, "ones_bf", ones_b, 128, 1)
    onesf = P.af(144, "ln_onesf")
    for r0 in (0, 32, 64):
        P.s.op("sp", lambda e, r0=r0: e.dma_start(
            onesf.ap(slice(r0, r0 + 1), slice(0, 128)), P.ct["ones_f"].ap()),
            reads=[P.cbuf["ones_f"]], pwrites=[onesf.buf], dma=True)

    xin = [P.af(4096, f"ln_x{i}") for i in range(2)]
    xbf = [P.ab(4096, f"ln_xbf{i}") for i in range(2)]
    xsq = [P.ab(4096, f"ln_xsq{i}") for i in range(2)]
    sum_ps, sq_ps = Buf("sum_ps"), Buf("sq_ps")
    for c in range(NK):
        xb, bb, qb = xin[c % 2], xbf[c % 2], xsq[c % 2]
        s.op("sp", lambda e, c=c, xb=xb: e.dma_start(
            xb.ap(), src[c * 128:(c + 1) * 128, :]),
            reads=[srcb], writes=[xb.buf], dma=True)
        s.op("act", lambda e, xb=xb, bb=bb: e.activation(bb.ap(), xb.ap(), AF.Copy),
             reads=[xb.buf], writes=[bb.buf])
        s.op("act", lambda e, xb=xb, qb=qb: e.activation(qb.ap(), xb.ap(), AF.Square),
             reads=[xb.buf], writes=[qb.buf])

        def mk_mm(row, tile, c=c):
            def f(e):
                last = None
                for tc in range(NTC):
                    last = e.matmul(P.psum[row:row + 1, tc * 512:(tc + 1) * 512],
                                    ones_b.ap(slice(0, 128), slice(0, 1)),
                                    tile.ap(slice(0, 128), slice(tc * 512, (tc + 1) * 512)),
                                    start=(c == 0), stop=(c == NK - 1))
                return last
            return f
        s.op("pe", mk_mm(0, bb), reads=[bb.buf, ones_b.buf],
             writes=[sum_ps] if c == 0 else [], pwrites=[sum_ps] if c > 0 else [])
        s.op("pe", mk_mm(32, qb), reads=[qb.buf, ones_b.buf],
             writes=[sq_ps] if c == 0 else [], pwrites=[sq_ps] if c > 0 else [])

    rows = P.af(4096, "ln_rows")
    rM, rR, rTm = slice(0, 1), slice(32, 33), slice(64, 65)
    s.op("act", lambda e: e.activation(rows.ap(rM), P.psum[0:1, :], AF.Copy, scale=1.0 / C),
         reads=[sum_ps], writes=[rows.buf])
    s.op("act", lambda e: e.activation(rows.ap(rTm), rows.ap(rM), AF.Square),
         reads=[rows.buf], pwrites=[rows.buf])
    s.op("dve", lambda e: e.scalar_tensor_tensor(rows.ap(rR), P.psum[32:33, :], 1.0 / C,
                                                 rows.ap(rTm), OP.mult, OP.subtract),
         reads=[sq_ps, rows.buf], pwrites=[rows.buf])
    s.op("act", lambda e: e.activation(rows.ap(rTm), rows.ap(rR), AF.Ln, bias=1e-5),
         reads=[rows.buf], pwrites=[rows.buf])
    s.op("act", lambda e: e.activation(rows.ap(rR), rows.ap(rTm), AF.Exp, scale=-0.5),
         reads=[rows.buf], pwrites=[rows.buf])

    xt2 = [P.af(512, f"ln_t{i}") for i in range(4)]
    xnf = [P.af(512, f"ln_v{i}") for i in range(4)]
    xnb = [P.ab(512, f"ln_b{i}") for i in range(4)]
    meanB, rstdB = Buf("meanB"), Buf("rstdB")
    if sumsq_to is not None:
        sqb = [P.ab(512, f"ln_s{i}") for i in range(4)]
        r1row = P.af(4096, "ln_r1row")
        ssD, ssB = P.dram[sumsq_to], P.dbuf[sumsq_to]
    for tc in range(NTC):
        cs = slice(tc * 512, (tc + 1) * 512)
        s.op("pe", lambda e, cs=cs: e.matmul(
            P.pb(6), onesf.ap(slice(0, 1), slice(0, 128)), rows.ap(rM, cs),
            start=True, stop=True),
            reads=[rows.buf, onesf.buf], writes=[meanB])
        s.op("pe", lambda e, cs=cs: e.matmul(
            P.pb(7), onesf.ap(slice(32, 33), slice(0, 128)), rows.ap(rR, cs),
            start=True, stop=True),
            reads=[rows.buf, onesf.buf], writes=[rstdB])
        for c in range(NK):
            i = c % 4
            xb, xo, xB = xt2[i], xnf[i], xnb[i]
            s.op("sp", lambda e, c=c, cs=cs, xb=xb: e.dma_start(
                xb.ap(), src[c * 128:(c + 1) * 128, cs]),
                reads=[srcb], writes=[xb.buf], dma=True)
            s.op("dve", lambda e, xb=xb, xo=xo: e.tensor_sub(xo.ap(), xb.ap(), P.pb(6)),
                 reads=[xb.buf, meanB], writes=[xo.buf])
            s.op("dve", lambda e, xo=xo: e.tensor_mul(xo.ap(), xo.ap(), P.pb(7)),
                 reads=[rstdB, xo.buf], pwrites=[xo.buf])
            s.op("act", lambda e, xo=xo, xB=xB: e.activation(xB.ap(), xo.ap(), AF.Copy),
                 reads=[xo.buf], writes=[xB.buf])
            s.op("sp", lambda e, c=c, cs=cs, xo=xo: e.dma_start(
                dstf[c * 128:(c + 1) * 128, cs], xo.ap()),
                reads=[xo.buf], pwrites=[dstfb], dma=True)
            s.op("sp", lambda e, c=c, cs=cs, xB=xB: e.dma_start(
                dstb[c * 128:(c + 1) * 128, cs], xB.ap()),
                reads=[xB.buf], pwrites=[dstbb], dma=True)
            if sumsq_to is not None:
                sb2 = sqb[i]
                s.op("act", lambda e, xo=xo, sb2=sb2: e.activation(
                    sb2.ap(), xo.ap(), AF.Square),
                    reads=[xo.buf], writes=[sb2.buf])
                s.op("pe", lambda e, c=c, sb2=sb2: e.matmul(
                    P.psum[0:1, 2560:3072], ones_b.ap(slice(0, 128), slice(0, 1)),
                    sb2.ap(), start=(c == 0), stop=(c == NK - 1)),
                    reads=[sb2.buf, ones_b.buf],
                    writes=[P.pbk[5]] if c == 0 else [],
                    pwrites=[P.pbk[5]] if c else [])
        if sumsq_to is not None:
            s.op("act", lambda e, cs=cs: e.activation(
                r1row.ap(slice(0, 1), cs), P.psum[0:1, 2560:3072], AF.Copy),
                reads=[P.pbk[5]],
                writes=[r1row.buf] if tc == 0 else [],
                pwrites=[r1row.buf] if tc else [])
    if sumsq_to is not None:
        s.op("sp", lambda e: e.dma_start(ssD.ap(), r1row.ap(slice(0, 1))),
             reads=[r1row.buf], writes=[ssB], dma=True)


def _rms_apply(P, rms, xsb):
    s = P.s
    srcn, wn, eps = rms
    src, srcb = P.dram[srcn], P.dbuf[srcn]
    ssD, ssB = P.dram["r1st"], P.dbuf["r1st"]
    nw = P.af(16, "rms_w")
    load_weight_f32(P, wn, nw, cols=8)
    onesf = P.af(144, "rms_onesf")
    P.s.op("sp", lambda e: e.dma_start(onesf.ap(slice(0, 1), slice(0, 128)),
                                       P.ct["ones_f"].ap()),
           reads=[P.cbuf["ones_f"]], writes=[onesf.buf], dma=True)
    xin = [P.af(512, f"rms_x{i}") for i in range(4)]
    rows = P.af(4096, "rms_rows")
    srow = P.af(4096, "rms_srow")
    r1 = slice(0, 1)
    s.op("sp", lambda e: e.dma_start(srow.ap(r1), ssD.ap()),
         reads=[ssB], writes=[srow.buf], dma=True)
    s.op("act", lambda e: e.activation(srow.ap(r1), srow.ap(r1),
                                       AF.Ln, scale=1.0 / C, bias=eps),
         reads=[srow.buf], pwrites=[srow.buf])
    s.op("act", lambda e: e.activation(rows.ap(r1), srow.ap(r1), AF.Exp, scale=-0.5),
         reads=[srow.buf], writes=[rows.buf])
    for tc in range(NTC):
        cs = slice(tc * 512, (tc + 1) * 512)
        s.op("pe", lambda e, cs=cs: e.matmul(
            P.pb(6), onesf.ap(slice(0, 1), slice(0, 128)), rows.ap(r1, cs),
            start=True, stop=True),
            reads=[rows.buf, onesf.buf], writes=[P.pbk[6]])
        for c in range(NK):
            i = c % 4
            xb, qb = xin[i], xin[(i + 2) % 4]
            s.op("sp", lambda e, c=c, cs=cs, xb=xb: e.dma_start(
                xb.ap(), src[c * 128:(c + 1) * 128, cs]),
                reads=[srcb], writes=[xb.buf], dma=True)
            s.op("dve", lambda e, xb=xb, qb=qb: e.tensor_mul(
                qb.ap(), xb.ap(), P.pb(6)),
                reads=[xb.buf, P.pbk[6]], writes=[qb.buf])
            s.op("act", lambda e, c=c, cs=cs, qb=qb: e.activation(
                xsb[c].ap(None, cs), qb.ap(), AF.Copy,
                scale=nw.ap(None, slice(c, c + 1))),
                reads=[qb.buf, nw.buf],
                writes=[xsb[c].buf] if tc == 0 else [],
                pwrites=[] if tc == 0 else [xsb[c].buf])


def attn_prep(P, cfg):
    s = P.s
    xsb = [P.ab(4096, f"px{c}") for c in range(8)]
    if cfg["rms"] is not None:
        _rms_apply(P, cfg["rms"], xsb)
    else:
        src, sbf = P.dram[cfg["xsrc"]], P.dbuf[cfg["xsrc"]]
        for c in range(8):
            s.op("sp", lambda e, c=c, t=xsb[c]: e.dma_start(
                t.ap(), src[c * 128:(c + 1) * 128, :]),
                reads=[sbf], writes=[xsb[c].buf], dma=True)

    vtd, vtb = P.dram[cfg["vt"]], P.dbuf[cfg["vt"]]
    vC, vCb = P.cache(cfg["v"])
    vw = [P.ab(1024, f"pv{c}") for c in range(8)]
    for c in range(8):
        vwc = vw[c]
        s.op("sp", lambda e, c=c, t=vwc: e.dma_start(
            t.ap(), vC[c * 128:(c + 1) * 128, :]),
            reads=[vCb], writes=[vwc.buf], dma=True)
    onesr = P.ab(144, "p_onesr")
    load_const(P, "ones_row_bf", onesr, 1, 128)
    vbrow = None
    if cfg["vb"] is not None:
        vbrow = P.ab(1024, "p_vbrow")
        vbsrc = P.wext[cfg["vb"]]
        s.op("pool", lambda e: e.dma_start(
            vbrow.ap(slice(0, 1), slice(0, 1024)), vbsrc.ap()),
            reads=[P.dbuf["w_" + cfg["vb"]]], writes=[vbrow.buf], dma=True)
    vob = [P.ab(1024, f"pvo{i}") for i in range(2)]
    for tt in range(32):
        ob = vob[tt % 2]
        banks = ((tt % 2) * 2, (tt % 2) * 2 + 1)

        def mkv(tt=tt, banks=banks):
            def f(e):
                last = None
                for hf in range(2):
                    for c in range(8):
                        last = e.matmul(
                            P.pb(banks[hf]),
                            xsb[c].ap(None, slice(tt * 128, (tt + 1) * 128)),
                            vw[c].ap(None, slice(hf * 512, (hf + 1) * 512)),
                            start=(c == 0), stop=(c == 7 and vbrow is None))
                    if vbrow is not None:
                        last = e.matmul(
                            P.pb(banks[hf]), onesr.ap(slice(0, 1), slice(0, 128)),
                            vbrow.ap(slice(0, 1), slice(hf * 512, (hf + 1) * 512)),
                            start=False, stop=True)
                return last
            return f
        rd = [t.buf for t in xsb] + [t.buf for t in vw] + [onesr.buf]
        if vbrow is not None:
            rd.append(vbrow.buf)
        s.op("pe", mkv(), reads=rd, writes=[P.pbk[banks[0]], P.pbk[banks[1]]])
        for hf in range(2):
            s.op("act", lambda e, b=banks[hf], hf=hf, ob=ob: e.activation(
                ob.ap(None, slice(hf * 512, (hf + 1) * 512)), P.pb(b), AF.Copy),
                reads=[P.pbk[banks[hf]]],
                writes=[ob.buf] if hf == 0 else [], pwrites=[ob.buf] if hf else [])
        s.op("sp", lambda e, tt=tt, ob=ob: e.dma_start(
            vtd[tt * 128:(tt + 1) * 128, :], ob.ap()),
            reads=[ob.buf], pwrites=[vtb], dma=True)

    qkC, qkCb = P.cache(cfg["qk"])
    qkbv = None
    if cfg["qkb"] is not None:
        qkbv = P.af(16, "p_qkb")
        load_weight_f32(P, cfg["qkb"], qkbv, cols=16)
    wq = [P.ab(1024, f"pwq{i}") for i in range(2)]
    if cfg["kind"] == "softmax" and not cfg["rope"]:
        qf = [P.ab(4096, f"pqf{i}") for i in range(2)]
    else:
        qf = [P.af(4096, f"pqf{i}") for i in range(2)]
    qkfd, qkfb = P.dram[cfg["qkf"]], P.dbuf[cfg["qkf"]]
    for h in range(H):
        wqs, qfs = wq[h % 2], qf[h % 2]
        s.op("sp", lambda e, h=h, t=wqs: e.dma_start(
            t.ap().rearrange("p (k o) -> p k o", o=128),
            qkC[h].rearrange("(k p) o -> p k o", p=128)),
            reads=[qkCb], writes=[wqs.buf], dma=True)
        for tc in range(NTC):
            bank = 4 + (tc % 4)

            def mkp(wqs=wqs, tc=tc, bank=bank):
                def f(e):
                    last = None
                    for c in range(8):
                        last = e.matmul(
                            P.pb(bank), wqs.ap(None, slice(c * 128, (c + 1) * 128)),
                            xsb[c].ap(None, slice(tc * 512, (tc + 1) * 512)),
                            start=(c == 0), stop=(c == 7))
                    return last
                return f
            s.op("pe", mkp(), reads=[wqs.buf] + [t.buf for t in xsb],
                 writes=[P.pbk[bank]])
            if cfg["kind"] == "softmax" and not cfg["rope"]:
                s.op("act", lambda e, tc=tc, bank=bank, qfs=qfs, h=h: e.activation(
                    qfs.ap(slice(64, 128), slice(tc * 512, (tc + 1) * 512)),
                    P.psum[64:128, bank * 512:(bank + 1) * 512], AF.Identity,
                    bias=qkbv.ap(slice(64, 128), slice(h, h + 1))),
                    reads=[P.pbk[bank], qkbv.buf],
                    writes=[qfs.buf] if tc == 0 else [],
                    pwrites=[qfs.buf] if tc else [])
                s.op("dve", lambda e, tc=tc, bank=bank, qfs=qfs, h=h:
                     e.tensor_scalar(
                         qfs.ap(slice(0, 64), slice(tc * 512, (tc + 1) * 512)),
                         P.psum[0:64, bank * 512:(bank + 1) * 512],
                         qkbv.ap(slice(0, 64), slice(h, h + 1)), 0.125,
                         OP.add, OP.mult),
                     reads=[P.pbk[bank], qkbv.buf],
                     writes=[qfs.buf] if tc == 0 else [],
                     pwrites=[qfs.buf] if tc else [])
            elif qkbv is not None:
                s.op("dve", lambda e, tc=tc, bank=bank, qfs=qfs, h=h:
                     e.tensor_scalar_add(
                         qfs.ap(None, slice(tc * 512, (tc + 1) * 512)), P.pb(bank),
                         qkbv.ap(None, slice(h, h + 1))),
                     reads=[P.pbk[bank], qkbv.buf],
                     writes=[qfs.buf] if tc == 0 else [],
                     pwrites=[qfs.buf] if tc else [])
            else:
                s.op("act", lambda e, tc=tc, bank=bank, qfs=qfs: e.activation(
                    qfs.ap(None, slice(tc * 512, (tc + 1) * 512)), P.pb(bank), AF.Copy),
                    reads=[P.pbk[bank]],
                    writes=[qfs.buf] if tc == 0 else [],
                    pwrites=[qfs.buf] if tc else [])
        s.op("sp", lambda e, h=h, qfs=qfs: e.dma_start(
            qkfd[h * 128:(h + 1) * 128, :], qfs.ap()),
            reads=[qfs.buf], pwrites=[qkfb], dma=True)


RETH_STAGE = 99
RETH_GN6 = 3


def attn_heads(P, cfg):
    s = P.s
    decay = cfg["kind"] == "decay"
    soft = not decay
    causal = cfg["causal"]
    rope = cfg["rope"]

    onesf = P.af(144, "h_onesf")
    for r0 in (0, 64):
        s.op("sp", lambda e, r0=r0: e.dma_start(
            onesf.ap(slice(r0, r0 + 1), slice(0, 128)), P.ct["ones_f"].ap()),
            reads=[P.cbuf["ones_f"]], pwrites=[onesf.buf], dma=True)
    ones_b = P.ab(16, "h_onesb")
    load_const(P, "ones_bf", ones_b, 128, 1)
    if rope:
        tabC = P.af(4096, "h_tabC")
        tabS = P.af(4096, "h_tabS")
        for tab, top, bot in ((tabC, "cos8", "cosT"), (tabS, "sin8", "sinS")):
            s.op("sp", lambda e, tab=tab, top=top: e.dma_start(
                tab.ap(slice(0, 64), slice(0, T)), P.ct[top].ap()),
                reads=[P.cbuf[top]], pwrites=[tab.buf], dma=True)
            s.op("sp", lambda e, tab=tab, bot=bot: e.dma_start(
                tab.ap(slice(64, 128), slice(0, T)), P.ct[bot].ap()),
                reads=[P.cbuf[bot]], pwrites=[tab.buf], dma=True)
    if decay:
        rf = P.af(560, "h_rf")
        load_const(P, "rf", rf, 128, 560)
        colf = [P.af(512, f"h_cf{i}") for i in range(2)]
        gv = P.af(16, "h_gnw")
        gb = P.af(16, "h_gnb")
        load_weight_f32(P, "gnw", gv, rows=64, cols=16)
        load_weight_f32(P, "gnb", gb, rows=64, cols=16)
    if causal:
        msk = P.ab(2048, "h_masks")
        load_const(P, "masks", msk, 128, 2048)

    QKF = [P.af(4096, f"h_qkf{i}") for i in range(2)]
    SW = [P.af(4096, f"h_sw{i}") for i in range(2)]
    SWb = []
    for i in range(2):
        b = SBuf(P.ar.t, SW[i].c0, 4096, BF16, f"h_swb{i}")
        b.buf = SW[i].buf
        SWb.append(b)
    QA = [P.ab(4096, f"h_qa{i}") for i in range(2)]
    KA = [P.ab(4096, f"h_ka{i}") for i in range(2)]
    npt = 4
    pt = [P.ab(1024 if soft else 512, f"h_pt{i}") for i in range(npt)]
    vst = 65 if soft else 64
    Vh = [P.ab(32 * vst, f"h_vh{i}") for i in range(2)]
    rows = P.af(4096, "h_rows")
    att = P.ab(4096, "h_att")
    rob = [P.ab(512, f"h_rob{i}") for i in range(2)]
    rosq = [P.ab(512, f"h_rosq{i}") for i in range(2)]

    hdD, hdB = P.dram[cfg["hd"]], P.dbuf[cfg["hd"]]
    qkfD, qkfB = P.dram[cfg["qkf"]], P.dbuf[cfg["qkf"]]
    vtD, vtB = P.dram[cfg["vt"]], P.dbuf[cfg["vt"]]

    if soft:
        for i in range(2):
            s.op("dve", lambda e, i=i: e.memset(KA[i].ap(slice(64, 65)), 1.0),
                 pwrites=[KA[i].buf])
            s.op("dve", lambda e, i=i: e.memset(
                P.ar.t[0:128, Vh[i].c0:Vh[i].c0 + 32 * 65].rearrange(
                    "p (kt d) -> p kt d", d=65)[:, :, 64:65], 1.0),
                 pwrites=[Vh[i].buf])

    def prep_early(h):
        p2 = h % 2
        qkf, qa, ka, vh, sw, swb = QKF[p2], QA[p2], KA[p2], Vh[p2], SW[p2], SWb[p2]
        if not (soft and not rope):
            s.op("sp", lambda e, h=h, qkf=qkf: e.dma_start(
                qkf.ap(), qkfD[h * 128:(h + 1) * 128, :]),
                reads=[qkfB], writes=[qkf.buf], dma=True)
        vdst = P.ar.t[0:128, vh.c0:vh.c0 + 32 * vst].rearrange(
            "p (kt d) -> p kt d", d=vst)[:, :, 0:64]
        for part in range(4):
            ks = slice(part * 8, (part + 1) * 8)
            s.op("sp", lambda e, h=h, ks=ks, part=part, vdst=vdst, vh=vh: e.dma_start(
                vdst[:, ks, :],
                vtD[part * 1024:(part + 1) * 1024, h * 64:(h + 1) * 64]
                .rearrange("(kt p) d -> p kt d", p=128)),
                reads=[vtB], pwrites=[vh.buf], dma=True)
        if decay:
            cf = colf[p2]
            s.op("sp", lambda e, h=h, cf=cf: e.dma_start(
                cf.ap(slice(0, 64), slice(0, 512)),
                P.ct["colf"][h * 64:(h + 1) * 64, :]),
                reads=[P.cbuf["colf"]], writes=[cf.buf], dma=True)
        if rope:
            for (d0, s0) in ((0, 32), (32, 0), (64, 96), (96, 64)):
                s.op("sp", lambda e, d0=d0, s0=s0, qkf=qkf, sw=sw: e.dma_start(
                    sw.ap(slice(d0, d0 + 32)), qkf.ap(slice(s0, s0 + 32))),
                    reads=[qkf.buf], pwrites=[sw.buf], dma=True)
            s.op("pool", lambda e, qkf=qkf: e.tensor_mul(qkf.ap(), qkf.ap(), tabC.ap()),
                 reads=[qkf.buf, tabC.buf], pwrites=[qkf.buf])
            s.op("pool", lambda e, sw=sw: e.tensor_mul(sw.ap(), sw.ap(), tabS.ap()),
                 reads=[sw.buf, tabS.buf], pwrites=[sw.buf])
            s.op("pool", lambda e, qkf=qkf, sw=sw: e.tensor_add(
                qkf.ap(), qkf.ap(), sw.ap()),
                reads=[qkf.buf, sw.buf], pwrites=[qkf.buf])
        if soft and not rope:
            s.op("sp", lambda e, h=h, qa=qa: e.dma_start(
                qa.ap(slice(0, 64)), qkfD[h * 128:h * 128 + 64, :]),
                reads=[qkfB], pwrites=[qa.buf], dma=True)
            s.op("sp", lambda e, h=h, ka=ka: e.dma_start(
                ka.ap(slice(0, 64)), qkfD[h * 128 + 64:(h + 1) * 128, :]),
                reads=[qkfB], pwrites=[ka.buf], dma=True)
            return
        if decay:
            cf = colf[p2]
            for tc in range(NTC):
                s.op("pool", lambda e, tc=tc, qa=qa, qkf=qkf, cf=cf: e.tensor_mul(
                    qa.ap(slice(0, 64), slice(tc * 512, (tc + 1) * 512)),
                    qkf.ap(slice(0, 64), slice(tc * 512, (tc + 1) * 512)),
                    cf.ap(slice(0, 64), slice(0, 512))),
                    reads=[qkf.buf, cf.buf],
                    writes=[qa.buf] if tc == 0 else [], pwrites=[qa.buf] if tc else [])
        elif rope:
            s.op("pool", lambda e, qa=qa, qkf=qkf: e.tensor_copy(
                qa.ap(slice(0, 64)), qkf.ap(slice(0, 64))),
                reads=[qkf.buf], writes=[qa.buf])
        else:
            s.op("pool", lambda e, qa=qa, qkf=qkf: e.tensor_scalar_mul(
                qa.ap(slice(0, 64)), qkf.ap(slice(0, 64)), 0.125),
                reads=[qkf.buf], writes=[qa.buf])
        s.op("pool", lambda e, qkf=qkf, swb=swb: e.tensor_copy(
            swb.ap(slice(64, 128)), qkf.ap(slice(64, 128))),
            reads=[qkf.buf], pwrites=[swb.buf])
        s.op("sp", lambda e, ka=ka, swb=swb: e.dma_start(
            ka.ap(slice(0, 64)), swb.ap(slice(64, 128))),
            reads=[swb.buf], pwrites=[ka.buf], dma=True)

    def prep_late(h):
        if not soft:
            return
        p2 = h % 2
        qa, ka, swb = QA[p2], KA[p2], SWb[p2]
        po = p2 * 2048
        r64 = slice(64, 65)
        s.op("act", lambda e, ka=ka, swb=swb: e.activation(
            swb.ap(slice(0, 64)), ka.ap(slice(0, 64)), AF.Square),
            reads=[ka.buf], pwrites=[swb.buf])
        for tc in range(NTC):
            s.op("pe", lambda e, tc=tc, swb=swb: e.matmul(
                P.psum[64:65, 3584:4096], ones_b.ap(slice(0, 64), slice(0, 1)),
                swb.ap(slice(0, 64), slice(tc * 512, (tc + 1) * 512)),
                start=True, stop=True, tile_position=(0, 64)),
                reads=[swb.buf, ones_b.buf], writes=[P.pbk[7]])
            s.op("dve", lambda e, tc=tc: e.reduce_max(
                rows.ap(r64, slice(po + tc, po + tc + 1)),
                P.psum[64:65, 3584:4096], AX),
                reads=[P.pbk[7]], pwrites=[rows.buf])
        s.op("dve", lambda e: e.reduce_max(
            rows.ap(r64, slice(po + 8, po + 9)), rows.ap(r64, slice(po, po + 8)), AX),
            reads=[rows.buf], pwrites=[rows.buf])
        s.op("act", lambda e: e.activation(
            rows.ap(r64, slice(po + 9, po + 10)), rows.ap(r64, slice(po + 8, po + 9)),
            AF.Ln),
            reads=[rows.buf], pwrites=[rows.buf])
        s.op("act", lambda e, qa=qa, swb=swb: e.activation(
            swb.ap(slice(0, 64)), qa.ap(slice(0, 64)), AF.Square),
            reads=[qa.buf], pwrites=[swb.buf])
        for tc in range(NTC):
            s.op("pe", lambda e, tc=tc, swb=swb: e.matmul(
                P.psum[64:65, 3584:4096], ones_b.ap(slice(0, 64), slice(0, 1)),
                swb.ap(slice(0, 64), slice(tc * 512, (tc + 1) * 512)),
                start=True, stop=True, tile_position=(0, 64)),
                reads=[swb.buf, ones_b.buf], writes=[P.pbk[7]])
            s.op("act", lambda e, tc=tc: e.activation(
                rows.ap(r64, slice(po + 512, po + 1024)),
                P.psum[64:65, 3584:4096], AF.Ln),
                reads=[P.pbk[7]], pwrites=[rows.buf])
            s.op("dve", lambda e, tc=tc: e.tensor_scalar_add(
                rows.ap(r64, slice(po + 1024, po + 1536)),
                rows.ap(r64, slice(po + 512, po + 1024)),
                rows.ap(r64, slice(po + 9, po + 10))),
                reads=[rows.buf], pwrites=[rows.buf])
            s.op("act", lambda e, tc=tc: e.activation(
                rows.ap(r64, slice(po + 512, po + 1024)),
                rows.ap(r64, slice(po + 1024, po + 1536)), AF.Exp, scale=0.5),
                reads=[rows.buf], pwrites=[rows.buf])
            s.op("dve", lambda e, tc=tc, qa=qa: e.tensor_scalar_mul(
                qa.ap(r64, slice(tc * 512, (tc + 1) * 512)),
                rows.ap(r64, slice(po + 512, po + 1024)), -1.0),
                reads=[rows.buf], pwrites=[qa.buf])

    prep_early(0)
    prep_late(0)

    for h in range(H):
        p2 = h % 2
        qkf, qa, ka, vh = QKF[p2], QA[p2], KA[p2], Vh[p2]
        sw, swb = SW[p2], SWb[p2]
        po = p2 * 2048

        pend = []

        def flush_pend(pend=pend):
            while pend:
                pend.pop(0)()

        if soft:
            tiles = []
            for qc in range(NTC):
                ng = (2 * qc + 2) if causal else 16
                for g in range(ng):
                    tiles.append((qc, g, ng))

            def em_qk(i, qa=qa, ka=ka):
                qc, g, ng = tiles[i]
                b0 = (i % 3) * 2
                qs = slice(qc * 512, (qc + 1) * 512)

                def mmst(e, g=g, b0=b0, qs=qs):
                    last = None
                    for j in range(2):
                        kt = 2 * g + j
                        last = e.matmul(
                            P.pb(b0 + j),
                            ka.ap(slice(0, 65), slice(kt * 128, (kt + 1) * 128)),
                            qa.ap(slice(0, 65), qs),
                            start=True, stop=True)
                    return last
                s.op("pe", mmst, reads=[qa.buf, ka.buf],
                     writes=[P.pbk[b0], P.pbk[b0 + 1]])

            def em_exp(i):
                qc, g, ng = tiles[i]
                b0 = (i % 3) * 2
                ptg = pt[i % 3]
                s.op("act", lambda e, b0=b0, ptg=ptg: e.activation(
                    ptg.ap(), P.pb(b0, 2), AF.Exp),
                    reads=[P.pbk[b0], P.pbk[b0 + 1]], writes=[ptg.buf])
                if causal:
                    for j in range(2):
                        kt = 2 * g + j
                        if kt >= 4 * qc:
                            mi = 3 - (kt - 4 * qc)
                            s.op("dve", lambda e, j=j, mi=mi, ptg=ptg: e.tensor_mul(
                                ptg.ap(None, slice(j * 512, (j + 1) * 512)),
                                ptg.ap(None, slice(j * 512, (j + 1) * 512)),
                                msk.ap(None, slice(mi * 512, (mi + 1) * 512))),
                                reads=[ptg.buf, msk.buf], pwrites=[ptg.buf])

            def em_pv(i, vh=vh):
                qc, g, ng = tiles[i]
                ptg = pt[i % 3]

                def mmpv(e, g=g, ng=ng, ptg=ptg):
                    last = None
                    for j in range(2):
                        kt = 2 * g + j
                        last = e.matmul(
                            P.psum[0:65, 3072:3584],
                            vh.ap(None, slice(kt * 65, kt * 65 + 65)),
                            ptg.ap(None, slice(j * 512, (j + 1) * 512)),
                            start=(kt == 0), stop=(kt == 2 * ng - 1))
                    return last
                s.op("pe", mmpv, reads=[ptg.buf, vh.buf],
                     writes=[P.pbk[6]] if g == 0 else [],
                     pwrites=[P.pbk[6]] if g else [])

            def mk_epi(qc, sw=sw, po=po):
                qs = slice(qc * 512, (qc + 1) * 512)
                s.op("act", lambda e, qs=qs, sw=sw: e.activation(
                    sw.ap(slice(0, 65), qs), P.psum[0:65, 3072:3584], AF.Copy),
                    reads=[P.pbk[6]], pwrites=[sw.buf])

                def rest(qc=qc, qs=qs, sw=sw, po=po):
                    r64 = slice(64, 65)
                    s.op("act", lambda e, qs=qs, sw=sw, po=po: e.activation(
                        rows.ap(r64, slice(po + 1536, po + 2048)),
                        sw.ap(slice(64, 65), qs), AF.Ln),
                        reads=[sw.buf], pwrites=[rows.buf])
                    s.op("act", lambda e, po=po: e.activation(
                        rows.ap(r64, slice(po + 1024, po + 1536)),
                        rows.ap(r64, slice(po + 1536, po + 2048)), AF.Exp,
                        scale=-1.0),
                        reads=[rows.buf], pwrites=[rows.buf])
                    s.op("pe", lambda e, po=po: e.matmul(
                        P.pb(7, rows=slice(0, 64)),
                        onesf.ap(slice(64, 65), slice(0, 64)),
                        rows.ap(r64, slice(po + 1024, po + 1536)),
                        start=True, stop=True),
                        reads=[rows.buf, onesf.buf], writes=[P.pbk[7]])
                    s.op("dve", lambda e, qs=qs, sw=sw: e.tensor_mul(
                        att.ap(slice(0, 64), qs), sw.ap(slice(0, 64), qs),
                        P.pb(7, rows=slice(0, 64))),
                        reads=[sw.buf, P.pbk[7]],
                        writes=[att.buf] if qc == 0 else [],
                        pwrites=[att.buf] if qc else [])
                pend.append(rest)

            LA = 3
            for j in range(min(LA, len(tiles))):
                em_qk(j)
            for i, (qc, g, ng) in enumerate(tiles):
                em_exp(i)
                if i + LA < len(tiles):
                    em_qk(i + LA)
                em_pv(i)
                if g == min(1, ng - 1) and pend:
                    flush_pend()
                if g == ng - 1:
                    mk_epi(qc)
                if i == max(0, len(tiles) // 8) and h + 1 < H:
                    prep_early(h + 1)
                if i == (len(tiles) * 3) // 4 and h + 1 < H:
                    prep_late(h + 1)
        else:
            tiles = []
            for qc in range(NTC):
                nkt = 4 * qc + 4
                for kt in range(nkt):
                    tiles.append((qc, kt, nkt))

            def em_st(i, qa=qa, ka=ka):
                qc, kt, nkt = tiles[i]
                bank = i % 4
                qs = slice(qc * 512, (qc + 1) * 512)
                s.op("pe", lambda e, kt=kt, bank=bank, qs=qs: e.matmul(
                    P.pb(bank), ka.ap(slice(0, 64), slice(kt * 128, (kt + 1) * 128)),
                    qa.ap(slice(0, 64), qs), start=True, stop=True),
                    reads=[qa.buf, ka.buf], writes=[P.pbk[bank]])

            def em_evac(i, h=h):
                qc, kt, nkt = tiles[i]
                bank = i % 4
                ptg = pt[i % 4]
                d0i = (qc * 512 - kt * 128 + 384) // 128
                rfap = rf.ap(None, slice(h * 35 + d0i, h * 35 + d0i + 1))
                if kt >= 4 * qc:
                    mi = 3 - (kt - 4 * qc)
                    s.op("dve", lambda e, bank=bank, ptg=ptg, rfap=rfap, mi=mi:
                         e.scalar_tensor_tensor(
                             ptg.ap(), P.pb(bank), rfap,
                             msk.ap(None, slice(mi * 512, (mi + 1) * 512)),
                             OP.mult, OP.mult),
                         reads=[P.pbk[bank], rf.buf, msk.buf], writes=[ptg.buf])
                elif kt % 2 == 0:
                    s.op("act", lambda e, bank=bank, ptg=ptg, rfap=rfap:
                         e.activation(ptg.ap(), P.pb(bank), AF.Copy, scale=rfap),
                         reads=[P.pbk[bank], rf.buf], writes=[ptg.buf])
                else:
                    s.op("dve", lambda e, bank=bank, ptg=ptg, rfap=rfap:
                         e.tensor_scalar_mul(ptg.ap(), P.pb(bank), rfap),
                         reads=[P.pbk[bank], rf.buf], writes=[ptg.buf])

            def em_pvd(i, vh=vh):
                qc, kt, nkt = tiles[i]
                ptg = pt[i % 4]
                s.op("pe", lambda e, kt=kt, nkt=nkt, vh=vh, ptg=ptg: e.matmul(
                    P.psum[0:64, 3072:3584],
                    vh.ap(None, slice(kt * 64, (kt + 1) * 64)),
                    ptg.ap(), start=(kt == 0), stop=(kt == nkt - 1)),
                    reads=[ptg.buf, vh.buf],
                    writes=[P.pbk[6]] if kt == 0 else [],
                    pwrites=[P.pbk[6]] if kt else [])

            def mk_epi(qc, h=h, sw=sw):
                qs = slice(qc * 512, (qc + 1) * 512)
                robs, rosqs = rob[qc % 2], rosq[qc % 2]
                s.op("act", lambda e, qs=qs, sw=sw: e.activation(
                    sw.ap(slice(0, 64), qs), P.psum[0:64, 3072:3584], AF.Copy),
                    reads=[P.pbk[6]], pwrites=[sw.buf])

                def rest(qc=qc, qs=qs, h=h, sw=sw, robs=robs, rosqs=rosqs):
                    r1 = slice(0, 1)
                    s.op("dve", lambda e, qs=qs, robs=robs, sw=sw: e.tensor_copy(
                        robs.ap(slice(0, 64)), sw.ap(slice(0, 64), qs)),
                        reads=[sw.buf], writes=[robs.buf])
                    s.op("act", lambda e, qs=qs, rosqs=rosqs, sw=sw: e.activation(
                        rosqs.ap(slice(0, 64)), sw.ap(slice(0, 64), qs), AF.Square),
                        reads=[sw.buf], writes=[rosqs.buf])
                    s.op("pe", lambda e, robs=robs: e.matmul(
                        P.psum[0:1, 3584:4096], ones_b.ap(slice(0, 64), slice(0, 1)),
                        robs.ap(slice(0, 64)), start=True, stop=True),
                        reads=[robs.buf, ones_b.buf], writes=[P.pbk[7]])
                    s.op("pe", lambda e, rosqs=rosqs: e.matmul(
                        P.psum[0:1, 2560:3072], ones_b.ap(slice(0, 64), slice(0, 1)),
                        rosqs.ap(slice(0, 64)), start=True, stop=True),
                        reads=[rosqs.buf, ones_b.buf], writes=[P.pbk[5]])
                    s.op("act", lambda e: e.activation(
                        rows.ap(r1, slice(0, 512)), P.psum[0:1, 3584:4096],
                        AF.Copy, scale=1.0 / 64),
                        reads=[P.pbk[7]], pwrites=[rows.buf])
                    s.op("act", lambda e: e.activation(
                        rows.ap(r1, slice(512, 1024)), rows.ap(r1, slice(0, 512)),
                        AF.Square),
                        reads=[rows.buf], pwrites=[rows.buf])
                    s.op("dve", lambda e: e.scalar_tensor_tensor(
                        rows.ap(r1, slice(1024, 1536)), P.psum[0:1, 2560:3072],
                        1.0 / 64, rows.ap(r1, slice(512, 1024)),
                        OP.mult, OP.subtract),
                        reads=[P.pbk[5], rows.buf], pwrites=[rows.buf])
                    s.op("act", lambda e: e.activation(
                        rows.ap(r1, slice(1536, 2048)), rows.ap(r1, slice(1024, 1536)),
                        AF.Ln, bias=1e-5),
                        reads=[rows.buf], pwrites=[rows.buf])
                    s.op("act", lambda e: e.activation(
                        rows.ap(r1, slice(2048, 2560)), rows.ap(r1, slice(1536, 2048)),
                        AF.Exp, scale=-0.5),
                        reads=[rows.buf], pwrites=[rows.buf])
                    s.op("pe", lambda e: e.matmul(
                        P.pb(4, rows=slice(0, 64)), onesf.ap(slice(0, 1), slice(0, 64)),
                        rows.ap(r1, slice(0, 512)), start=True, stop=True),
                        reads=[rows.buf, onesf.buf], writes=[P.pbk[4]])
                    s.op("pe", lambda e: e.matmul(
                        P.pb(5, rows=slice(0, 64)), onesf.ap(slice(0, 1), slice(0, 64)),
                        rows.ap(r1, slice(2048, 2560)), start=True, stop=True),
                        reads=[rows.buf, onesf.buf], writes=[P.pbk[5]])
                    s.op("dve", lambda e, qs=qs, sw=sw: e.tensor_sub(
                        sw.ap(slice(0, 64), qs), sw.ap(slice(0, 64), qs),
                        P.pb(4, rows=slice(0, 64))),
                        reads=[sw.buf, P.pbk[4]], pwrites=[sw.buf])
                    s.op("dve", lambda e, qs=qs, sw=sw: e.tensor_mul(
                        sw.ap(slice(0, 64), qs), sw.ap(slice(0, 64), qs),
                        P.pb(5, rows=slice(0, 64))),
                        reads=[sw.buf, P.pbk[5]], pwrites=[sw.buf])
                    s.op("dve", lambda e, qs=qs, h=h, sw=sw: e.tensor_scalar(
                        att.ap(slice(0, 64), qs), sw.ap(slice(0, 64), qs),
                        gv.ap(slice(0, 64), slice(h, h + 1)),
                        gb.ap(slice(0, 64), slice(h, h + 1)), OP.mult, OP.add),
                        reads=[sw.buf, gv.buf, gb.buf],
                        writes=[att.buf] if qc == 0 else [],
                        pwrites=[att.buf] if qc else [])
                pend.append(rest)

            LA = 4
            for j in range(min(LA, len(tiles))):
                em_st(j)
            for i, (qc, kt, nkt) in enumerate(tiles):
                em_evac(i)
                if i + LA < len(tiles):
                    em_st(i + LA)
                em_pvd(i)
                if kt == min(1, nkt - 1) and pend:
                    flush_pend()
                if kt == nkt - 1:
                    mk_epi(qc)
                if i == max(0, len(tiles) // 8) and h + 1 < H:
                    prep_early(h + 1)
                if i == (len(tiles) * 3) // 4 and h + 1 < H:
                    prep_late(h + 1)
        flush_pend()
        s.op("sp", lambda e, h=h: e.dma_start(
            hdD[h * 64:(h + 1) * 64, :], att.ap(slice(0, 64))),
            reads=[att.buf], pwrites=[hdB], dma=True)


def attn_out(P, cfg):
    s = P.s
    ostats = cfg.get("out_stats", False)
    hdD, hdB = P.dram[cfg["hd"]], P.dbuf[cfg["hd"]]
    resD, resB = P.dram[cfg["res"]], P.dbuf[cfg["res"]]
    dstD, dstB = P.dram[cfg["dst"]], P.dbuf[cfg["dst"]]
    wC, wCb = P.cache(cfg["ow"])
    hd = [P.ab(4096, f"o_hd{c}") for c in range(8)]
    ww = [P.ab(1024, f"o_w{c}") for c in range(8)]
    for c in range(8):
        hdc, wwc = hd[c], ww[c]
        s.op("sp", lambda e, c=c, t=hdc: e.dma_start(
            t.ap(), hdD[c * 128:(c + 1) * 128, :]),
            reads=[hdB], writes=[hdc.buf], dma=True)
        s.op("sp", lambda e, c=c, t=wwc: e.dma_start(
            t.ap(), wC[c * 128:(c + 1) * 128, :]),
            reads=[wCb], writes=[wwc.buf], dma=True)
    obv = None
    if cfg["ob"] is not None:
        obv = P.af(16, "o_b")
        load_weight_f32(P, cfg["ob"], obv, cols=8)
    xr = [P.af(4096, f"o_x{i}") for i in range(2)]
    ou = [P.af(4096, f"o_o{i}") for i in range(2)]
    if ostats:
        sqb = [P.ab(512, f"o_s{i}") for i in range(2)]
        ones_b = P.ab(16, "o_onesb")
        load_const(P, "ones_bf", ones_b, 128, 1)
    for o in range(8):
        xrs, ous = xr[o % 2], ou[o % 2]
        s.op("sp", lambda e, o=o, xrs=xrs: e.dma_start(
            xrs.ap(), resD[o * 128:(o + 1) * 128, :]),
            reads=[resB], writes=[xrs.buf], dma=True)
        for tc in range(NTC):
            bank = tc % 4

            def mko(o=o, tc=tc, bank=bank):
                def f(e):
                    last = None
                    for c in range(8):
                        last = e.matmul(
                            P.pb(bank), ww[c].ap(None, slice(o * 128, (o + 1) * 128)),
                            hd[c].ap(None, slice(tc * 512, (tc + 1) * 512)),
                            start=(c == 0), stop=(c == 7))
                    return last
                return f
            s.op("pe", mko(), reads=[t.buf for t in ww] + [t.buf for t in hd],
                 writes=[P.pbk[bank]])
            bias = obv.ap(None, slice(o, o + 1)) if obv is not None else 0.0
            s.op("dve", lambda e, tc=tc, bank=bank, ous=ous, xrs=xrs, bias=bias:
                 e.scalar_tensor_tensor(
                     ous.ap(None, slice(tc * 512, (tc + 1) * 512)), P.pb(bank), bias,
                     xrs.ap(None, slice(tc * 512, (tc + 1) * 512)), OP.add, OP.add),
                 reads=[P.pbk[bank], xrs.buf] + ([obv.buf] if obv is not None else []),
                 writes=[ous.buf] if tc == 0 else [], pwrites=[ous.buf] if tc else [])
            if ostats:
                sb2 = sqb[tc % 2]
                sr = 32 * (tc // 4)
                sbk = 4 + (tc % 4)
                s.op("act", lambda e, tc=tc, ous=ous, sb2=sb2: e.activation(
                    sb2.ap(), ous.ap(None, slice(tc * 512, (tc + 1) * 512)), AF.Square),
                    reads=[ous.buf], writes=[sb2.buf])
                s.op("pe", lambda e, o=o, sb2=sb2, sr=sr, sbk=sbk: e.matmul(
                    P.psum[sr:sr + 1, sbk * 512:(sbk + 1) * 512],
                    ones_b.ap(slice(0, 128), slice(0, 1)), sb2.ap(),
                    start=(o == 0), stop=(o == 7), tile_position=(0, sr)),
                    reads=[sb2.buf, ones_b.buf],
                    writes=[P.pbk[sbk]] if o == 0 and tc < 4 else [],
                    pwrites=[P.pbk[sbk]] if not (o == 0 and tc < 4) else [])
        s.op("sp", lambda e, o=o, ous=ous: e.dma_start(
            dstD[o * 128:(o + 1) * 128, :], ous.ap()),
            reads=[ous.buf], pwrites=[dstB], dma=True)


def ph_moe(P):
    s = P.s
    x1D, x1B = P.dram["x1T"], P.dbuf["x1T"]
    paD, paB = P.dram["partialT"], P.dbuf["partialT"]
    y0D, y0B = P.dram["y0T"], P.dbuf["y0T"]
    gC, gCb = P.cache("gateT")
    w1C, w1Cb = P.cache("moe_w1")
    w2C, w2Cb = P.cache("moe_w2")
    w3C, w3Cb = P.cache("moe_w3")
    b3C, b3Cb = P.cache("moe_b3")

    n2 = P.af(16, "m_n2")
    load_weight_f32(P, "norm2w", n2, cols=8)
    b1v = P.af(144, "m_b1")
    b2v = P.af(144, "m_b2")
    for vec, wn in ((b1v, "moe_b1"), (b2v, "moe_b2")):
        src = P.wext[wn]
        s.op("sp", lambda e, vec=vec, src=src: e.dma_start(
            vec.ap(None, slice(0, 128)).rearrange("p (e k) -> p e k", k=16),
            src.rearrange("e p k -> p e k")),
            reads=[P.dbuf["w_" + wn]], writes=[vec.buf], dma=True)
    onesf = P.af(144, "m_onesf")
    for r0 in (0, 32):
        s.op("sp", lambda e, r0=r0: e.dma_start(
            onesf.ap(slice(r0, r0 + 1), slice(0, 128)), P.ct["ones_f"].ap()),
            reads=[P.cbuf["ones_f"]], pwrites=[onesf.buf], dma=True)
    ones_b = P.ab(16, "m_onesb")
    load_const(P, "ones_bf", ones_b, 128, 1)
    ident = P.ab(128, "m_ident")
    load_const(P, "ident", ident, 128, 128)
    sel8 = P.ab(1024, "m_sel8")
    load_const(P, "sel8", sel8, 8, 1024)
    gsb = P.ab(64, "m_gate")
    s.op("sp", lambda e: e.dma_start(
        gsb.ap(None, slice(0, 64)).rearrange("p (c e) -> p c e", e=8),
        gC.rearrange("(c p) e -> p c e", p=128)),
        reads=[gCb], writes=[gsb.buf], dma=True)
    b3sb = P.ab(1024, "m_b3")
    s.op("sp", lambda e: e.dma_start(b3sb.ap(slice(0, 8)), b3C.ap()),
         reads=[b3Cb], writes=[b3sb.buf], dma=True)

    xin = [P.af(512, f"m_x{i}") for i in range(4)]
    sq = [P.ab(512, f"m_q{i}") for i in range(4)]
    r2 = [P.ab(512, f"m_r2{c}") for c in range(8)]
    acc = [P.af(512, f"m_ac{o}") for o in range(8)]
    lg = P.af(16, "m_lg")
    tp1 = P.af(16, "m_t1")
    tp2 = P.af(16, "m_t2")
    cmb = P.ab(16, "m_cmb")
    combT = P.ab(512, "m_combT")
    combB = P.ab(512, "m_combB")
    w1sb = [P.ab(16384, f"m_w1{i}") for i in range(2)]
    w2sb = [P.ab(16384, "m_w20")]
    w3r = [P.ab(1024, f"m_w3{i}") for i in range(2)]
    usil = P.af(512, "m_us")
    hsb = P.ab(8192, "m_h")
    t1s = P.af(512, "m_t1s")
    yo = [P.af(512, f"m_yo{i}") for i in range(2)]
    rows2 = P.af(4096, "m_rows")
    srow = SBuf(P.ar.t, hsb.c0, 4096, F32, "m_srow")
    srow.buf = hsb.buf

    assert MODE == "fused"
    r1 = slice(0, 1)
    for ch in range(8):
        r = 32 * (ch // 4)
        bank = 4 + (ch % 4)
        seg = slice(ch * 512, (ch + 1) * 512)
        s.op("act", lambda e, r=r, bank=bank, seg=seg: e.activation(
            srow.ap(slice(r, r + 1), seg),
            P.psum[r:r + 1, bank * 512:(bank + 1) * 512],
            AF.Ln, scale=1.0 / C, bias=1e-6),
            reads=[P.pbk[bank]],
            writes=[srow.buf] if ch == 0 else [], pwrites=[srow.buf] if ch else [])
        s.op("act", lambda e, r=r, seg=seg: e.activation(
            rows2.ap(slice(r, r + 1), seg), srow.ap(slice(r, r + 1), seg),
            AF.Exp, scale=-0.5),
            reads=[srow.buf],
            writes=[rows2.buf] if ch == 0 else [], pwrites=[rows2.buf] if ch else [])

    def ldw(slot, cache, cb, e):
        for c in range(8):
            s.op("sp", lambda e2, c=c, slot=slot, cache=cache, e=e: e2.dma_start(
                slot.ap(None, slice(c * 2048, (c + 1) * 2048)),
                cache[e, c * 128:(c + 1) * 128, :]),
                reads=[cb], writes=[slot.buf] if c == 0 else [],
                pwrites=[slot.buf] if c else [], dma=True)

    for ch in range(8):
        ts = slice(ch * 512, (ch + 1) * 512)
        rr = 32 * (ch // 4)
        s.op("pe", lambda e, ts=ts, rr=rr: e.matmul(
            P.pb(6), onesf.ap(slice(rr, rr + 1), slice(0, 128)),
            rows2.ap(slice(rr, rr + 1), ts), start=True, stop=True),
            reads=[rows2.buf, onesf.buf], writes=[P.pbk[6]])
        for c in range(8):
            xb, xb2 = xin[c % 2], xin[2 + c % 2]
            s.op("sp", lambda e, c=c, ts=ts, xb=xb: e.dma_start(
                xb.ap(), x1D[c * 128:(c + 1) * 128, ts]),
                reads=[x1B], writes=[xb.buf], dma=True)
            s.op("dve", lambda e, xb=xb, xb2=xb2: e.tensor_mul(
                xb2.ap(), xb.ap(), P.pb(6)),
                reads=[xb.buf, P.pbk[6]], writes=[xb2.buf])
            s.op("act", lambda e, c=c, xb2=xb2: e.activation(
                r2[c].ap(), xb2.ap(), AF.Copy,
                scale=n2.ap(None, slice(c, c + 1))),
                reads=[xb2.buf, n2.buf], writes=[r2[c].buf])
        for tt in range(4):
            def gmm(e, tt=tt):
                last = None
                for c in range(8):
                    last = e.matmul(P.psum[0:128, 3584:3592],
                                    r2[c].ap(None, slice(tt * 128, (tt + 1) * 128)),
                                    gsb.ap(None, slice(c * 8, (c + 1) * 8)),
                                    start=(c == 0), stop=(c == 7))
                return last
            s.op("pe", gmm, reads=[t.buf for t in r2] + [gsb.buf], writes=[P.pbk[7]])
            s.op("act", lambda e: e.activation(lg.ap(None, slice(0, 8)),
                                               P.psum[0:128, 3584:3592], AF.Copy),
                 reads=[P.pbk[7]], writes=[lg.buf])
            s.op("dve", lambda e: e.reduce_max(tp1.ap(None, slice(0, 1)),
                                               lg.ap(None, slice(0, 8)), AX),
                 reads=[lg.buf], writes=[tp1.buf])
            s.op("dve", lambda e: e.tensor_scalar(
                tp2.ap(None, slice(0, 8)), lg.ap(None, slice(0, 8)),
                tp1.ap(None, slice(0, 1)), -1e30, OP.is_equal, OP.mult),
                reads=[lg.buf, tp1.buf], writes=[tp2.buf])
            s.op("dve", lambda e: e.tensor_add(
                tp2.ap(None, slice(8, 16)), tp2.ap(None, slice(0, 8)),
                lg.ap(None, slice(0, 8))),
                reads=[tp2.buf, lg.buf], pwrites=[tp2.buf])
            s.op("dve", lambda e: e.reduce_max(tp1.ap(None, slice(1, 2)),
                                               tp2.ap(None, slice(8, 16)), AX),
                 reads=[tp2.buf], pwrites=[tp1.buf])
            s.op("dve", lambda e: e.tensor_scalar_mul(
                tp1.ap(None, slice(2, 3)), tp1.ap(None, slice(0, 1)), -1.0),
                reads=[tp1.buf], pwrites=[tp1.buf])
            s.op("act", lambda e: e.activation(
                tp1.ap(None, slice(3, 4)), tp1.ap(None, slice(1, 2)), AF.Exp,
                bias=tp1.ap(None, slice(2, 3))),
                reads=[tp1.buf], pwrites=[tp1.buf])
            s.op("dve", lambda e: e.tensor_scalar_add(
                tp1.ap(None, slice(4, 5)), tp1.ap(None, slice(3, 4)), 1.0),
                reads=[tp1.buf], pwrites=[tp1.buf])
            s.op("dve", lambda e: e.reciprocal(
                tp1.ap(None, slice(5, 6)), tp1.ap(None, slice(4, 5))),
                reads=[tp1.buf], pwrites=[tp1.buf])
            s.op("act", lambda e: e.activation(
                tp2.ap(None, slice(0, 8)), lg.ap(None, slice(0, 8)), AF.Exp,
                bias=tp1.ap(None, slice(2, 3))),
                reads=[lg.buf, tp1.buf], pwrites=[tp2.buf])
            s.op("dve", lambda e: e.tensor_scalar(
                tp2.ap(None, slice(8, 16)), lg.ap(None, slice(0, 8)),
                tp1.ap(None, slice(1, 2)), None, OP.is_ge),
                reads=[lg.buf, tp1.buf], pwrites=[tp2.buf])
            s.op("dve", lambda e: e.tensor_mul(
                tp2.ap(None, slice(0, 8)), tp2.ap(None, slice(0, 8)),
                tp2.ap(None, slice(8, 16))),
                reads=[tp2.buf], pwrites=[tp2.buf])
            s.op("dve", lambda e: e.tensor_scalar_mul(
                cmb.ap(None, slice(0, 8)), tp2.ap(None, slice(0, 8)),
                tp1.ap(None, slice(5, 6))),
                reads=[tp2.buf, tp1.buf], writes=[cmb.buf])
            s.op("pe", lambda e: e.transpose(
                P.psum[:, 1536:2048].bitcast(BF16)[0:8, 0:128],
                cmb.ap(None, slice(0, 8)), ident.ap()),
                reads=[cmb.buf, ident.buf], writes=[P.pbk[3]])
            s.op("act", lambda e, tt=tt: e.activation(
                combT.ap(slice(0, 8), slice(tt * 128, (tt + 1) * 128)),
                P.psum[:, 1536:2048].bitcast(BF16)[0:8, 0:128], AF.Copy),
                reads=[P.pbk[3]],
                writes=[combT.buf] if tt == 0 else [],
                pwrites=[combT.buf] if tt else [])
        for o in range(8):
            bank = o % 4
            s.op("pe", lambda e, o=o, bank=bank: e.matmul(
                P.pb(bank), b3sb.ap(slice(0, 8), slice(o * 128, (o + 1) * 128)),
                combT.ap(slice(0, 8)), start=True, stop=True),
                reads=[b3sb.buf, combT.buf], writes=[P.pbk[bank]])
            s.op("act", lambda e, o=o, bank=bank: e.activation(
                acc[o].ap(), P.pb(bank), AF.Copy),
                reads=[P.pbk[bank]], writes=[acc[o].buf])
        for ex in range(8):
            w1s, w2s = w1sb[ex % 2], w2sb[0]
            ldw(w1s, w1C, w1Cb, ex)
            ldw(w2s, w2C, w2Cb, ex)
            s.op("pe", lambda e, ex=ex: e.matmul(
                P.pb(4),
                sel8.ap(slice(0, 8), slice(ex * 128, (ex + 1) * 128)),
                combT.ap(slice(0, 8)), start=True, stop=True),
                reads=[sel8.buf, combT.buf], writes=[P.pbk[4]])
            s.op("dve", lambda e: e.tensor_copy(
                combB.ap(), P.pb(4)),
                reads=[P.pbk[4]], writes=[combB.buf])
            for ht in range(16):
                a = (ht % 2) * 2

                def umm(e, ht=ht, a=a, ws=w1s):
                    last = None
                    for c in range(8):
                        last = e.matmul(
                            P.pb(a),
                            ws.ap(None, slice(c * 2048 + ht * 128,
                                              c * 2048 + (ht + 1) * 128)),
                            r2[c].ap(), start=(c == 0), stop=(c == 7))
                    return last

                def umm2(e, ht=ht, a=a, ws=w2s):
                    last = None
                    for c in range(8):
                        last = e.matmul(
                            P.pb(a + 1),
                            ws.ap(None, slice(c * 2048 + ht * 128,
                                              c * 2048 + (ht + 1) * 128)),
                            r2[c].ap(), start=(c == 0), stop=(c == 7))
                    return last
                s.op("pe", umm, reads=[w1s.buf] + [t.buf for t in r2],
                     writes=[P.pbk[a]])
                s.op("pe", umm2, reads=[w2s.buf] + [t.buf for t in r2],
                     writes=[P.pbk[a + 1]])
                s.op("act", lambda e, ht=ht, a=a, ex=ex: e.activation(
                    usil.ap(), P.pb(a), AF.Silu,
                    bias=b1v.ap(None, slice(ex * 16 + ht, ex * 16 + ht + 1))),
                    reads=[P.pbk[a], b1v.buf], writes=[usil.buf])
                s.op("dve", lambda e, ht=ht, a=a, ex=ex: e.scalar_tensor_tensor(
                    hsb.ap(None, slice(ht * 512, (ht + 1) * 512)), P.pb(a + 1),
                    b2v.ap(None, slice(ex * 16 + ht, ex * 16 + ht + 1)),
                    usil.ap(), OP.add, OP.mult),
                    reads=[P.pbk[a + 1], b2v.buf, usil.buf],
                    writes=[hsb.buf] if ht == 0 else [],
                    pwrites=[hsb.buf] if ht else [])
            for ht in range(16):
                w3s = w3r[ht % 2]
                s.op("sp", lambda e, ht=ht, ex=ex, w3s=w3s: e.dma_start(
                    w3s.ap(), w3C[ex, ht * 128:(ht + 1) * 128, :]),
                    reads=[w3Cb], writes=[w3s.buf], dma=True)

                def w3mm(e, ht=ht, w3s=w3s):
                    last = None
                    for o in range(8):
                        last = e.matmul(
                            P.pb(o), w3s.ap(None, slice(o * 128, (o + 1) * 128)),
                            hsb.ap(None, slice(ht * 512, (ht + 1) * 512)),
                            start=(ht == 0), stop=(ht == 15))
                    return last
                s.op("pe", w3mm, reads=[w3s.buf, hsb.buf],
                     writes=[P.pbk[o2] for o2 in range(8)] if ht == 0 else [],
                     pwrites=[P.pbk[o2] for o2 in range(8)] if ht else [])
            for o in range(8):
                s.op("dve", lambda e, o=o: e.tensor_mul(
                    t1s.ap(), P.pb(o), combB.ap()),
                    reads=[P.pbk[o], combB.buf], writes=[t1s.buf])
                s.op("pool", lambda e, o=o: e.tensor_add(
                    acc[o].ap(), acc[o].ap(), t1s.ap()),
                    reads=[acc[o].buf, t1s.buf], pwrites=[acc[o].buf])
        for o in range(8):
            pi, x2 = xin[o % 2], xin[2 + o % 2]
            s.op("sp", lambda e, o=o, ts=ts, pi=pi: e.dma_start(
                pi.ap(), paD[o * 128:(o + 1) * 128, ts]),
                reads=[paB], writes=[pi.buf], dma=True)
            s.op("sp", lambda e, o=o, ts=ts, x2=x2: e.dma_start(
                x2.ap(), x1D[o * 128:(o + 1) * 128, ts]),
                reads=[x1B], writes=[x2.buf], dma=True)
            yos = yo[o % 2]
            s.op("dve", lambda e, o=o, x2=x2, yos=yos: e.tensor_add(
                yos.ap(), acc[o].ap(), x2.ap()),
                reads=[acc[o].buf, x2.buf], writes=[yos.buf])
            s.op("pool", lambda e, o=o, pi=pi, yos=yos: e.tensor_add(
                yos.ap(), yos.ap(), pi.ap()),
                reads=[yos.buf, pi.buf], pwrites=[yos.buf])
            s.op("sp", lambda e, o=o, ts=ts, yos=yos: e.dma_start(
                y0D[o * 128:(o + 1) * 128, ts], yos.ap()),
                reads=[yos.buf], pwrites=[y0B], dma=True)


def ph_ff1(P):
    s = P.s
    ynD, ynB = P.dram["ynTb"], P.dbuf["ynTb"]
    p1D, p1B = P.dram["p1T"], P.dbuf["p1T"]
    outD, outB = P.dram["outT"], P.dbuf["outT"]
    w1C, w1Cb = P.cache("ff_w1")
    w2C, w2Cb = P.cache("ff_w2")
    h1C, h1Cb = P.cache("ff_h1")
    h2C, h2Cb = P.cache("ff_h2")
    w3C, w3Cb = P.cache("ff_w3")
    s1 = P.af(32, "f_s1")
    s2 = P.af(32, "f_s2")
    hs1 = P.af(32, "f_hs1")
    hs2 = P.af(32, "f_hs2")
    s3 = P.af(16, "f_s3")
    load_weight_f32(P, "ff_s1", s1, cols=32)
    load_weight_f32(P, "ff_s2", s2, cols=32)
    load_weight_f32(P, "ff_hs1", hs1, cols=32)
    load_weight_f32(P, "ff_hs2", hs2, cols=32)
    load_weight_f32(P, "ff_s3", s3, cols=8)

    yn = [P.ab(1024, f"f_y{c}") for c in range(8)]
    hb = P.ab(32768, "f_h")
    gb = P.ab(32768, "f_g")
    wA = [P.ab(1024, f"f_wa{c}") for c in range(8)]
    wB = [P.ab(1024, f"f_wb{c}") for c in range(8)]
    wr = [P.ab(1024, f"f_wr{i}") for i in range(4)]
    usil = [P.af(512, f"f_u{i}") for i in range(4)]
    p1b = []
    for i in range(4):
        b = SBuf(P.ar.t, gb.c0 + i * 2048, 1024, F32, f"f_p{i}")
        b.buf = gb.buf
        p1b.append(b)
    ob = []
    for i in range(2):
        b = SBuf(P.ar.t, gb.c0 + (4 + i) * 2048, 1024, F32, f"f_o{i}")
        b.buf = gb.buf
        ob.append(b)
    TCW = 1024

    for ch in range(4):
        ts = slice(ch * TCW, (ch + 1) * TCW)
        for c in range(8):
            s.op("sp", lambda e, c=c, ts=ts: e.dma_start(
                yn[c].ap(), ynD[c * 128:(c + 1) * 128, ts]),
                reads=[ynB], writes=[yn[c].buf], dma=True)
        for ob4 in range(4):
            for c in range(8):
                s.op("sp", lambda e, c=c, ob4=ob4: e.dma_start(
                    wA[c].ap(), w1C[c * 128:(c + 1) * 128,
                                    ob4 * 1024:(ob4 + 1) * 1024]),
                    reads=[w1Cb], writes=[wA[c].buf], dma=True)
                s.op("pool", lambda e, c=c, ob4=ob4: e.dma_start(
                    wB[c].ap(), w2C[c * 128:(c + 1) * 128,
                                    ob4 * 1024:(ob4 + 1) * 1024]),
                    reads=[w2Cb], writes=[wB[c].buf], dma=True)
            for oi in range(8):
                ot = ob4 * 8 + oi
                for sub in range(2):
                    b1k = (oi * 2 + sub) % 4
                    b2k = 4 + b1k
                    us = usil[b1k]

                    def mku(e, oi=oi, sub=sub, b1k=b1k):
                        last = None
                        for c in range(8):
                            last = e.matmul(
                                P.pb(b1k),
                                wA[c].ap(None, slice(oi * 128, (oi + 1) * 128)),
                                yn[c].ap(None, slice(sub * 512, (sub + 1) * 512)),
                                start=(c == 0), stop=(c == 7))
                        return last

                    def mku2(e, oi=oi, sub=sub, b2k=b2k):
                        last = None
                        for c in range(8):
                            last = e.matmul(
                                P.pb(b2k),
                                wB[c].ap(None, slice(oi * 128, (oi + 1) * 128)),
                                yn[c].ap(None, slice(sub * 512, (sub + 1) * 512)),
                                start=(c == 0), stop=(c == 7))
                        return last
                    s.op("pe", mku, reads=[t.buf for t in wA] + [t.buf for t in yn],
                         writes=[P.pbk[b1k]])
                    s.op("pe", mku2, reads=[t.buf for t in wB] + [t.buf for t in yn],
                         writes=[P.pbk[b2k]])
                    s.op("act", lambda e, ot=ot, b1k=b1k, us=us: e.activation(
                        us.ap(), P.pb(b1k), AF.Silu,
                        scale=s1.ap(None, slice(ot, ot + 1))),
                        reads=[P.pbk[b1k], s1.buf], writes=[us.buf])
                    s.op("dve", lambda e, ot=ot, sub=sub, b2k=b2k, us=us:
                         e.scalar_tensor_tensor(
                             hb.ap(None, slice(ot * 1024 + sub * 512,
                                               ot * 1024 + (sub + 1) * 512)),
                             P.pb(b2k), s2.ap(None, slice(ot, ot + 1)),
                             us.ap(), OP.mult, OP.mult),
                         reads=[P.pbk[b2k], s2.buf, us.buf],
                         writes=[hb.buf] if ot == 0 and sub == 0 else [],
                         pwrites=[hb.buf] if (ot or sub) else [])
        for (wc, wcb, hs, src, dst) in ((h1C, h1Cb, hs1, hb, gb),
                                        (h2C, h2Cb, hs2, gb, hb)):
            for obk in range(8):
                for k2 in range(16):
                    wrs = wr[k2 % 4]
                    s.op("sp" if k2 % 2 == 0 else "pool",
                         lambda e, k2=k2, obk=obk, wrs=wrs, wc=wc: e.dma_start(
                             wrs.ap(None, slice(0, 1024)).rearrange(
                                 "p (k o) -> p k o", k=2),
                             wc[k2 * 256:(k2 + 1) * 256,
                                obk * 512:(obk + 1) * 512].rearrange(
                                 "(k p) o -> p k o", p=128)),
                         reads=[wcb], writes=[wrs.buf], dma=True)

                    def mkh(e, k2=k2, obk=obk, wrs=wrs, src=src):
                        last = None
                        for kk in range(2):
                            kt = k2 * 2 + kk
                            for sub in range(2):
                                for oi in range(4):
                                    last = e.matmul(
                                        P.pb(sub * 4 + oi),
                                        wrs.ap(None, slice(kk * 512 + oi * 128,
                                                           kk * 512 + (oi + 1) * 128)),
                                        src.ap(None, slice(kt * 1024 + sub * 512,
                                                           kt * 1024 + (sub + 1) * 512)),
                                        start=(kt == 0), stop=(kt == 31))
                        return last
                    s.op("pe", mkh, reads=[wrs.buf, src.buf],
                         writes=[P.pbk[i] for i in range(8)] if k2 == 0 else [],
                         pwrites=[P.pbk[i] for i in range(8)] if k2 else [])
                for sub in range(2):
                    for oi in range(4):
                        ot = obk * 4 + oi
                        s.op("act", lambda e, ot=ot, sub=sub, oi=oi, hs=hs, dst=dst:
                             e.activation(
                                 dst.ap(None, slice(ot * 1024 + sub * 512,
                                                    ot * 1024 + (sub + 1) * 512)),
                                 P.pb(sub * 4 + oi), AF.Gelu,
                                 scale=hs.ap(None, slice(ot, ot + 1))),
                             reads=[P.pbk[sub * 4 + oi], hs.buf],
                             writes=[dst.buf] if obk == 0 and ot == 0 and sub == 0 else [],
                             pwrites=[dst.buf] if (obk or ot or sub) else [])
        for obk in range(2):
            for oi in range(4):
                o = obk * 4 + oi
                pb1 = p1b[oi]
                s.op("sp", lambda e, o=o, ts=ts, pb1=pb1: e.dma_start(
                    pb1.ap(), p1D[o * 128:(o + 1) * 128, ts]),
                    reads=[p1B], writes=[pb1.buf], dma=True)
            for k2 in range(16):
                wrs = wr[k2 % 4]
                s.op("sp" if k2 % 2 == 0 else "pool",
                     lambda e, k2=k2, obk=obk, wrs=wrs: e.dma_start(
                         wrs.ap(None, slice(0, 1024)).rearrange(
                             "p (k o) -> p k o", k=2),
                         w3C[k2 * 256:(k2 + 1) * 256,
                             obk * 512:(obk + 1) * 512].rearrange(
                             "(k p) o -> p k o", p=128)),
                     reads=[w3Cb], writes=[wrs.buf], dma=True)

                def mk3(e, k2=k2, obk=obk, wrs=wrs):
                    last = None
                    for kk in range(2):
                        kt = k2 * 2 + kk
                        for sub in range(2):
                            for oi in range(4):
                                last = e.matmul(
                                    P.pb(sub * 4 + oi),
                                    wrs.ap(None, slice(kk * 512 + oi * 128,
                                                       kk * 512 + (oi + 1) * 128)),
                                    hb.ap(None, slice(kt * 1024 + sub * 512,
                                                      kt * 1024 + (sub + 1) * 512)),
                                    start=(kt == 0), stop=(kt == 31))
                    return last
                s.op("pe", mk3, reads=[wrs.buf, hb.buf],
                     writes=[P.pbk[i] for i in range(8)] if k2 == 0 else [],
                     pwrites=[P.pbk[i] for i in range(8)] if k2 else [])
            for oi in range(4):
                o = obk * 4 + oi
                pb1 = p1b[oi]
                obs = ob[oi % 2]
                for sub in range(2):
                    s.op("dve", lambda e, o=o, oi=oi, sub=sub, pb1=pb1, obs=obs:
                         e.scalar_tensor_tensor(
                             obs.ap(None, slice(sub * 512, (sub + 1) * 512)),
                             P.pb(sub * 4 + oi), s3.ap(None, slice(o, o + 1)),
                             pb1.ap(None, slice(sub * 512, (sub + 1) * 512)),
                             OP.mult, OP.add),
                         reads=[P.pbk[sub * 4 + oi], s3.buf, pb1.buf],
                         writes=[obs.buf] if sub == 0 else [],
                         pwrites=[obs.buf] if sub else [])
                s.op("sp", lambda e, o=o, ts=ts, obs=obs: e.dma_start(
                    outD[o * 128:(o + 1) * 128, ts], obs.ap()),
                    reads=[obs.buf], pwrites=[outB], dma=True)


PHASE_FNS = {
    "ln0": ph_ln0, "ln1": ph_ln1,
    "moe": ph_moe, "ff1": ph_ff1,
    "retp": lambda P: attn_prep(P, ACFG["ret"]),
    "reth": lambda P: attn_heads(P, ACFG["ret"]),
    "reto": lambda P: attn_out(P, ACFG["ret"]),
    "fap": lambda P: attn_prep(P, ACFG["fa"]),
    "fah": lambda P: attn_heads(P, ACFG["fa"]),
    "fao": lambda P: attn_out(P, ACFG["fa"]),
    "a1p": lambda P: attn_prep(P, ACFG["at1"]),
    "a1h": lambda P: attn_heads(P, ACFG["at1"]),
    "a1o": lambda P: attn_out(P, ACFG["at1"]),
}


def _run(inputs, phase_groups, taps=()):
    consts = make_consts()
    w = prep_weights(inputs)
    wshapes = {k: v.shape for k, v in w.items()}
    x = inputs["L0_attn_x"]
    xTs = [np.ascontiguousarray(np.asarray(x[i], dtype=np.float32).T) for i in range(8)]

    hbm = [{} for _ in range(8)]
    out = None
    taps_res = {}
    for gi, group in enumerate(phase_groups):
        later = set()
        for g2 in phase_groups[gi + 1:]:
            for p in g2:
                later |= set(PHASE_NEEDS[p])
        gtaps = [t for t in taps
                 if t in sum((PHASE_MAKES[p] for p in group), [])]
        P = Prog(group, wshapes, consts, later_needs=later, taps=gtaps)
        nc = P.build()
        in_maps = []
        for ci in range(8):
            m = {}
            for name in P.ext_in:
                if name == "xT":
                    m["xT"] = xTs[ci]
                elif name in w:
                    m["w_" + name] = w[name]
                else:
                    m[name] = hbm[ci][name]
            in_maps.append(m)
        res = run_bass_kernel_spmd(nc, in_maps, core_ids=list(range(8)))
        for ci in range(8):
            for name, arr in res.results[ci].items():
                if name.startswith("tap_"):
                    taps_res.setdefault(name[4:], [None] * 8)[ci] = arr
                elif name == "outT":
                    if out is None:
                        out = [None] * 8
                    out[ci] = arr
                else:
                    hbm[ci][name] = arr
    return out, taps_res, hbm


def kernel(**inputs):
    groups = [PHASES] if MODE == "fused" else [[p] for p in PHASES]
    out, _, _ = _run(inputs, groups)
    res = np.stack([o.T for o in out], axis=0).astype(np.float32)
    return np.ascontiguousarray(res)
